# version 12
# baseline (speedup 1.0000x reference)
import numpy as np
from ml_dtypes import bfloat16 as _np_bf16
import concourse.bass as bass
import concourse.mybir as mybir
from concourse.bass_utils import run_bass_kernel_spmd
from contextlib import ExitStack

F32 = mybir.dt.float32
BF16 = mybir.dt.bfloat16
AF = mybir.ActivationFunctionType
ALU = mybir.AluOpType
AX = mybir.AxisListType

T = 2048
D = 1024
NT = 16
H = 16
HD = 64
LHID = 512
HHID = 4096
NHEAVY = 1024
EPS_LN = 1e-5
C0 = float(np.float32(0.1) * np.float32(np.log(8.0)))
T32 = -87.3365478515625
T64 = -745.2
N_CD = 50
N_BIS = 52


class Sem:
    def __init__(self, handle):
        self.h = handle
        self.n = 0

    def inc(self, instr, k=1):
        instr.then_inc(self.h, k)
        self.n += k
        return self.n


def build_program(dbg=False):
    nc = bass.Bass("TRN2", target_bir_lowering=False, detect_race_conditions=False)

    x_d = nc.dram_tensor("x", [T, D], F32, kind="ExternalInput").ap()
    wqkv_d = nc.dram_tensor("wqkv", [D, 3 * D], F32, kind="ExternalInput").ap()
    lng_d = nc.dram_tensor("lng", [HD], F32, kind="ExternalInput").ap()
    lnb_d = nc.dram_tensor("lnb", [HD], F32, kind="ExternalInput").ap()
    wproj_d = nc.dram_tensor("wproj", [D, D], F32, kind="ExternalInput").ap()
    bproj_d = nc.dram_tensor("bproj", [D], F32, kind="ExternalInput").ap()
    rt_d = nc.dram_tensor("rt", [D], F32, kind="ExternalInput").ap()
    gl_d = nc.dram_tensor("gl", [D], F32, kind="ExternalInput").ap()
    w1l_d = nc.dram_tensor("w1l", [D, LHID], F32, kind="ExternalInput").ap()
    b1l_d = nc.dram_tensor("b1l", [LHID], F32, kind="ExternalInput").ap()
    w2l_d = nc.dram_tensor("w2l", [LHID, D], F32, kind="ExternalInput").ap()
    b2l_d = nc.dram_tensor("b2l", [D], F32, kind="ExternalInput").ap()
    gh_d = nc.dram_tensor("gh", [D], F32, kind="ExternalInput").ap()
    w1h_d = nc.dram_tensor("w1h", [D, HHID], F32, kind="ExternalInput").ap()
    b1h_d = nc.dram_tensor("b1h", [HHID], F32, kind="ExternalInput").ap()
    w2h_d = nc.dram_tensor("w2h", [HHID, D], F32, kind="ExternalInput").ap()
    b2h_d = nc.dram_tensor("b2h", [D], F32, kind="ExternalInput").ap()
    out_d = nc.dram_tensor("out", [T, D], F32, kind="ExternalOutput").ap()
    if dbg:
        dbg_rt = nc.dram_tensor("dbg_rt", [128, NT * 4 + 8], F32, kind="ExternalOutput").ap()
        dbg_traj = nc.dram_tensor("dbg_traj", [1, 128], F32, kind="ExternalOutput").ap()
        dbg_attn = nc.dram_tensor("dbg_attn", [T, D], F32, kind="ExternalOutput").ap()
        dbg_light = nc.dram_tensor("dbg_light", [T, D], F32, kind="ExternalOutput").ap()
        dbg_nfm = nc.dram_tensor("dbg_nfm", [128, 8 * T], BF16, kind="ExternalOutput").ap()
        dbg_qkT = nc.dram_tensor("dbg_qkT", [128, 2 * D], BF16, kind="ExternalOutput").ap()
        dbg_A = nc.dram_tensor("dbg_A", [128, H * 128], BF16, kind="ExternalOutput").ap()
        dbg_ao = nc.dram_tensor("dbg_ao", [128, D], BF16, kind="ExternalOutput").ap()
        dbg_qh13 = nc.dram_tensor("dbg_qh13", [128, 2 * D], BF16, kind="ExternalOutput").ap()
        dbg_qkT13 = nc.dram_tensor("dbg_qkT13", [128, 2 * D], BF16, kind="ExternalOutput").ap()
        dbg_A13 = nc.dram_tensor("dbg_A13", [128, H * 128], BF16, kind="ExternalOutput").ap()
        dbg_AT13 = nc.dram_tensor("dbg_AT13", [128, H * 128], BF16, kind="ExternalOutput").ap()
        dbg_ao13 = nc.dram_tensor("dbg_ao13", [128, D], BF16, kind="ExternalOutput").ap()
        dbg_v13 = nc.dram_tensor("dbg_v13", [128, D], BF16, kind="ExternalOutput").ap()
    ds_s = nc.dram_tensor("ds_s", [128], F32, kind="Internal").ap()
    ds_c = nc.dram_tensor("ds_c", [NT], F32, kind="Internal").ap()

    Lmat = np.triu(np.ones((128, 128), dtype=np.float32))
    U16 = np.triu(np.ones((NT, NT), dtype=np.float32), 1)
    L_d = nc.inline_tensor(Lmat, name="Lmat")
    I_d = nc.inline_tensor(np.eye(128).astype(_np_bf16), name="I128")
    U_d = nc.inline_tensor(U16, name="U16")

    ctx = ExitStack()
    e = ctx.enter_context

    outacc = e(nc.sbuf_tensor("outacc", [128, NT * D], F32))
    nfm = e(nc.sbuf_tensor("nfm", [128, 8 * T], BF16))
    s_t = e(nc.sbuf_tensor("s_t", [128, NT], F32))
    invr_t = e(nc.sbuf_tensor("invr_t", [128, NT], F32))
    mask_t = e(nc.sbuf_tensor("mask_t", [128, NT], F32))
    key_t = e(nc.sbuf_tensor("key_t", [128, NT], F32))
    gl_sb = e(nc.sbuf_tensor("gl_sb", [128, 8], F32))
    gh_sb = e(nc.sbuf_tensor("gh_sb", [128, 8], F32))
    b1l_sb = e(nc.sbuf_tensor("b1l_sb", [128, 4], F32))
    b1h_sb = e(nc.sbuf_tensor("b1h_sb", [128, 8], F32))
    rows_b = e(nc.sbuf_tensor("rows_b", [1, 3 * D], BF16))
    lnq_g = e(nc.sbuf_tensor("lnq_g", [128, 1], F32))
    lnq_b = e(nc.sbuf_tensor("lnq_b", [128, 1], F32))
    lnk_g = e(nc.sbuf_tensor("lnk_g", [128, 1], F32))
    lnk_b = e(nc.sbuf_tensor("lnk_b", [128, 1], F32))
    rt_rep = e(nc.sbuf_tensor("rt_rep", [128, D], F32))
    rt_row = e(nc.sbuf_tensor("rt_row", [1, D], F32))
    L_sb = e(nc.sbuf_tensor("L_sb", [128, 128], F32))
    U_sb = e(nc.sbuf_tensor("U_sb", [NT, NT], F32))
    ones128c = e(nc.sbuf_tensor("ones128c", [128, 1], F32))
    ones1r = e(nc.sbuf_tensor("ones1r", [1, 128], F32))
    ones1r_b = e(nc.sbuf_tensor("ones1r_b", [1, 128], BF16))
    smax_b = e(nc.sbuf_tensor("smax_b", [128, 1], F32))
    Ab = e(nc.sbuf_tensor("Ab", [128, 1], F32))
    mmv = e(nc.sbuf_tensor("mmv", [128, 1], F32))
    negmm10 = e(nc.sbuf_tensor("negmm10", [128, 1], F32))
    mmc = e(nc.sbuf_tensor("mmc", [1, 1], F32))
    lnv = e(nc.sbuf_tensor("lnv", [1, 1], F32))
    Asc = e(nc.sbuf_tensor("Asc", [1, 1], F32))
    zt = e(nc.sbuf_tensor("zt", [128, NT], F32))
    rowsum = e(nc.sbuf_tensor("rowsum", [128, 1], F32))
    tr128 = e(nc.sbuf_tensor("tr128", [1, 128], F32))
    tr16b = e(nc.sbuf_tensor("tr16b", [NT, 1], F32))
    tr1 = e(nc.sbuf_tensor("tr1", [1, NT], F32))
    sm1 = e(nc.sbuf_tensor("sm1", [1, 1], F32))
    lo_t = e(nc.sbuf_tensor("lo_t", [1, 1], F32))
    hi_t = e(nc.sbuf_tensor("hi_t", [1, 1], F32))
    mid_t = e(nc.sbuf_tensor("mid_t", [1, 1], F32))
    midb = e(nc.sbuf_tensor("midb", [128, 1], F32))
    gam_t = e(nc.sbuf_tensor("gam_t", [1, 1], F32))
    tmp1 = e(nc.sbuf_tensor("tmp1", [1, 1], F32))
    tmp2 = e(nc.sbuf_tensor("tmp2", [1, 1], F32))
    tmp3 = e(nc.sbuf_tensor("tmp3", [1, 1], F32))
    tadp = e(nc.sbuf_tensor("tadp", [1, 1], F32))
    tadb = e(nc.sbuf_tensor("tadb", [128, 1], F32))
    lob = e(nc.sbuf_tensor("lob", [128, 1], F32))
    hib = e(nc.sbuf_tensor("hib", [128, 1], F32))
    nhi_t = e(nc.sbuf_tensor("nhi_t", [1, 1], F32))
    fcb = e(nc.sbuf_tensor("fcb", [128, 1], F32))
    cmpt = e(nc.sbuf_tensor("cmpt", [128, NT], F32))
    eqt = e(nc.sbuf_tensor("eqt", [128, NT], F32))
    colpre_sb = e(nc.sbuf_tensor("colpre_sb", [1, NT], F32))
    traj = e(nc.sbuf_tensor("traj", [1, 128], F32))
    stat_a = e(nc.sbuf_tensor("stat_a", [128, H], F32))
    stat_b = e(nc.sbuf_tensor("stat_b", [128, H], F32))
    stat_c = e(nc.sbuf_tensor("stat_c", [128, H], F32))
    stat_d = e(nc.sbuf_tensor("stat_d", [128, H], F32))
    stat_e = e(nc.sbuf_tensor("stat_e", [128, H], F32))
    stat_f = e(nc.sbuf_tensor("stat_f", [128, H], F32))
    stat_h = e(nc.sbuf_tensor("stat_h", [128, H], F32))
    col_a = e(nc.sbuf_tensor("col_a", [128, 1], F32))
    col_b = e(nc.sbuf_tensor("col_b", [128, 1], F32))
    col_c = e(nc.sbuf_tensor("col_c", [128, 1], F32))
    col_d = e(nc.sbuf_tensor("col_d", [128, 1], F32))
    col_e = e(nc.sbuf_tensor("col_e", [128, 1], F32))
    col_f = e(nc.sbuf_tensor("col_f", [128, 1], F32))
    col_g = e(nc.sbuf_tensor("col_g", [128, 1], F32))
    ca4 = e(nc.sbuf_tensor("ca4", [128, 4], F32))
    cb4 = e(nc.sbuf_tensor("cb4", [128, 4], F32))
    cc4 = e(nc.sbuf_tensor("cc4", [128, 4], F32))
    cd4 = e(nc.sbuf_tensor("cd4", [128, 4], F32))
    ce4 = e(nc.sbuf_tensor("ce4", [128, 4], F32))
    cf4 = e(nc.sbuf_tensor("cf4", [128, 4], F32))
    cg4 = e(nc.sbuf_tensor("cg4", [128, 4], F32))
    ch4 = e(nc.sbuf_tensor("ch4", [128, 4], F32))
    rsum_c = e(nc.sbuf_tensor("rsum_c", [128, 1], F32))
    nrsum_c = e(nc.sbuf_tensor("nrsum_c", [128, 1], F32))
    I_sb = e(nc.sbuf_tensor("I_sb", [128, 128], BF16))
    cbias = e(nc.sbuf_tensor("cbias", [128, 2], F32))
    nc.const_aps.register(F32, EPS_LN, cbias[:, 0:1])
    nc.const_aps.register(F32, -8.0, cbias[:, 1:2])
    xn_sc = e(nc.sbuf_tensor("xn_sc", [128, D], F32))
    qk_f32 = e(nc.sbuf_tensor("qk_f32", [128, D], F32))
    qkhat = e(nc.sbuf_tensor("qkhat", [128, 2 * D], BF16))
    ntm_sc = qkhat[:, 0:D]
    qkT = e(nc.sbuf_tensor("qkT", [128, 2 * D], BF16))
    v_bf = e(nc.sbuf_tensor("v_bf", [128, D], BF16))
    A_bf = e(nc.sbuf_tensor("A_bf", [128, H * 128], BF16))
    AT_bf = e(nc.sbuf_tensor("AT_bf", [128, H * 128], BF16))
    ao_fm = e(nc.sbuf_tensor("ao_fm", [128, D], BF16))
    sums16 = e(nc.sbuf_tensor("sums16", [128, H], F32))
    invs16 = e(nc.sbuf_tensor("invs16", [128, H], F32))
    h_sc = e(nc.sbuf_tensor("h_sc", [128, 256], BF16))

    pq = e(nc.psum_tensor("pq", [128, 512], F32))
    pq2 = e(nc.psum_tensor("pq2", [128, 512], F32))
    pSA = e(nc.psum_tensor("pSA", [128, 512], F32))
    po = [e(nc.psum_tensor(f"po{i}", [128, 512], F32)) for i in range(4)]
    pmisc = e(nc.psum_tensor("pmisc", [128, 512], F32))
    pS = pSA[:, 0:256]
    pAV = pSA[:, 256:512]
    pr = pmisc[:, 48:48 + NT]
    pb = pmisc[:, 0:1]
    pb2 = pmisc[:, 1:2]
    p1 = pmisc[0:1, 16:32]
    p2 = pmisc[0:1, 32:48]

    sW = Sem(e(nc.semaphore("sW")))
    sX = Sem(e(nc.semaphore("sX")))
    sT = Sem(e(nc.semaphore("sT")))
    sSt = Sem(e(nc.semaphore("sSt")))
    sV = Sem(e(nc.semaphore("sV")))
    sS = Sem(e(nc.semaphore("sS")))
    sM = Sem(e(nc.semaphore("sM")))
    sG = Sem(e(nc.semaphore("sG")))
    sWB = Sem(e(nc.semaphore("sWB")))
    sWH = Sem(e(nc.semaphore("sWH")))

    def VF(v, ins):
        sV.inc(ins)
        v.wait_ge(sV.h, sV.n)

    blk = e(nc.Block())

    @blk.sync
    def _(sy):
        for dma in (
            lambda: sy.dma_start(out=L_sb[:], in_=L_d.ap()),
            lambda: sy.dma_start(out=U_sb[:], in_=U_d.ap()),
            lambda: sy.dma_start(out=rt_row[:], in_=rt_d[None, :]),
            lambda: sy.dma_start(out=col_e[0:64, :], in_=lng_d[:, None]),
            lambda: sy.dma_start(out=col_e[64:128, :], in_=lng_d[:, None]),
            lambda: sy.dma_start(out=col_f[0:64, :], in_=lnb_d[:, None]),
            lambda: sy.dma_start(out=col_f[64:128, :], in_=lnb_d[:, None]),
            lambda: sy.dma_start(out=gl_sb[:], in_=gl_d.rearrange("(c p) -> p c", p=128), allow_slow_non_contiguous=True),
            lambda: sy.dma_start(out=gh_sb[:], in_=gh_d.rearrange("(c p) -> p c", p=128), allow_slow_non_contiguous=True),
            lambda: sy.dma_start(out=b1l_sb[:], in_=b1l_d.rearrange("(c p) -> p c", p=128), allow_slow_non_contiguous=True),
        ):
            sX.inc(dma(), 16)
        X_CONST_ = sX.n
        for j in range(NT):
            sX.inc(sy.dma_start(out=outacc[:, j * D:(j + 1) * D],
                                in_=x_d[j * 128:(j + 1) * 128, :]), 16)
    X_CONST = 16 * 10
    X_ALL = X_CONST + 16 * NT

    @blk.gpsimd
    def _(g):
        sWB.inc(g.dma_start(out=rows_b[:, 0:D], in_=bproj_d[None, :]), 16)
        sWB.inc(g.dma_start(out=rows_b[:, D:2 * D], in_=b2l_d[None, :]), 16)
        sWB.inc(g.dma_start(out=rows_b[:, 2 * D:3 * D], in_=b2h_d[None, :]), 16)
        sG.inc(g.memset(cbias[:, 0:1], EPS_LN))
        sG.inc(g.memset(cbias[:, 1:2], -8.0))
        sG.inc(g.memset(ones128c[:], 1.0))
        sG.inc(g.memset(ones1r[:], 1.0))
        sG.inc(g.memset(ones1r_b[:], 1.0))
        sG.inc(g.memset(Ab[:], 0.0))
        sG.inc(g.memzero(A_bf[:]))
        sG.inc(g.dma_start(out=I_sb[:], in_=I_d.ap()), 16)
    W_SMALL = 16 * 3
    G_CONST = 7 + 16

    wctx = ExitStack()
    we = wctx.enter_context
    wqkv_sb = we(nc.sbuf_tensor("wqkv_sb", [128, 8 * 3 * D], BF16))
    wproj_sb = we(nc.sbuf_tensor("wproj_sb", [128, 8 * D], BF16))

    @blk.gpsimd
    def _(g):
        for kc in range(8):
            sW.inc(g.dma_start(out=wqkv_sb[:, kc * 3 * D:(kc + 1) * 3 * D],
                               in_=wqkv_d[kc * 128:(kc + 1) * 128, :]), 16)
        for kc in range(8):
            sW.inc(g.dma_start(out=wproj_sb[:, kc * D:(kc + 1) * D],
                               in_=wproj_d[kc * 128:(kc + 1) * 128, :]), 16)
    W_ATTN = sW.n

    @blk.vector
    def _(v):
        v.wait_ge(sX.h, X_ALL)
        v.wait_ge(sWB.h, W_SMALL)
        v.tensor_scalar(out=lnq_g[:], in0=col_e[:], scalar1=0.125, scalar2=None, op0=ALU.mult)
        v.tensor_scalar(out=lnq_b[:], in0=col_f[:], scalar1=0.125, scalar2=None, op0=ALU.mult)
        v.tensor_copy(out=lnk_g[:], in_=col_e[:])
        sV.inc(v.tensor_copy(out=lnk_b[:], in_=col_f[:]))

    @blk.tensor
    def _(t):
        t.wait_ge(sG.h, G_CONST)
        t.wait_ge(sX.h, X_ALL)
        t.matmul(pq[:], ones1r[:], rt_row[:, 0:512], start=True, stop=True)
        sM.inc(t.matmul(pq2[:], ones1r[:], rt_row[:, 512:1024], start=True, stop=True))

    @blk.vector
    def _(v):
        v.wait_ge(sM.h, sM.n)
        v.tensor_copy(out=rt_rep[:, 0:512], in_=pq[:])
        sV.inc(v.tensor_copy(out=rt_rep[:, 512:1024], in_=pq2[:]))
    V_RT = sV.n

    ln_c = 1.0 / D

    @blk.vector
    def _(v):
        v.wait_ge(sX.h, X_ALL)
        v.wait_ge(sV.h, V_RT)
        v.wait_ge(sG.h, G_CONST)
        VF(v, v.reduce_sum(out=rsum_c[:], in_=rt_rep[:], axis=AX.X))
        sV.inc(v.tensor_scalar(out=nrsum_c[:], in0=rsum_c[:], scalar1=-1.0, scalar2=None,
                               op0=ALU.mult))

    a_ev = {}
    a_tp = {}
    tp_banks = [(pq, pq2), (pSA, po[0])]
    for gi in range(4):
        j0 = gi * 4

        @blk.vector
        def _(v, j0=j0):
            for jj in range(4):
                base = (j0 + jj) * D
                xa = outacc[:, base:base + D]
                v.tensor_mul(out=qk_f32[:, 0:512], in0=outacc[:, base:base + 512],
                             in1=rt_rep[:, 0:512])
                v.reduce_sum(out=ca4[:, jj:jj + 1], in_=xa, axis=AX.X)
                v.tensor_mul(out=qk_f32[:, 512:1024], in0=outacc[:, base + 512:base + D],
                             in1=rt_rep[:, 512:1024])
                last = v.reduce_sum(out=ch4[:, jj:jj + 1], in_=qk_f32[:], axis=AX.X)
            VF(v, last)
            sV.inc(v.engine_nop())

        @blk.scalar
        def _(sc, j0=j0):
            sc.wait_ge(sV.h, sV.n)
            for jj in range(4):
                base = (j0 + jj) * D
                last = sc.activation(out=xn_sc[:], in_=outacc[:, base:base + D],
                                     func=AF.Square, accum_out=cb4[:, jj:jj + 1])
            sS.inc(last)

        @blk.vector
        def _(v):
            v.wait_ge(sS.h, sS.n)
            VF(v, v.tensor_scalar(out=cc4[:], in0=ca4[:], scalar1=ln_c, scalar2=None,
                                  op0=ALU.mult))
            VF(v, v.tensor_mul(out=cd4[:], in0=cc4[:], in1=cc4[:]))
            sV.inc(v.scalar_tensor_tensor(out=cb4[:], in0=cb4[:], scalar=ln_c,
                                          in1=cd4[:], op0=ALU.mult, op1=ALU.subtract))

        @blk.scalar
        def _(sc):
            sc.wait_ge(sV.h, sV.n)
            sS.inc(sc.activation(out=cd4[:], in_=cb4[:], func=AF.Sqrt, bias=EPS_LN))

        @blk.vector
        def _(v):
            v.wait_ge(sS.h, sS.n)
            VF(v, v.reciprocal(out=ce4[:], in_=cd4[:]))
            VF(v, v.tensor_mul(out=cf4[:], in0=cb4[:], in1=ce4[:]))
            sV.inc(v.tensor_mul(out=cf4[:], in0=cf4[:], in1=ce4[:]))

        @blk.scalar
        def _(sc):
            sc.wait_ge(sV.h, sV.n)
            sS.inc(sc.activation(out=cf4[:], in_=cf4[:], func=AF.Sqrt))

        @blk.vector
        def _(v, j0=j0):
            v.wait_ge(sS.h, sS.n)
            VF(v, v.tensor_scalar_max(cf4[:], cf4[:], 1e-12 / 32.0))
            v.tensor_copy(out=invr_t[:, j0:j0 + 4], in_=cf4[:])
            VF(v, v.reciprocal(out=cg4[:], in_=cf4[:]))
            VF(v, v.tensor_mul(out=cg4[:], in0=cg4[:], in1=ce4[:]))
            VF(v, v.scalar_tensor_tensor(out=cd4[:], in0=cc4[:], scalar=nrsum_c[:],
                                         in1=ch4[:], op0=ALU.mult, op1=ALU.add))
            sV.inc(v.tensor_mul(out=s_t[:, j0:j0 + 4], in0=cd4[:], in1=ce4[:]))

        for jj in range(4):
            j = j0 + jj

            @blk.vector
            def _(v, j=j, jj=jj):
                if j >= 2 and (j - 2) in a_tp:
                    v.wait_ge(sM.h, a_tp[j - 2])
                base = j * D
                sV.inc(v.tensor_scalar(out=qkhat[:, (j % 2) * D:(j % 2) * D + D],
                                       in0=outacc[:, base:base + D],
                                       scalar1=cc4[:, jj:jj + 1], scalar2=cg4[:, jj:jj + 1],
                                       op0=ALU.subtract, op1=ALU.mult))

            @blk.tensor
            def _(t, j=j):
                t.wait_ge(sV.h, sV.n)
                if j >= 2 and (j - 2) in a_ev:
                    t.wait_ge(sS.h, a_ev[j - 2])
                b0, b1 = tp_banks[j % 2]
                for kc in range(8):
                    bank = b0 if kc < 4 else b1
                    qtr = (kc % 4) * 128
                    mm = t.matmul(bank[:, qtr:qtr + 128],
                                  qkhat[(0):(128), (j % 2) * D + kc * 128:(j % 2) * D + kc * 128 + 128],
                                  I_sb[:], start=True, stop=True)
                sM.inc(mm)
            a_tp[j] = sM.n

            @blk.scalar
            def _(sc, j=j):
                sc.wait_ge(sM.h, a_tp[j])
                b0, b1 = tp_banks[j % 2]
                for kc in range(8):
                    bank = b0 if kc < 4 else b1
                    qtr = (kc % 4) * 128
                    last = sc.activation(out=nfm[:, kc * T + j * 128:kc * T + j * 128 + 128],
                                         in_=bank[:, qtr:qtr + 128], func=AF.Copy)
                sS.inc(last)
            a_ev[j] = sS.n
    NFM_S = sS.n
    T_NFM = sT.n
    V_PHA = sV.n

    RQ = []

    def r_prep():
        @blk.vector
        def _(v):
            v.wait_ge(sV.h, V_PHA)
            sV.inc(v.reduce_max(out=col_a[:], in_=s_t[:], axis=AX.X))

        @blk.sync
        def _(sy):
            sy.wait_ge(sV.h, sV.n)
            sX.inc(sy.dma_start(out=ds_s[:, None], in_=col_a[:]), 16)
            sy.wait_ge(sX.h, sX.n)
            sX.inc(sy.dma_start(out=tr128[:], in_=ds_s[None, :]), 16)

        @blk.vector
        def _(v):
            v.wait_ge(sX.h, sX.n)
            VF(v, v.reduce_max(out=sm1[:], in_=tr128[:], axis=AX.X))
            VF(v, v.memset(tmp1[:], 1.0))
            VF(v, v.tensor_scalar(out=tmp2[:], in0=tmp1[:], scalar1=float(2.0 ** -30),
                                  scalar2=None, op0=ALU.add))
            VF(v, v.tensor_sub(out=tmp2[:], in0=tmp2[:], in1=tmp1[:]))
            VF(v, v.tensor_scalar(out=gam_t[:], in0=tmp2[:], scalar1=1e-15, scalar2=None,
                                  op0=ALU.is_gt))
            VF(v, v.memset(tmp1[:], T32))
            VF(v, v.scalar_tensor_tensor(out=tadp[:], in0=gam_t[:], scalar=float(T64 - T32),
                                         in1=tmp1[:], op0=ALU.mult, op1=ALU.add))

        @blk.tensor
        def _(t):
            t.wait_ge(sV.h, sV.n)
            t.matmul(pb[:], ones1r[:], sm1[:], start=True, stop=True)
            sM.inc(t.matmul(pb2[:], ones1r[:], tadp[:], start=True, stop=True))

        @blk.vector
        def _(v):
            v.wait_ge(sM.h, sM.n)
            v.wait_ge(sG.h, G_CONST)
            VF(v, v.tensor_copy(out=smax_b[:], in_=pb[:]))
            VF(v, v.tensor_copy(out=tadb[:], in_=pb2[:]))
    RQ.append(r_prep)

    def r_cd_u1(it):
        def f():
            @blk.vector
            def _(v):
                if it > 0:
                    v.wait_ge(sM.h, sM.n)
                    VF(v, v.tensor_copy(out=Ab[:], in_=pb[:]))
                VF(v, v.tensor_tensor(out=mmv[:], in0=smax_b[:], in1=Ab[:], op=ALU.min))
                v.tensor_scalar(out=negmm10[:], in0=mmv[:], scalar1=-10.0, scalar2=None,
                                op0=ALU.mult)
                VF(v, v.tensor_scalar(out=mmc[:], in0=mmv[0:1, :], scalar1=C0, scalar2=None,
                                      op0=ALU.subtract))
                sV.inc(v.tensor_scalar(out=zt[:], in0=s_t[:], scalar1=Ab[:], scalar2=None,
                                       op0=ALU.min))

            @blk.scalar
            def _(ss):
                ss.wait_ge(sV.h, sV.n)
                sS.inc(ss.activation(out=cmpt[:], in_=zt[:], func=AF.Exp, bias=negmm10[:],
                                     scale=10.0, accum_out=rowsum[:]))
        return f

    def r_cd_u2():
        @blk.tensor
        def _(t):
            t.wait_ge(sS.h, sS.n)
            sM.inc(t.matmul(p1[0:1, 0:1], rowsum[:], ones128c[:], start=True, stop=True))

        @blk.scalar
        def _(ss):
            ss.wait_ge(sM.h, sM.n)
            sS.inc(ss.activation(out=lnv[:], in_=p1[0:1, 0:1], func=AF.Ln))

        @blk.vector
        def _(v):
            v.wait_ge(sS.h, sS.n)
            VF(v, v.scalar_tensor_tensor(out=Asc[:], in0=lnv[:], scalar=0.1,
                                         in1=mmc[:], op0=ALU.mult, op1=ALU.add))

    def r_cd_u3():
        @blk.tensor
        def _(t):
            t.wait_ge(sV.h, sV.n)
            sM.inc(t.matmul(pb[:], ones1r[:], Asc[:], start=True, stop=True))

    for it in range(N_CD):
        RQ.append(r_cd_u1(it))
        RQ.append(r_cd_u2)
        RQ.append(r_cd_u3)

    def r_key():
        @blk.vector
        def _(v):
            v.wait_ge(sM.h, sM.n)
            VF(v, v.tensor_copy(out=Ab[:], in_=pb[:]))
            VF(v, v.memset(hi_t[:], 1.0))
            VF(v, v.memset(mid_t[:], 0.0))
            VF(v, v.tensor_scalar(out=zt[:], in0=s_t[:], scalar1=Ab[:], scalar2=0.0,
                                  op0=ALU.subtract, op1=ALU.min))
            VF(v, v.tensor_scalar(out=key_t[:], in0=zt[:], scalar1=10.0, scalar2=None,
                                  op0=ALU.mult))
            VF(v, v.tensor_scalar(out=key_t[:], in0=key_t[:], scalar1=tadb[:], scalar2=None,
                                  op0=ALU.max))
            VF(v, v.tensor_scalar(out=lo_t[:], in0=tadp[:], scalar1=1.0, scalar2=None,
                                  op0=ALU.subtract))
            VF(v, v.tensor_add(out=mid_t[:], in0=lo_t[:], in1=hi_t[:]))
            VF(v, v.tensor_scalar(out=mid_t[:], in0=mid_t[:], scalar1=0.5, scalar2=None,
                                  op0=ALU.mult))
    RQ.append(r_key)

    def r_bis_u1():
        @blk.tensor
        def _(t):
            t.wait_ge(sV.h, sV.n)
            sM.inc(t.matmul(pb[:], ones1r[:], mid_t[:], start=True, stop=True))

        @blk.vector
        def _(v):
            v.wait_ge(sM.h, sM.n)
            VF(v, v.tensor_copy(out=midb[:], in_=pb[:]))
            VF(v, v.tensor_scalar(out=cmpt[:], in0=key_t[:], scalar1=midb[:], scalar2=None,
                                  op0=ALU.is_gt))
            sV.inc(v.reduce_sum(out=rowsum[:], in_=cmpt[:], axis=AX.X))

    def r_bis_u2():
        @blk.tensor
        def _(t):
            t.wait_ge(sV.h, sV.n)
            sM.inc(t.matmul(p1[0:1, 0:1], rowsum[:], ones128c[:], start=True, stop=True))

        @blk.vector
        def _(v):
            v.wait_ge(sM.h, sM.n)
            VF(v, v.tensor_scalar(out=gam_t[:], in0=p1[0:1, 0:1], scalar1=float(NHEAVY),
                                  scalar2=None, op0=ALU.is_ge))
            VF(v, v.tensor_sub(out=tmp1[:], in0=mid_t[:], in1=lo_t[:]))
            VF(v, v.tensor_mul(out=tmp1[:], in0=tmp1[:], in1=gam_t[:]))
            VF(v, v.tensor_add(out=lo_t[:], in0=lo_t[:], in1=tmp1[:]))
            VF(v, v.tensor_sub(out=tmp2[:], in0=hi_t[:], in1=mid_t[:]))
            VF(v, v.tensor_scalar(out=tmp3[:], in0=gam_t[:], scalar1=-1.0, scalar2=1.0,
                                  op0=ALU.mult, op1=ALU.add))
            VF(v, v.tensor_mul(out=tmp2[:], in0=tmp2[:], in1=tmp3[:]))
            VF(v, v.tensor_sub(out=hi_t[:], in0=hi_t[:], in1=tmp2[:]))
            VF(v, v.tensor_add(out=mid_t[:], in0=lo_t[:], in1=hi_t[:]))
            VF(v, v.tensor_scalar(out=mid_t[:], in0=mid_t[:], scalar1=0.5, scalar2=None,
                                  op0=ALU.mult))

    for it in range(N_BIS):
        RQ.append(r_bis_u1)
        RQ.append(r_bis_u2)

    def r_mask1():
        @blk.tensor
        def _(t):
            t.wait_ge(sV.h, sV.n)
            t.matmul(pb[:], ones1r[:], lo_t[:], start=True, stop=True)
            sM.inc(t.matmul(pb2[:], ones1r[:], hi_t[:], start=True, stop=True))

        @blk.vector
        def _(v):
            v.wait_ge(sM.h, sM.n)
            VF(v, v.tensor_copy(out=lob[:], in_=pb[:]))
            VF(v, v.tensor_copy(out=hib[:], in_=pb2[:]))
            VF(v, v.tensor_scalar(out=cmpt[:], in0=key_t[:], scalar1=hib[:], scalar2=None,
                                  op0=ALU.is_gt))
            VF(v, v.reduce_sum(out=rowsum[:], in_=cmpt[:], axis=AX.X))
            VF(v, v.tensor_scalar(out=eqt[:], in0=key_t[:], scalar1=lob[:], scalar2=None,
                                  op0=ALU.is_gt))
            VF(v, v.tensor_scalar(out=zt[:], in0=key_t[:], scalar1=hib[:], scalar2=None,
                                  op0=ALU.is_le))
            sV.inc(v.tensor_mul(out=eqt[:], in0=eqt[:], in1=zt[:]))
    RQ.append(r_mask1)

    def r_mask2():
        @blk.tensor
        def _(t):
            t.wait_ge(sV.h, sV.n)
            t.matmul(p1[0:1, 0:1], rowsum[:], ones128c[:], start=True, stop=True)
            sM.inc(t.matmul(p2, ones128c[:], eqt[:], start=True, stop=True))

        @blk.vector
        def _(v):
            v.wait_ge(sM.h, sM.n)
            VF(v, v.tensor_copy(out=nhi_t[:], in_=p1[0:1, 0:1]))
            VF(v, v.tensor_copy(out=tr1[:], in_=p2))

        @blk.sync
        def _(sy):
            sy.wait_ge(sV.h, sV.n)
            sX.inc(sy.dma_start(out=ds_c[None, :], in_=tr1[0:1, :]), 16)
            sy.wait_ge(sX.h, sX.n)
            sX.inc(sy.dma_start(out=tr16b[:], in_=ds_c[:, None]), 16)
    RQ.append(r_mask2)

    def r_mask3():
        @blk.tensor
        def _(t):
            t.wait_ge(sX.h, sX.n)
            sM.inc(t.matmul(p1[0:1, 0:NT], tr16b[:], U_sb[:], start=True, stop=True))

        @blk.vector
        def _(v):
            v.wait_ge(sM.h, sM.n)
            VF(v, v.tensor_copy(out=colpre_sb[:], in_=p1[0:1, 0:NT]))
            VF(v, v.tensor_scalar(out=nhi_t[:], in0=nhi_t[:], scalar1=-1.0,
                                  scalar2=float(NHEAVY), op0=ALU.mult, op1=ALU.add))

        @blk.tensor
        def _(t):
            t.wait_ge(sV.h, sV.n)
            t.matmul(pr[:], L_sb[:], eqt[:], start=True, stop=False)
            t.matmul(pr[:], ones1r[:], colpre_sb[:], start=False, stop=True)
            sM.inc(t.matmul(pb[:], ones1r[:], nhi_t[:], start=True, stop=True))

        @blk.vector
        def _(v):
            v.wait_ge(sM.h, sM.n)
            VF(v, v.tensor_copy(out=fcb[:], in_=pb[:]))
            VF(v, v.tensor_scalar(out=zt[:], in0=pr[:], scalar1=fcb[:], scalar2=None,
                                  op0=ALU.is_le))
            VF(v, v.tensor_mul(out=zt[:], in0=zt[:], in1=eqt[:]))
            sV.inc(v.tensor_add(out=mask_t[:], in0=cmpt[:], in1=zt[:]))
    RQ.append(r_mask3)

    def drain_routing(n):
        import os as _os2
        if _os2.environ.get("KDRAIN_MODE", "ileave") == "skip":
            for _ in range(n):
                if RQ:
                    RQ.pop(0)
            return
        for _ in range(n):
            if RQ:
                RQ.pop(0)()

    import os as _os
    _KM = _os.environ.get("KDRAIN_MODE", "ileave")
    if _KM == "all":
        drain_routing(len(RQ))
    elif _KM.isdigit():
        drain_routing(int(_KM))

    def _dr(kind, n):
        if _KM == "skip":
            for _ in range(n):
                if RQ:
                    RQ.pop(0)
            return
        if _KM == "ileave":
            if kind != "head":
                drain_routing(n)
        elif _KM == kind:
            drain_routing(n)


    fb = [pq, pq2, pSA]
    bb = [po[0], po[1], po[2], po[3]]
    PM = {}

    def f_w1(j):
        @blk.tensor
        def _(t, j=j):
            if j == 0:
                t.wait_ge(sS.h, NFM_S)
                t.wait_ge(sW.h, W_ATTN)
                t.wait_ge(sWB.h, W_SMALL)
                t.wait_ge(sV.h, V_RT)
            else:
                t.wait_ge(sV.h, PM[('gbB', j - 1)])
            for og in range(3):
                for kc in range(8):
                    mm = t.matmul(fb[og][:],
                                  nfm[:, kc * T + j * 128:kc * T + j * 128 + 128],
                                  wqkv_sb[:, kc * 3 * D + og * 512:kc * 3 * D + og * 512 + 512],
                                  start=(kc == 0), stop=(kc == 7))
            sM.inc(mm)
        PM[('w1', j)] = sM.n

    def f_ev1(j):
        @blk.vector
        def _(v, j=j):
            v.wait_ge(sM.h, PM[('w1', j)])
            iv = invr_t[:, j:j + 1]
            v.tensor_scalar(out=qk_f32[:, 0:512], in0=pq[:], scalar1=iv, scalar2=None, op0=ALU.mult)
            v.tensor_scalar(out=qk_f32[:, 512:1024], in0=pq2[:], scalar1=iv, scalar2=None, op0=ALU.mult)
            v.reduce_sum(out=stat_a[:], in_=qk_f32.rearrange("p (s k) -> p s k", k=HD), axis=AX.X)
            sV.inc(v.tensor_scalar(out=xn_sc[:, 0:512], in0=pSA[:], scalar1=iv, scalar2=None,
                                   op0=ALU.mult))
        PM[('ev1', j)] = sV.n

    def f_w2(j):
        @blk.tensor
        def _(t, j=j):
            t.wait_ge(sV.h, PM[('ev1', j)])
            for og in range(3, 6):
                for kc in range(8):
                    mm = t.matmul(fb[og - 3][:],
                                  nfm[:, kc * T + j * 128:kc * T + j * 128 + 128],
                                  wqkv_sb[:, kc * 3 * D + og * 512:kc * 3 * D + og * 512 + 512],
                                  start=(kc == 0), stop=(kc == 7))
            sM.inc(mm)
        PM[('w2', j)] = sM.n

    def f_ev2a(j):
        @blk.vector
        def _(v, j=j):
            v.wait_ge(sM.h, PM[('w2', j)])
            iv = invr_t[:, j:j + 1]
            v.tensor_scalar(out=xn_sc[:, 512:1024], in0=pq[:], scalar1=iv, scalar2=None,
                            op0=ALU.mult)
            sV.inc(v.reduce_sum(out=stat_e[:], in_=xn_sc.rearrange("p (s k) -> p s k", k=HD),
                                axis=AX.X))
        PM[('ev2a', j)] = sV.n

    def f_stats(j):
        @blk.scalar
        def _(sc, j=j):
            sc.wait_ge(sV.h, PM[('ev2a', j)])
            for h in range(H):
                sc.activation(out=qkhat[:, 0:HD], in_=qk_f32[:, h * HD:(h + 1) * HD],
                              func=AF.Square, accum_out=stat_b[:, h:h + 1])
            for h in range(H):
                last = sc.activation(out=qkhat[:, 0:HD], in_=xn_sc[:, h * HD:(h + 1) * HD],
                                     func=AF.Square, accum_out=stat_f[:, h:h + 1])
            sS.inc(last)
        PM[('stats', j)] = sS.n

    def f_muvar(j):
        @blk.vector
        def _(v, j=j):
            v.wait_ge(sS.h, PM[('stats', j)])
            VF(v, v.tensor_scalar(out=stat_a[:], in0=stat_a[:], scalar1=1.0 / HD, scalar2=None,
                                  op0=ALU.mult))
            VF(v, v.tensor_scalar(out=stat_e[:], in0=stat_e[:], scalar1=1.0 / HD, scalar2=None,
                                  op0=ALU.mult))
            VF(v, v.tensor_mul(out=stat_c[:], in0=stat_a[:], in1=stat_a[:]))
            VF(v, v.scalar_tensor_tensor(out=stat_b[:], in0=stat_b[:], scalar=1.0 / HD,
                                         in1=stat_c[:], op0=ALU.mult, op1=ALU.subtract))
            VF(v, v.tensor_mul(out=stat_c[:], in0=stat_e[:], in1=stat_e[:]))
            sV.inc(v.scalar_tensor_tensor(out=stat_f[:], in0=stat_f[:], scalar=1.0 / HD,
                                          in1=stat_c[:], op0=ALU.mult, op1=ALU.subtract))
        PM[('muvar', j)] = sV.n

    def f_sd(j):
        @blk.scalar
        def _(sc, j=j):
            sc.wait_ge(sV.h, PM[('muvar', j)])
            sc.activation(out=stat_b[:], in_=stat_b[:], func=AF.Sqrt, bias=EPS_LN)
            sS.inc(sc.activation(out=stat_f[:], in_=stat_f[:], func=AF.Sqrt, bias=EPS_LN))
        PM[('sd', j)] = sS.n

    def f_hat(j):
        @blk.vector
        def _(v, j=j):
            v.wait_ge(sS.h, PM[('sd', j)])
            if j >= 1:
                v.wait_ge(sM.h, PM[('tpB', j - 1)])
            VF(v, v.reciprocal(out=stat_d[:], in_=stat_b[:]))
            VF(v, v.reciprocal(out=stat_h[:], in_=stat_f[:]))
            for h in range(H):
                v.tensor_scalar(out=qkhat[:, h * HD:(h + 1) * HD],
                                in0=qk_f32[:, h * HD:(h + 1) * HD],
                                scalar1=stat_a[:, h:h + 1], scalar2=stat_d[:, h:h + 1],
                                op0=ALU.subtract, op1=ALU.mult)
            for h in range(H):
                last = v.tensor_scalar(out=qkhat[:, D + h * HD:D + (h + 1) * HD],
                                       in0=xn_sc[:, h * HD:(h + 1) * HD],
                                       scalar1=stat_e[:, h:h + 1], scalar2=stat_h[:, h:h + 1],
                                       op0=ALU.subtract, op1=ALU.mult)
            sV.inc(last)
        PM[('hat', j)] = sV.n

    def f_ev2v(j):
        @blk.vector
        def _(v, j=j):
            if j >= 1:
                v.wait_ge(sM.h, PM[('av', j - 1)])
            iv = invr_t[:, j:j + 1]
            v.tensor_scalar(out=v_bf[:, 0:512], in0=pq2[:], scalar1=iv, scalar2=None, op0=ALU.mult)
            sV.inc(v.tensor_scalar(out=v_bf[:, 512:1024], in0=pSA[:], scalar1=iv, scalar2=None,
                                   op0=ALU.mult))
        PM[('ev2v', j)] = sV.n

    def f_tpA(j):
        @blk.tensor
        def _(t, j=j):
            t.wait_ge(sV.h, PM[('ev2v', j)])
            for c in range(12):
                mm = t.matmul(fb[c % 3][:, (c // 3) * 128:(c // 3) * 128 + 128],
                              qkhat[:, c * 128:(c + 1) * 128], I_sb[:],
                              start=True, stop=True)
            sM.inc(mm)
        PM[('tpA', j)] = sM.n

    def f_gbA(j):
        @blk.vector
        def _(v, j=j):
            v.wait_ge(sM.h, PM[('tpA', j)])
            for c in range(12):
                g_ap = lnq_g[:] if c < 8 else lnk_g[:]
                b_ap = lnq_b[:] if c < 8 else lnk_b[:]
                last = v.tensor_scalar(out=qkT[:, c * 128:(c + 1) * 128],
                                       in0=fb[c % 3][:, (c // 3) * 128:(c // 3) * 128 + 128],
                                       scalar1=g_ap, scalar2=b_ap, op0=ALU.mult, op1=ALU.add)
            sV.inc(last)
        PM[('gbA', j)] = sV.n

    def f_tpB(j):
        @blk.tensor
        def _(t, j=j):
            t.wait_ge(sV.h, PM[('gbA', j)])
            for c in range(12, 16):
                mm = t.matmul(pq[:, (c - 12) * 128:(c - 12) * 128 + 128],
                              qkhat[:, c * 128:(c + 1) * 128], I_sb[:],
                              start=True, stop=True)
            sM.inc(mm)
        PM[('tpB', j)] = sM.n

    def f_gbB(j):
        @blk.vector
        def _(v, j=j):
            v.wait_ge(sM.h, PM[('tpB', j)])
            for c in range(12, 16):
                last = v.tensor_scalar(out=qkT[:, c * 128:(c + 1) * 128],
                                       in0=pq[:, (c - 12) * 128:(c - 12) * 128 + 128],
                                       scalar1=lnk_g[:], scalar2=lnk_b[:],
                                       op0=ALU.mult, op1=ALU.add)
            sV.inc(last)
        PM[('gbB', j)] = sV.n

    def b_smm(j):
        for h in range(H):
            @blk.tensor
            def _(t, j=j, h=h):
                if h == 0:
                    t.wait_ge(sV.h, PM[('gbB', j)])
                    if j >= 1:
                        t.wait_ge(sV.h, PM[('add', j - 1)])
                c = h // 2
                r0 = (h % 2) * HD
                sM.inc(t.matmul(bb[h // 4][:, (h % 4) * 128:(h % 4) * 128 + 128],
                                qkT[r0:r0 + HD, c * 128:(c + 1) * 128],
                                qkT[r0:r0 + HD, D + c * 128:D + (c + 1) * 128],
                                start=True, stop=True))
        PM[('smm', j)] = sM.n

    def b_exp(j):
        @blk.scalar
        def _(sc, j=j):
            sc.wait_ge(sM.h, PM[('smm', j)])
            for h in range(H):
                bank = bb[h // 4]
                qtr = (h % 4) * 128
                sc.activation(out=A_bf[0:64, h * 128:h * 128 + 64],
                              in_=bank[0:64, qtr:qtr + 64], func=AF.Exp,
                              bias=-8.0, accum_out=sums16[0:64, h:h + 1])
                last = sc.activation(out=A_bf[64:128, h * 128 + 64:h * 128 + 128],
                                     in_=bank[64:128, qtr + 64:qtr + 128], func=AF.Exp,
                                     bias=-8.0, accum_out=sums16[64:128, h:h + 1])
            sS.inc(last)
        PM[('exp', j)] = sS.n

    def b_norm(j):
        @blk.vector
        def _(v, j=j):
            v.wait_ge(sS.h, PM[('exp', j)])
            VF(v, v.reciprocal(out=invs16[:], in_=sums16[:]))
            for h in range(H):
                last = v.tensor_scalar(out=A_bf[:, h * 128:(h + 1) * 128],
                                       in0=A_bf[:, h * 128:(h + 1) * 128],
                                       scalar1=invs16[:, h:h + 1], scalar2=None, op0=ALU.mult)
            sV.inc(last)
        PM[('norm', j)] = sV.n

    def b_at(j):
        @blk.tensor
        def _(t, j=j):
            t.wait_ge(sV.h, PM[('norm', j)])
            for h in range(H):
                mm = t.matmul(bb[h % 4][:, (h // 4) * 128:(h // 4) * 128 + 128],
                              A_bf[:, h * 128:(h + 1) * 128], I_sb[:],
                              start=True, stop=True)
            sM.inc(mm)
        PM[('at', j)] = sM.n

    def b_atev(j):
        @blk.vector
        def _(v, j=j):
            v.wait_ge(sM.h, PM[('at', j)])
            for h in range(H):
                last = v.tensor_copy(out=AT_bf[:, h * 128:(h + 1) * 128],
                                     in_=bb[h % 4][:, (h // 4) * 128:(h // 4) * 128 + 128])
            sV.inc(last)
        PM[('atev', j)] = sV.n

    def b_av(j):
        @blk.tensor
        def _(t, j=j):
            t.wait_ge(sV.h, PM[('atev', j)])
            for h in range(H):
                c = h // 2
                r0 = (h % 2) * HD
                bank = po[0] if c < 4 else po[1]
                mm = t.matmul(bank[r0:r0 + HD, (c % 4) * 128:(c % 4) * 128 + 128],
                              v_bf[:, h * HD:(h + 1) * HD],
                              AT_bf[:, h * 128:(h + 1) * 128], start=True, stop=True)
            sM.inc(mm)
        PM[('av', j)] = sM.n

    def b_ao(j):
        @blk.scalar
        def _(sc, j=j):
            sc.wait_ge(sM.h, PM[('av', j)])
            for c in range(8):
                bank = po[0] if c < 4 else po[1]
                last = sc.activation(out=ao_fm[:, c * 128:(c + 1) * 128],
                                     in_=bank[:, (c % 4) * 128:(c % 4) * 128 + 128], func=AF.Copy)
            sS.inc(last)
        PM[('ao', j)] = sS.n

    def b_proj(j):
        @blk.tensor
        def _(t, j=j):
            t.wait_ge(sS.h, PM[('ao', j)])
            for of in range(2):
                dst = po[2 + of]
                for dc in range(8):
                    t.matmul(dst[:], ao_fm[:, dc * 128:(dc + 1) * 128],
                             wproj_sb[:, dc * D + of * 512:dc * D + of * 512 + 512],
                             start=(dc == 0), stop=False)
                mm = t.matmul(dst[:], ones1r_b[:], rows_b[0:1, of * 512:of * 512 + 512],
                              start=False, stop=True)
            sM.inc(mm)
        PM[('proj', j)] = sM.n

    def b_add(j):
        @blk.vector
        def _(v, j=j):
            v.wait_ge(sM.h, PM[('proj', j)])
            for of in range(2):
                sl = outacc[:, j * D + of * 512:j * D + of * 512 + 512]
                last = v.tensor_add(out=sl, in0=po[2 + of][:], in1=sl)
            sV.inc(last)
        PM[('add', j)] = sV.n

    def emit_back(j):
        b_smm(j); b_exp(j); b_norm(j); b_at(j); b_atev(j)

    for j in range(NT):
        f_w1(j); f_ev1(j); f_w2(j); f_ev2a(j); f_stats(j)
        f_muvar(j)
        _dr('vbf', 2)
        f_sd(j); f_hat(j)
        _dr('vbf', 2)
        if j >= 1:
            b_smm(j - 1); b_exp(j - 1); b_norm(j - 1); b_at(j - 1); b_atev(j - 1)
            b_av(j - 1)
        f_ev2v(j)
        _dr('vbf', 2)
        f_tpA(j); f_gbA(j); f_tpB(j); f_gbB(j)
        if j >= 1:
            b_ao(j - 1); b_proj(j - 1); b_add(j - 1)
            _dr('proj', 3)
    j = NT - 1
    b_smm(j); b_exp(j); b_norm(j); b_at(j); b_atev(j)
    b_av(j); b_ao(j); b_proj(j); b_add(j)
    V_ATTN = sV.n

    wctx.close()
    hctx = ExitStack()
    he = hctx.enter_context
    hl_fm = he(nc.sbuf_tensor("hl_fm", [128, 4 * T], BF16))
    w1h_sb = he(nc.sbuf_tensor("w1h_sb", [128, 8 * 1024], BF16))
    w2h_sb = he(nc.sbuf_tensor("w2h_sb", [128, 8 * 1024], BF16))
    w1l_sb = he(nc.sbuf_tensor("w1l_sb", [128, 8 * LHID], BF16))
    w2l_sb = he(nc.sbuf_tensor("w2l_sb", [128, 4 * D], BF16))

    @blk.gpsimd
    def _(g):
        g.wait_ge(sV.h, V_ATTN)
        for kc in range(8):
            sW.inc(g.dma_start(out=w1l_sb[:, kc * LHID:(kc + 1) * LHID],
                               in_=w1l_d[kc * 128:(kc + 1) * 128, :]), 16)
        for hc in range(4):
            sW.inc(g.dma_start(out=w2l_sb[:, hc * D:(hc + 1) * D],
                               in_=w2l_d[hc * 128:(hc + 1) * 128, :]), 16)
    W_LGT = sW.n

    @blk.vector
    def _(v):
        v.wait_ge(sW.h, W_LGT)
        for kc in range(8):
            last = v.tensor_scalar(out=w1l_sb[:, kc * LHID:(kc + 1) * LHID],
                                   in0=w1l_sb[:, kc * LHID:(kc + 1) * LHID],
                                   scalar1=gl_sb[:, kc:kc + 1], scalar2=None, op0=ALU.mult)
        sV.inc(last)
    V_WL = sV.n

    l_mark = {}
    lt_ready = {}
    lo_add = {}

    def emit_hidden(tg):
        for hc in range(4):
            idx = tg * 4 + hc

            @blk.tensor
            def _(t, hc=hc, tg=tg, idx=idx):
                if idx == 0:
                    t.wait_ge(sV.h, V_WL)
                if idx >= 2:
                    t.wait_ge(sS.h, l_mark[idx - 2])
                dst = pq if idx % 2 == 0 else pq2
                for kc in range(8):
                    mm = t.matmul(dst[:], w1l_sb[:, kc * LHID + hc * 128:kc * LHID + hc * 128 + 128],
                                  nfm[:, kc * T + tg * 512:kc * T + (tg + 1) * 512],
                                  start=(kc == 0), stop=(kc == 7))
                sM.inc(mm)

            @blk.scalar
            def _(sc, hc=hc, tg=tg, idx=idx):
                sc.wait_ge(sM.h, sM.n)
                src = pq if idx % 2 == 0 else pq2
                sS.inc(sc.activation(out=hl_fm[:, hc * T + tg * 512:hc * T + (tg + 1) * 512],
                                     in_=src[:], func=AF.Gelu, bias=b1l_sb[:, hc:hc + 1]))
            l_mark[idx] = sS.n
            _dr('light', min(2, max(0, len(RQ) - 3)))
        lt_ready[tg] = sS.n

    def emit_out(j):
        @blk.tensor
        def _(t, j=j):
            t.wait_ge(sS.h, lt_ready[j // 4])
            if j >= 2:
                t.wait_ge(sV.h, lo_add[j - 2])
            for of in range(2):
                dst = po[2 * (j % 2) + of]
                for hc in range(4):
                    t.matmul(dst[:], hl_fm[:, hc * T + j * 128:hc * T + j * 128 + 128],
                             w2l_sb[:, hc * D + of * 512:hc * D + of * 512 + 512],
                             start=(hc == 0), stop=False)
                mm = t.matmul(dst[:], ones1r_b[:], rows_b[0:1, D + of * 512:D + of * 512 + 512],
                              start=False, stop=True)
            sM.inc(mm)

        @blk.vector
        def _(v, j=j):
            v.wait_ge(sM.h, sM.n)
            for of in range(2):
                sl = outacc[:, j * D + of * 512:j * D + of * 512 + 512]
                last = v.tensor_add(out=sl, in0=po[2 * (j % 2) + of][:], in1=sl)
            sV.inc(last)
        lo_add[j] = sV.n

    emit_hidden(0)
    emit_hidden(1)
    for j in range(4):
        emit_out(j)
    emit_hidden(2)
    for j in range(4, 8):
        emit_out(j)
    emit_hidden(3)
    for j in range(8, 16):
        emit_out(j)
    V_LIGHT = sV.n

    drain_routing(len(RQ))
    V_MASK = sV.n

    if dbg:
        @blk.sync
        def _(sy):
            sy.wait_ge(sV.h, V_MASK)
            sy.wait_ge(sT.h, T_NFM)
            sSt.inc(sy.dma_start(out=dbg_rt[:, 0:NT], in_=s_t[:]), 16)
            sSt.inc(sy.dma_start(out=dbg_rt[:, NT:2 * NT], in_=invr_t[:]), 16)
            sSt.inc(sy.dma_start(out=dbg_rt[:, 2 * NT:3 * NT], in_=mask_t[:]), 16)
            sSt.inc(sy.dma_start(out=dbg_rt[:, 3 * NT:4 * NT], in_=key_t[:]), 16)
            sSt.inc(sy.dma_start(out=dbg_rt[:, 64:65], in_=smax_b[:], allow_slow_non_contiguous=True), 16)
            sSt.inc(sy.dma_start(out=dbg_rt[:, 65:66], in_=Ab[:], allow_slow_non_contiguous=True), 16)
            sSt.inc(sy.dma_start(out=dbg_rt[:, 66:67], in_=tadb[:], allow_slow_non_contiguous=True), 16)
            sSt.inc(sy.dma_start(out=dbg_rt[:, 67:68], in_=lob[:], allow_slow_non_contiguous=True), 16)
            sSt.inc(sy.dma_start(out=dbg_rt[:, 68:69], in_=hib[:], allow_slow_non_contiguous=True), 16)
            sSt.inc(sy.dma_start(out=dbg_rt[:, 69:70], in_=fcb[:], allow_slow_non_contiguous=True), 16)
            sSt.inc(sy.dma_start(out=dbg_rt[0:1, 70:71], in_=gam_t[:], allow_slow_non_contiguous=True), 16)
            sSt.inc(sy.dma_start(out=dbg_rt[0:1, 71:72], in_=tadp[:], allow_slow_non_contiguous=True), 16)
            sSt.inc(sy.dma_start(out=dbg_traj, in_=traj[:]), 16)
            sSt.inc(sy.dma_start(out=dbg_nfm, in_=nfm[:]), 16)


    if dbg:
        @blk.sync
        def _(sy):
            sy.wait_ge(sV.h, V_LIGHT)
            for j in range(NT):
                sSt.inc(sy.dma_start(out=dbg_light[j * 128:(j + 1) * 128, :],
                                     in_=outacc[:, j * D:(j + 1) * D]), 16)

    HV_STATE = {'gm': None, 'am': None}
    for g in range(4):
        @blk.gpsimd
        def _(gg, g=g):
            if g == 0:
                gg.wait_ge(sV.h, V_ATTN)
            else:
                gg.wait_ge(sM.h, sM.n)
            for kc in range(8):
                sWH.inc(gg.dma_start(out=w1h_sb[:, kc * 1024:(kc + 1) * 1024],
                                     in_=w1h_d[kc * 128:(kc + 1) * 128,
                                               g * 1024:(g + 1) * 1024]), 16)
            for hc in range(8):
                sWH.inc(gg.dma_start(out=w2h_sb[:, hc * 1024:(hc + 1) * 1024],
                                     in_=w2h_d[g * 1024 + hc * 128:g * 1024 + (hc + 1) * 128, :]), 16)
            sWH.inc(gg.dma_start(out=b1h_sb[:],
                                 in_=b1h_d[g * 1024:(g + 1) * 1024].rearrange(
                                     "(c p) -> p c", p=128),
                                 allow_slow_non_contiguous=True), 16)
        W_G = sWH.n

        @blk.vector
        def _(v, g=g, W_G=W_G):
            v.wait_ge(sWH.h, W_G)
            if g == 0:
                v.wait_ge(sV.h, V_LIGHT)
            for kc in range(8):
                last = v.tensor_scalar(out=w1h_sb[:, kc * 1024:(kc + 1) * 1024],
                                       in0=w1h_sb[:, kc * 1024:(kc + 1) * 1024],
                                       scalar1=gh_sb[:, kc:kc + 1], scalar2=None, op0=ALU.mult)
            sV.inc(last)
        V_WG = sV.n

        hq = [(pq, 0), (pq, 256), (pq2, 0), (pq2, 256), (pSA, 0), (pSA, 256), (po[0], 0), (po[0], 256)]
        for tg in range(8):
            @blk.tensor
            def _(t, g=g, tg=tg, V_WG=V_WG):
                if tg == 0:
                    t.wait_ge(sV.h, V_WG)
                if HV_STATE['gm'] is not None:
                    t.wait_ge(sS.h, HV_STATE['gm'])
                for hc in range(8):
                    bank, off = hq[hc]
                    for kc in range(8):
                        mm = t.matmul(bank[:, off:off + 256],
                                      w1h_sb[:, kc * 1024 + hc * 128:kc * 1024 + hc * 128 + 128],
                                      nfm[:, kc * T + tg * 256:kc * T + (tg + 1) * 256],
                                      start=(kc == 0), stop=(kc == 7))
                    sM.inc(mm)
                    HV_STATE[('h', hc)] = sM.n

            for hc in range(8):
                @blk.scalar
                def _(sc, g=g, hc=hc):
                    sc.wait_ge(sM.h, HV_STATE[('h', hc | 1)])
                    bank, off = hq[hc]
                    sS.inc(sc.activation(out=A_bf[:, hc * 256:(hc + 1) * 256],
                                         in_=bank[:, off:off + 256], func=AF.Gelu,
                                         bias=b1h_sb[:, hc:hc + 1]))
                HV_STATE[('g', hc)] = sS.n
            HV_STATE['gm'] = sS.n

            for hc in range(8):
                @blk.tensor
                def _(t, g=g, hc=hc):
                    t.wait_ge(sS.h, HV_STATE[('g', hc)])
                    if hc == 0 and HV_STATE['am'] is not None:
                        t.wait_ge(sV.h, HV_STATE['am'])
                    ob = [po[1], po[2], po[3], pmisc]
                    for st in range(2):
                        for of in range(2):
                            mm = t.matmul(ob[st * 2 + of][:],
                                          A_bf[:, hc * 256 + st * 128:hc * 256 + st * 128 + 128],
                                          w2h_sb[:, hc * 1024 + of * 512:hc * 1024 + of * 512 + 512],
                                          start=(hc == 0), stop=(hc == 7 and g < 3))
                    if g == 3 and hc == 7:
                        for st in range(2):
                            for of in range(2):
                                mm = t.matmul(ob[st * 2 + of][:], ones1r_b[:],
                                              rows_b[0:1, 2 * D + of * 512:2 * D + of * 512 + 512],
                                              start=False, stop=True)
                    sM.inc(mm)

            @blk.vector
            def _(v, g=g, tg=tg):
                v.wait_ge(sM.h, sM.n)
                ob = [po[1], po[2], po[3], pmisc]
                for st in range(2):
                    jj = tg * 2 + st
                    for of in range(2):
                        sl = outacc[:, jj * D + of * 512:jj * D + of * 512 + 512]
                        last = v.scalar_tensor_tensor(out=sl, in0=ob[st * 2 + of][:],
                                                      scalar=mask_t[:, jj:jj + 1], in1=sl,
                                                      op0=ALU.mult, op1=ALU.add)
                sV.inc(last)
            HV_STATE['am'] = sV.n

    @blk.sync
    def _(sy):
        sy.wait_ge(sV.h, sV.n)
        for j in range(NT):
            sSt.inc(sy.dma_start(out=out_d[j * 128:(j + 1) * 128, :],
                                 in_=outacc[:, j * D:(j + 1) * D]), 16)
        sy.wait_ge(sSt.h, sSt.n)

    hctx.close()
    ctx.close()
    return nc


_NC_CACHE = []


def kernel(L0_attn_x, L0_attn_W_qkv, L0_attn_ln_gamma, L0_attn_ln_beta, L0_attn_W_proj,
           L0_attn_b_proj, L0_ffn_routing_token, L0_ffn_gamma_light, L0_ffn_w1_light,
           L0_ffn_b1_light, L0_ffn_w2_light, L0_ffn_b2_light, L0_ffn_gamma_heavy,
           L0_ffn_w1_heavy, L0_ffn_b1_heavy, L0_ffn_w2_heavy, L0_ffn_b2_heavy):
    x = np.ascontiguousarray(np.asarray(L0_attn_x, dtype=np.float32))
    shared = {
        "wqkv": np.ascontiguousarray(L0_attn_W_qkv, dtype=np.float32),
        "lng": np.ascontiguousarray(L0_attn_ln_gamma, dtype=np.float32),
        "lnb": np.ascontiguousarray(L0_attn_ln_beta, dtype=np.float32),
        "wproj": np.ascontiguousarray(L0_attn_W_proj, dtype=np.float32),
        "bproj": np.ascontiguousarray(L0_attn_b_proj, dtype=np.float32),
        "rt": np.ascontiguousarray(L0_ffn_routing_token, dtype=np.float32),
        "gl": np.ascontiguousarray(L0_ffn_gamma_light, dtype=np.float32),
        "w1l": np.ascontiguousarray(L0_ffn_w1_light, dtype=np.float32),
        "b1l": np.ascontiguousarray(L0_ffn_b1_light, dtype=np.float32),
        "w2l": np.ascontiguousarray(L0_ffn_w2_light, dtype=np.float32),
        "b2l": np.ascontiguousarray(L0_ffn_b2_light, dtype=np.float32),
        "gh": np.ascontiguousarray(L0_ffn_gamma_heavy, dtype=np.float32),
        "w1h": np.ascontiguousarray(L0_ffn_w1_heavy, dtype=np.float32),
        "b1h": np.ascontiguousarray(L0_ffn_b1_heavy, dtype=np.float32),
        "w2h": np.ascontiguousarray(L0_ffn_w2_heavy, dtype=np.float32),
        "b2h": np.ascontiguousarray(L0_ffn_b2_heavy, dtype=np.float32),
    }
    if not _NC_CACHE:
        _NC_CACHE.append(build_program())
    nc = _NC_CACHE[0]
    in_maps = [dict(shared, x=np.ascontiguousarray(x[b])) for b in range(8)]
    res = run_bass_kernel_spmd(nc, in_maps, core_ids=list(range(8)))
    return np.stack([np.asarray(r["out"]) for r in res.results], axis=0)


# version 13
# speedup vs baseline: 1.1032x; 1.1032x over previous
import numpy as np
from ml_dtypes import bfloat16 as _np_bf16
import concourse.bass as bass
import concourse.mybir as mybir
from concourse.bass_utils import run_bass_kernel_spmd
from contextlib import ExitStack

F32 = mybir.dt.float32
BF16 = mybir.dt.bfloat16
AF = mybir.ActivationFunctionType
ALU = mybir.AluOpType
AX = mybir.AxisListType

T = 2048
D = 1024
NT = 16
H = 16
HD = 64
LHID = 512
HHID = 4096
NHEAVY = 1024
EPS_LN = 1e-5
C0 = float(np.float32(0.1) * np.float32(np.log(8.0)))
T32 = -87.3365478515625
T64 = -745.2
N_CD = 50
N_BIS = 52


class Sem:
    def __init__(self, handle):
        self.h = handle
        self.n = 0

    def inc(self, instr, k=1):
        instr.then_inc(self.h, k)
        self.n += k
        return self.n


def build_program(dbg=False):
    nc = bass.Bass("TRN2", target_bir_lowering=False, detect_race_conditions=False)

    x_d = nc.dram_tensor("x", [T, D], F32, kind="ExternalInput").ap()
    wqkv_d = nc.dram_tensor("wqkv", [D, 3 * D], F32, kind="ExternalInput").ap()
    lng_d = nc.dram_tensor("lng", [HD], F32, kind="ExternalInput").ap()
    lnb_d = nc.dram_tensor("lnb", [HD], F32, kind="ExternalInput").ap()
    wproj_d = nc.dram_tensor("wproj", [D, D], F32, kind="ExternalInput").ap()
    bproj_d = nc.dram_tensor("bproj", [D], F32, kind="ExternalInput").ap()
    rt_d = nc.dram_tensor("rt", [D], F32, kind="ExternalInput").ap()
    gl_d = nc.dram_tensor("gl", [D], F32, kind="ExternalInput").ap()
    w1l_d = nc.dram_tensor("w1l", [D, LHID], F32, kind="ExternalInput").ap()
    b1l_d = nc.dram_tensor("b1l", [LHID], F32, kind="ExternalInput").ap()
    w2l_d = nc.dram_tensor("w2l", [LHID, D], F32, kind="ExternalInput").ap()
    b2l_d = nc.dram_tensor("b2l", [D], F32, kind="ExternalInput").ap()
    gh_d = nc.dram_tensor("gh", [D], F32, kind="ExternalInput").ap()
    w1h_d = nc.dram_tensor("w1h", [D, HHID], F32, kind="ExternalInput").ap()
    b1h_d = nc.dram_tensor("b1h", [HHID], F32, kind="ExternalInput").ap()
    w2h_d = nc.dram_tensor("w2h", [HHID, D], F32, kind="ExternalInput").ap()
    b2h_d = nc.dram_tensor("b2h", [D], F32, kind="ExternalInput").ap()
    out_d = nc.dram_tensor("out", [T, D], F32, kind="ExternalOutput").ap()
    if dbg:
        dbg_rt = nc.dram_tensor("dbg_rt", [128, NT * 4 + 8], F32, kind="ExternalOutput").ap()
        dbg_traj = nc.dram_tensor("dbg_traj", [1, 128], F32, kind="ExternalOutput").ap()
        dbg_attn = nc.dram_tensor("dbg_attn", [T, D], F32, kind="ExternalOutput").ap()
        dbg_light = nc.dram_tensor("dbg_light", [T, D], F32, kind="ExternalOutput").ap()
        dbg_nfm = nc.dram_tensor("dbg_nfm", [128, 8 * T], BF16, kind="ExternalOutput").ap()
        dbg_qkT = nc.dram_tensor("dbg_qkT", [128, 2 * D], BF16, kind="ExternalOutput").ap()
        dbg_A = nc.dram_tensor("dbg_A", [128, H * 128], BF16, kind="ExternalOutput").ap()
        dbg_ao = nc.dram_tensor("dbg_ao", [128, D], BF16, kind="ExternalOutput").ap()
        dbg_qh13 = nc.dram_tensor("dbg_qh13", [128, 2 * D], BF16, kind="ExternalOutput").ap()
        dbg_qkT13 = nc.dram_tensor("dbg_qkT13", [128, 2 * D], BF16, kind="ExternalOutput").ap()
        dbg_A13 = nc.dram_tensor("dbg_A13", [128, H * 128], BF16, kind="ExternalOutput").ap()
        dbg_AT13 = nc.dram_tensor("dbg_AT13", [128, H * 128], BF16, kind="ExternalOutput").ap()
        dbg_ao13 = nc.dram_tensor("dbg_ao13", [128, D], BF16, kind="ExternalOutput").ap()
        dbg_v13 = nc.dram_tensor("dbg_v13", [128, D], BF16, kind="ExternalOutput").ap()
    ds_s = nc.dram_tensor("ds_s", [128], F32, kind="Internal").ap()
    ds_c = nc.dram_tensor("ds_c", [NT], F32, kind="Internal").ap()

    Lmat = np.triu(np.ones((128, 128), dtype=np.float32))
    U16 = np.triu(np.ones((NT, NT), dtype=np.float32), 1)
    L_d = nc.inline_tensor(Lmat, name="Lmat")
    I_d = nc.inline_tensor(np.eye(128).astype(_np_bf16), name="I128")
    U_d = nc.inline_tensor(U16, name="U16")

    ctx = ExitStack()
    e = ctx.enter_context

    outacc = e(nc.sbuf_tensor("outacc", [128, NT * D], F32))
    nfm = e(nc.sbuf_tensor("nfm", [128, 8 * T], BF16))
    s_t = e(nc.sbuf_tensor("s_t", [128, NT], F32))
    invr_t = e(nc.sbuf_tensor("invr_t", [128, NT], F32))
    mask_t = e(nc.sbuf_tensor("mask_t", [128, NT], F32))
    key_t = e(nc.sbuf_tensor("key_t", [128, NT], F32))
    gl_sb = e(nc.sbuf_tensor("gl_sb", [128, 8], F32))
    gh_sb = e(nc.sbuf_tensor("gh_sb", [128, 8], F32))
    b1l_sb = e(nc.sbuf_tensor("b1l_sb", [128, 4], F32))
    b1h_sb = e(nc.sbuf_tensor("b1h_sb", [128, 8], F32))
    rows_b = e(nc.sbuf_tensor("rows_b", [1, 3 * D], BF16))
    lnq_g = e(nc.sbuf_tensor("lnq_g", [128, 1], F32))
    lnq_b = e(nc.sbuf_tensor("lnq_b", [128, 1], F32))
    lnk_g = e(nc.sbuf_tensor("lnk_g", [128, 1], F32))
    lnk_b = e(nc.sbuf_tensor("lnk_b", [128, 1], F32))
    rt_rep = e(nc.sbuf_tensor("rt_rep", [128, D], F32))
    rt_row = e(nc.sbuf_tensor("rt_row", [1, D], F32))
    L_sb = e(nc.sbuf_tensor("L_sb", [128, 128], F32))
    U_sb = e(nc.sbuf_tensor("U_sb", [NT, NT], F32))
    ones128c = e(nc.sbuf_tensor("ones128c", [128, 1], F32))
    ones1r = e(nc.sbuf_tensor("ones1r", [1, 128], F32))
    ones1r_b = e(nc.sbuf_tensor("ones1r_b", [1, 128], BF16))
    smax_b = e(nc.sbuf_tensor("smax_b", [128, 1], F32))
    Ab = e(nc.sbuf_tensor("Ab", [128, 1], F32))
    mmv = e(nc.sbuf_tensor("mmv", [128, 1], F32))
    negmm10 = e(nc.sbuf_tensor("negmm10", [128, 1], F32))
    mmc = e(nc.sbuf_tensor("mmc", [1, 1], F32))
    lnv = e(nc.sbuf_tensor("lnv", [1, 1], F32))
    Asc = e(nc.sbuf_tensor("Asc", [1, 1], F32))
    zt = e(nc.sbuf_tensor("zt", [128, NT], F32))
    rowsum = e(nc.sbuf_tensor("rowsum", [128, 1], F32))
    tr128 = e(nc.sbuf_tensor("tr128", [1, 128], F32))
    tr16b = e(nc.sbuf_tensor("tr16b", [NT, 1], F32))
    tr1 = e(nc.sbuf_tensor("tr1", [1, NT], F32))
    sm1 = e(nc.sbuf_tensor("sm1", [1, 1], F32))
    lo_t = e(nc.sbuf_tensor("lo_t", [1, 1], F32))
    hi_t = e(nc.sbuf_tensor("hi_t", [1, 1], F32))
    mid_t = e(nc.sbuf_tensor("mid_t", [1, 1], F32))
    midb = e(nc.sbuf_tensor("midb", [128, 1], F32))
    gam_t = e(nc.sbuf_tensor("gam_t", [1, 1], F32))
    tmp1 = e(nc.sbuf_tensor("tmp1", [1, 1], F32))
    tmp2 = e(nc.sbuf_tensor("tmp2", [1, 1], F32))
    tmp3 = e(nc.sbuf_tensor("tmp3", [1, 1], F32))
    tadp = e(nc.sbuf_tensor("tadp", [1, 1], F32))
    tadb = e(nc.sbuf_tensor("tadb", [128, 1], F32))
    lob = e(nc.sbuf_tensor("lob", [128, 1], F32))
    hib = e(nc.sbuf_tensor("hib", [128, 1], F32))
    nhi_t = e(nc.sbuf_tensor("nhi_t", [1, 1], F32))
    fcb = e(nc.sbuf_tensor("fcb", [128, 1], F32))
    cmpt = e(nc.sbuf_tensor("cmpt", [128, NT], F32))
    eqt = e(nc.sbuf_tensor("eqt", [128, NT], F32))
    colpre_sb = e(nc.sbuf_tensor("colpre_sb", [1, NT], F32))
    traj = e(nc.sbuf_tensor("traj", [1, 128], F32))
    stat_a = e(nc.sbuf_tensor("stat_a", [128, H], F32))
    stat_b = e(nc.sbuf_tensor("stat_b", [128, H], F32))
    stat_c = e(nc.sbuf_tensor("stat_c", [128, H], F32))
    stat_d = e(nc.sbuf_tensor("stat_d", [128, H], F32))
    stat_e = e(nc.sbuf_tensor("stat_e", [128, H], F32))
    stat_f = e(nc.sbuf_tensor("stat_f", [128, H], F32))
    stat_h = e(nc.sbuf_tensor("stat_h", [128, H], F32))
    col_a = e(nc.sbuf_tensor("col_a", [128, 1], F32))
    col_b = e(nc.sbuf_tensor("col_b", [128, 1], F32))
    col_c = e(nc.sbuf_tensor("col_c", [128, 1], F32))
    col_d = e(nc.sbuf_tensor("col_d", [128, 1], F32))
    col_e = e(nc.sbuf_tensor("col_e", [128, 1], F32))
    col_f = e(nc.sbuf_tensor("col_f", [128, 1], F32))
    col_g = e(nc.sbuf_tensor("col_g", [128, 1], F32))
    ca4 = e(nc.sbuf_tensor("ca4", [128, 4], F32))
    cb4 = e(nc.sbuf_tensor("cb4", [128, 4], F32))
    cc4 = e(nc.sbuf_tensor("cc4", [128, 4], F32))
    cd4 = e(nc.sbuf_tensor("cd4", [128, 4], F32))
    ce4 = e(nc.sbuf_tensor("ce4", [128, 4], F32))
    cf4 = e(nc.sbuf_tensor("cf4", [128, 4], F32))
    cg4 = e(nc.sbuf_tensor("cg4", [128, 4], F32))
    ch4 = e(nc.sbuf_tensor("ch4", [128, 4], F32))
    rsum_c = e(nc.sbuf_tensor("rsum_c", [128, 1], F32))
    nrsum_c = e(nc.sbuf_tensor("nrsum_c", [128, 1], F32))
    I_sb = e(nc.sbuf_tensor("I_sb", [128, 128], BF16))
    cbias = e(nc.sbuf_tensor("cbias", [128, 2], F32))
    nc.const_aps.register(F32, EPS_LN, cbias[:, 0:1])
    nc.const_aps.register(F32, -8.0, cbias[:, 1:2])
    xn_sc = e(nc.sbuf_tensor("xn_sc", [128, D], F32))
    qk_f32 = e(nc.sbuf_tensor("qk_f32", [128, D], F32))
    qkhat = e(nc.sbuf_tensor("qkhat", [128, 2 * D], BF16))
    ntm_sc = qkhat[:, 0:D]
    qkT = e(nc.sbuf_tensor("qkT", [128, 2 * D], BF16))
    v_bf = e(nc.sbuf_tensor("v_bf", [128, D], BF16))
    A_bf = e(nc.sbuf_tensor("A_bf", [128, H * 128], BF16))
    AT_bf = e(nc.sbuf_tensor("AT_bf", [128, H * 128], BF16))
    ao_fm = e(nc.sbuf_tensor("ao_fm", [128, D], BF16))
    sums16 = e(nc.sbuf_tensor("sums16", [128, H], F32))
    invs16 = e(nc.sbuf_tensor("invs16", [128, H], F32))
    h_sc = e(nc.sbuf_tensor("h_sc", [128, 256], BF16))

    pq = e(nc.psum_tensor("pq", [128, 512], F32))
    pq2 = e(nc.psum_tensor("pq2", [128, 512], F32))
    pSA = e(nc.psum_tensor("pSA", [128, 512], F32))
    po = [e(nc.psum_tensor(f"po{i}", [128, 512], F32)) for i in range(4)]
    pmisc = e(nc.psum_tensor("pmisc", [128, 512], F32))
    pS = pSA[:, 0:256]
    pAV = pSA[:, 256:512]
    pr = pmisc[:, 48:48 + NT]
    pb = pmisc[:, 0:1]
    pb2 = pmisc[:, 1:2]
    p1 = pmisc[0:1, 16:32]
    p2 = pmisc[0:1, 32:48]

    sW = Sem(e(nc.semaphore("sW")))
    sX = Sem(e(nc.semaphore("sX")))
    sT = Sem(e(nc.semaphore("sT")))
    sSt = Sem(e(nc.semaphore("sSt")))
    sV = Sem(e(nc.semaphore("sV")))
    sS = Sem(e(nc.semaphore("sS")))
    sM = Sem(e(nc.semaphore("sM")))
    sG = Sem(e(nc.semaphore("sG")))
    sWB = Sem(e(nc.semaphore("sWB")))
    sWH = Sem(e(nc.semaphore("sWH")))

    def VF(v, ins):
        sV.inc(ins)
        v.wait_ge(sV.h, sV.n)

    blk = e(nc.Block())

    @blk.sync
    def _(sy):
        for dma in (
            lambda: sy.dma_start(out=L_sb[:], in_=L_d.ap()),
            lambda: sy.dma_start(out=U_sb[:], in_=U_d.ap()),
            lambda: sy.dma_start(out=rt_row[:], in_=rt_d[None, :]),
            lambda: sy.dma_start(out=col_e[0:64, :], in_=lng_d[:, None]),
            lambda: sy.dma_start(out=col_e[64:128, :], in_=lng_d[:, None]),
            lambda: sy.dma_start(out=col_f[0:64, :], in_=lnb_d[:, None]),
            lambda: sy.dma_start(out=col_f[64:128, :], in_=lnb_d[:, None]),
            lambda: sy.dma_start(out=gl_sb[:], in_=gl_d.rearrange("(c p) -> p c", p=128), allow_slow_non_contiguous=True),
            lambda: sy.dma_start(out=gh_sb[:], in_=gh_d.rearrange("(c p) -> p c", p=128), allow_slow_non_contiguous=True),
            lambda: sy.dma_start(out=b1l_sb[:], in_=b1l_d.rearrange("(c p) -> p c", p=128), allow_slow_non_contiguous=True),
        ):
            sX.inc(dma(), 16)
        X_CONST_ = sX.n
        for j in range(NT):
            sX.inc(sy.dma_start(out=outacc[:, j * D:(j + 1) * D],
                                in_=x_d[j * 128:(j + 1) * 128, :]), 16)
    X_CONST = 16 * 10
    X_ALL = X_CONST + 16 * NT

    @blk.gpsimd
    def _(g):
        sWB.inc(g.dma_start(out=rows_b[:, 0:D], in_=bproj_d[None, :]), 16)
        sWB.inc(g.dma_start(out=rows_b[:, D:2 * D], in_=b2l_d[None, :]), 16)
        sWB.inc(g.dma_start(out=rows_b[:, 2 * D:3 * D], in_=b2h_d[None, :]), 16)
        sG.inc(g.memset(cbias[:, 0:1], EPS_LN))
        sG.inc(g.memset(cbias[:, 1:2], -8.0))
        sG.inc(g.memset(ones128c[:], 1.0))
        sG.inc(g.memset(ones1r[:], 1.0))
        sG.inc(g.memset(ones1r_b[:], 1.0))
        sG.inc(g.memset(Ab[:], 0.0))
        sG.inc(g.memzero(A_bf[:]))
        sG.inc(g.dma_start(out=I_sb[:], in_=I_d.ap()), 16)
    W_SMALL = 16 * 3
    G_CONST = 7 + 16

    wctx = ExitStack()
    we = wctx.enter_context
    wqkv_sb = we(nc.sbuf_tensor("wqkv_sb", [128, 8 * 3 * D], BF16))
    wproj_sb = we(nc.sbuf_tensor("wproj_sb", [128, 8 * D], BF16))

    @blk.gpsimd
    def _(g):
        for kc in range(8):
            sW.inc(g.dma_start(out=wqkv_sb[:, kc * 3 * D:(kc + 1) * 3 * D],
                               in_=wqkv_d[kc * 128:(kc + 1) * 128, :]), 16)
        for kc in range(8):
            sW.inc(g.dma_start(out=wproj_sb[:, kc * D:(kc + 1) * D],
                               in_=wproj_d[kc * 128:(kc + 1) * 128, :]), 16)
    W_ATTN = sW.n

    @blk.vector
    def _(v):
        v.wait_ge(sX.h, X_ALL)
        v.wait_ge(sWB.h, W_SMALL)
        v.tensor_scalar(out=lnq_g[:], in0=col_e[:], scalar1=0.125, scalar2=None, op0=ALU.mult)
        v.tensor_scalar(out=lnq_b[:], in0=col_f[:], scalar1=0.125, scalar2=None, op0=ALU.mult)
        v.tensor_copy(out=lnk_g[:], in_=col_e[:])
        sV.inc(v.tensor_copy(out=lnk_b[:], in_=col_f[:]))

    @blk.tensor
    def _(t):
        t.wait_ge(sG.h, G_CONST)
        t.wait_ge(sX.h, X_ALL)
        t.matmul(pq[:], ones1r[:], rt_row[:, 0:512], start=True, stop=True)
        sM.inc(t.matmul(pq2[:], ones1r[:], rt_row[:, 512:1024], start=True, stop=True))

    @blk.vector
    def _(v):
        v.wait_ge(sM.h, sM.n)
        v.tensor_copy(out=rt_rep[:, 0:512], in_=pq[:])
        sV.inc(v.tensor_copy(out=rt_rep[:, 512:1024], in_=pq2[:]))
    V_RT = sV.n

    ln_c = 1.0 / D

    @blk.vector
    def _(v):
        v.wait_ge(sX.h, X_ALL)
        v.wait_ge(sV.h, V_RT)
        v.wait_ge(sG.h, G_CONST)
        VF(v, v.reduce_sum(out=rsum_c[:], in_=rt_rep[:], axis=AX.X))
        sV.inc(v.tensor_scalar(out=nrsum_c[:], in0=rsum_c[:], scalar1=-1.0, scalar2=None,
                               op0=ALU.mult))

    a_ev = {}
    a_tp = {}
    tp_banks = [(pq, pq2), (pSA, po[0])]
    for gi in range(4):
        j0 = gi * 4

        @blk.vector
        def _(v, j0=j0):
            for jj in range(4):
                base = (j0 + jj) * D
                xa = outacc[:, base:base + D]
                v.tensor_mul(out=qk_f32[:, 0:512], in0=outacc[:, base:base + 512],
                             in1=rt_rep[:, 0:512])
                v.reduce_sum(out=ca4[:, jj:jj + 1], in_=xa, axis=AX.X)
                v.tensor_mul(out=qk_f32[:, 512:1024], in0=outacc[:, base + 512:base + D],
                             in1=rt_rep[:, 512:1024])
                last = v.reduce_sum(out=ch4[:, jj:jj + 1], in_=qk_f32[:], axis=AX.X)
            VF(v, last)
            sV.inc(v.engine_nop())

        @blk.scalar
        def _(sc, j0=j0):
            sc.wait_ge(sV.h, sV.n)
            for jj in range(4):
                base = (j0 + jj) * D
                last = sc.activation(out=xn_sc[:], in_=outacc[:, base:base + D],
                                     func=AF.Square, accum_out=cb4[:, jj:jj + 1])
            sS.inc(last)

        @blk.vector
        def _(v):
            v.wait_ge(sS.h, sS.n)
            VF(v, v.tensor_scalar(out=cc4[:], in0=ca4[:], scalar1=ln_c, scalar2=None,
                                  op0=ALU.mult))
            VF(v, v.tensor_mul(out=cd4[:], in0=cc4[:], in1=cc4[:]))
            sV.inc(v.scalar_tensor_tensor(out=cb4[:], in0=cb4[:], scalar=ln_c,
                                          in1=cd4[:], op0=ALU.mult, op1=ALU.subtract))

        @blk.scalar
        def _(sc):
            sc.wait_ge(sV.h, sV.n)
            sS.inc(sc.activation(out=cd4[:], in_=cb4[:], func=AF.Sqrt, bias=EPS_LN))

        @blk.vector
        def _(v):
            v.wait_ge(sS.h, sS.n)
            VF(v, v.reciprocal(out=ce4[:], in_=cd4[:]))
            VF(v, v.tensor_mul(out=cf4[:], in0=cb4[:], in1=ce4[:]))
            sV.inc(v.tensor_mul(out=cf4[:], in0=cf4[:], in1=ce4[:]))

        @blk.scalar
        def _(sc):
            sc.wait_ge(sV.h, sV.n)
            sS.inc(sc.activation(out=cf4[:], in_=cf4[:], func=AF.Sqrt))

        @blk.vector
        def _(v, j0=j0):
            v.wait_ge(sS.h, sS.n)
            VF(v, v.tensor_scalar_max(cf4[:], cf4[:], 1e-12 / 32.0))
            v.tensor_copy(out=invr_t[:, j0:j0 + 4], in_=cf4[:])
            VF(v, v.reciprocal(out=cg4[:], in_=cf4[:]))
            VF(v, v.tensor_mul(out=cg4[:], in0=cg4[:], in1=ce4[:]))
            VF(v, v.scalar_tensor_tensor(out=cd4[:], in0=cc4[:], scalar=nrsum_c[:],
                                         in1=ch4[:], op0=ALU.mult, op1=ALU.add))
            sV.inc(v.tensor_mul(out=s_t[:, j0:j0 + 4], in0=cd4[:], in1=ce4[:]))

        for jj in range(4):
            j = j0 + jj

            @blk.vector
            def _(v, j=j, jj=jj):
                if j >= 2 and (j - 2) in a_tp:
                    v.wait_ge(sM.h, a_tp[j - 2])
                base = j * D
                sV.inc(v.tensor_scalar(out=qkhat[:, (j % 2) * D:(j % 2) * D + D],
                                       in0=outacc[:, base:base + D],
                                       scalar1=cc4[:, jj:jj + 1], scalar2=cg4[:, jj:jj + 1],
                                       op0=ALU.subtract, op1=ALU.mult))

            @blk.tensor
            def _(t, j=j):
                t.wait_ge(sV.h, sV.n)
                if j >= 2 and (j - 2) in a_ev:
                    t.wait_ge(sS.h, a_ev[j - 2])
                b0, b1 = tp_banks[j % 2]
                for kc in range(8):
                    bank = b0 if kc < 4 else b1
                    qtr = (kc % 4) * 128
                    mm = t.matmul(bank[:, qtr:qtr + 128],
                                  qkhat[(0):(128), (j % 2) * D + kc * 128:(j % 2) * D + kc * 128 + 128],
                                  I_sb[:], start=True, stop=True)
                sM.inc(mm)
            a_tp[j] = sM.n

            @blk.scalar
            def _(sc, j=j):
                sc.wait_ge(sM.h, a_tp[j])
                b0, b1 = tp_banks[j % 2]
                for kc in range(8):
                    bank = b0 if kc < 4 else b1
                    qtr = (kc % 4) * 128
                    last = sc.activation(out=nfm[:, kc * T + j * 128:kc * T + j * 128 + 128],
                                         in_=bank[:, qtr:qtr + 128], func=AF.Copy)
                sS.inc(last)
            a_ev[j] = sS.n
    NFM_S = sS.n
    T_NFM = sT.n
    V_PHA = sV.n

    RQ = []

    def r_prep():
        @blk.vector
        def _(v):
            v.wait_ge(sV.h, V_PHA)
            sV.inc(v.reduce_max(out=col_a[:], in_=s_t[:], axis=AX.X))

        @blk.sync
        def _(sy):
            sy.wait_ge(sV.h, sV.n)
            sX.inc(sy.dma_start(out=ds_s[:, None], in_=col_a[:]), 16)
            sy.wait_ge(sX.h, sX.n)
            sX.inc(sy.dma_start(out=tr128[:], in_=ds_s[None, :]), 16)

        @blk.vector
        def _(v):
            v.wait_ge(sX.h, sX.n)
            VF(v, v.reduce_max(out=sm1[:], in_=tr128[:], axis=AX.X))
            VF(v, v.memset(tmp1[:], 1.0))
            VF(v, v.tensor_scalar(out=tmp2[:], in0=tmp1[:], scalar1=float(2.0 ** -30),
                                  scalar2=None, op0=ALU.add))
            VF(v, v.tensor_sub(out=tmp2[:], in0=tmp2[:], in1=tmp1[:]))
            VF(v, v.tensor_scalar(out=gam_t[:], in0=tmp2[:], scalar1=1e-15, scalar2=None,
                                  op0=ALU.is_gt))
            VF(v, v.memset(tmp1[:], T32))
            VF(v, v.scalar_tensor_tensor(out=tadp[:], in0=gam_t[:], scalar=float(T64 - T32),
                                         in1=tmp1[:], op0=ALU.mult, op1=ALU.add))

        @blk.tensor
        def _(t):
            t.wait_ge(sV.h, sV.n)
            t.matmul(pb[:], ones1r[:], sm1[:], start=True, stop=True)
            sM.inc(t.matmul(pb2[:], ones1r[:], tadp[:], start=True, stop=True))

        @blk.vector
        def _(v):
            v.wait_ge(sM.h, sM.n)
            v.wait_ge(sG.h, G_CONST)
            VF(v, v.tensor_copy(out=smax_b[:], in_=pb[:]))
            VF(v, v.tensor_copy(out=tadb[:], in_=pb2[:]))
    RQ.append(r_prep)

    def r_cd_u1(it):
        def f():
            @blk.vector
            def _(v):
                if it > 0:
                    v.wait_ge(sM.h, sM.n)
                    VF(v, v.tensor_copy(out=Ab[:], in_=pb[:]))
                VF(v, v.tensor_tensor(out=mmv[:], in0=smax_b[:], in1=Ab[:], op=ALU.min))
                v.tensor_scalar(out=negmm10[:], in0=mmv[:], scalar1=-10.0, scalar2=None,
                                op0=ALU.mult)
                VF(v, v.tensor_scalar(out=mmc[:], in0=mmv[0:1, :], scalar1=C0, scalar2=None,
                                      op0=ALU.subtract))
                sV.inc(v.tensor_scalar(out=zt[:], in0=s_t[:], scalar1=Ab[:], scalar2=None,
                                       op0=ALU.min))

            @blk.scalar
            def _(ss):
                ss.wait_ge(sV.h, sV.n)
                sS.inc(ss.activation(out=cmpt[:], in_=zt[:], func=AF.Exp, bias=negmm10[:],
                                     scale=10.0, accum_out=rowsum[:]))
        return f

    def r_cd_u2():
        @blk.tensor
        def _(t):
            t.wait_ge(sS.h, sS.n)
            sM.inc(t.matmul(p1[0:1, 0:1], rowsum[:], ones128c[:], start=True, stop=True))

        @blk.scalar
        def _(ss):
            ss.wait_ge(sM.h, sM.n)
            sS.inc(ss.activation(out=lnv[:], in_=p1[0:1, 0:1], func=AF.Ln))

        @blk.vector
        def _(v):
            v.wait_ge(sS.h, sS.n)
            VF(v, v.scalar_tensor_tensor(out=Asc[:], in0=lnv[:], scalar=0.1,
                                         in1=mmc[:], op0=ALU.mult, op1=ALU.add))

    def r_cd_u3():
        @blk.tensor
        def _(t):
            t.wait_ge(sV.h, sV.n)
            sM.inc(t.matmul(pb[:], ones1r[:], Asc[:], start=True, stop=True))

    for it in range(N_CD):
        RQ.append(r_cd_u1(it))
        RQ.append(r_cd_u2)
        RQ.append(r_cd_u3)

    def r_key():
        @blk.vector
        def _(v):
            v.wait_ge(sM.h, sM.n)
            VF(v, v.tensor_copy(out=Ab[:], in_=pb[:]))
            VF(v, v.memset(hi_t[:], 1.0))
            VF(v, v.memset(mid_t[:], 0.0))
            VF(v, v.tensor_scalar(out=zt[:], in0=s_t[:], scalar1=Ab[:], scalar2=0.0,
                                  op0=ALU.subtract, op1=ALU.min))
            VF(v, v.tensor_scalar(out=key_t[:], in0=zt[:], scalar1=10.0, scalar2=None,
                                  op0=ALU.mult))
            VF(v, v.tensor_scalar(out=key_t[:], in0=key_t[:], scalar1=tadb[:], scalar2=None,
                                  op0=ALU.max))
            VF(v, v.tensor_scalar(out=lo_t[:], in0=tadp[:], scalar1=1.0, scalar2=None,
                                  op0=ALU.subtract))
            VF(v, v.tensor_add(out=mid_t[:], in0=lo_t[:], in1=hi_t[:]))
            VF(v, v.tensor_scalar(out=mid_t[:], in0=mid_t[:], scalar1=0.5, scalar2=None,
                                  op0=ALU.mult))
    RQ.append(r_key)

    def r_bis_u1():
        @blk.tensor
        def _(t):
            t.wait_ge(sV.h, sV.n)
            sM.inc(t.matmul(pb[:], ones1r[:], mid_t[:], start=True, stop=True))

        @blk.vector
        def _(v):
            v.wait_ge(sM.h, sM.n)
            VF(v, v.tensor_copy(out=midb[:], in_=pb[:]))
            VF(v, v.tensor_scalar(out=cmpt[:], in0=key_t[:], scalar1=midb[:], scalar2=None,
                                  op0=ALU.is_gt))
            sV.inc(v.reduce_sum(out=rowsum[:], in_=cmpt[:], axis=AX.X))

    def r_bis_u2():
        @blk.tensor
        def _(t):
            t.wait_ge(sV.h, sV.n)
            sM.inc(t.matmul(p1[0:1, 0:1], rowsum[:], ones128c[:], start=True, stop=True))

        @blk.vector
        def _(v):
            v.wait_ge(sM.h, sM.n)
            VF(v, v.tensor_scalar(out=gam_t[:], in0=p1[0:1, 0:1], scalar1=float(NHEAVY),
                                  scalar2=None, op0=ALU.is_ge))
            VF(v, v.tensor_sub(out=tmp1[:], in0=mid_t[:], in1=lo_t[:]))
            VF(v, v.tensor_mul(out=tmp1[:], in0=tmp1[:], in1=gam_t[:]))
            VF(v, v.tensor_add(out=lo_t[:], in0=lo_t[:], in1=tmp1[:]))
            VF(v, v.tensor_sub(out=tmp2[:], in0=hi_t[:], in1=mid_t[:]))
            VF(v, v.tensor_scalar(out=tmp3[:], in0=gam_t[:], scalar1=-1.0, scalar2=1.0,
                                  op0=ALU.mult, op1=ALU.add))
            VF(v, v.tensor_mul(out=tmp2[:], in0=tmp2[:], in1=tmp3[:]))
            VF(v, v.tensor_sub(out=hi_t[:], in0=hi_t[:], in1=tmp2[:]))
            VF(v, v.tensor_add(out=mid_t[:], in0=lo_t[:], in1=hi_t[:]))
            VF(v, v.tensor_scalar(out=mid_t[:], in0=mid_t[:], scalar1=0.5, scalar2=None,
                                  op0=ALU.mult))

    for it in range(N_BIS):
        RQ.append(r_bis_u1)
        RQ.append(r_bis_u2)

    def r_mask1():
        @blk.tensor
        def _(t):
            t.wait_ge(sV.h, sV.n)
            t.matmul(pb[:], ones1r[:], lo_t[:], start=True, stop=True)
            sM.inc(t.matmul(pb2[:], ones1r[:], hi_t[:], start=True, stop=True))

        @blk.vector
        def _(v):
            v.wait_ge(sM.h, sM.n)
            VF(v, v.tensor_copy(out=lob[:], in_=pb[:]))
            VF(v, v.tensor_copy(out=hib[:], in_=pb2[:]))
            VF(v, v.tensor_scalar(out=cmpt[:], in0=key_t[:], scalar1=hib[:], scalar2=None,
                                  op0=ALU.is_gt))
            VF(v, v.reduce_sum(out=rowsum[:], in_=cmpt[:], axis=AX.X))
            VF(v, v.tensor_scalar(out=eqt[:], in0=key_t[:], scalar1=lob[:], scalar2=None,
                                  op0=ALU.is_gt))
            VF(v, v.tensor_scalar(out=zt[:], in0=key_t[:], scalar1=hib[:], scalar2=None,
                                  op0=ALU.is_le))
            sV.inc(v.tensor_mul(out=eqt[:], in0=eqt[:], in1=zt[:]))
    RQ.append(r_mask1)

    def r_mask2():
        @blk.tensor
        def _(t):
            t.wait_ge(sV.h, sV.n)
            t.matmul(p1[0:1, 0:1], rowsum[:], ones128c[:], start=True, stop=True)
            sM.inc(t.matmul(p2, ones128c[:], eqt[:], start=True, stop=True))

        @blk.vector
        def _(v):
            v.wait_ge(sM.h, sM.n)
            VF(v, v.tensor_copy(out=nhi_t[:], in_=p1[0:1, 0:1]))
            VF(v, v.tensor_copy(out=tr1[:], in_=p2))

        @blk.sync
        def _(sy):
            sy.wait_ge(sV.h, sV.n)
            sX.inc(sy.dma_start(out=ds_c[None, :], in_=tr1[0:1, :]), 16)
            sy.wait_ge(sX.h, sX.n)
            sX.inc(sy.dma_start(out=tr16b[:], in_=ds_c[:, None]), 16)
    RQ.append(r_mask2)

    def r_mask3():
        @blk.tensor
        def _(t):
            t.wait_ge(sX.h, sX.n)
            sM.inc(t.matmul(p1[0:1, 0:NT], tr16b[:], U_sb[:], start=True, stop=True))

        @blk.vector
        def _(v):
            v.wait_ge(sM.h, sM.n)
            VF(v, v.tensor_copy(out=colpre_sb[:], in_=p1[0:1, 0:NT]))
            VF(v, v.tensor_scalar(out=nhi_t[:], in0=nhi_t[:], scalar1=-1.0,
                                  scalar2=float(NHEAVY), op0=ALU.mult, op1=ALU.add))

        @blk.tensor
        def _(t):
            t.wait_ge(sV.h, sV.n)
            t.matmul(pr[:], L_sb[:], eqt[:], start=True, stop=False)
            t.matmul(pr[:], ones1r[:], colpre_sb[:], start=False, stop=True)
            sM.inc(t.matmul(pb[:], ones1r[:], nhi_t[:], start=True, stop=True))

        @blk.vector
        def _(v):
            v.wait_ge(sM.h, sM.n)
            VF(v, v.tensor_copy(out=fcb[:], in_=pb[:]))
            VF(v, v.tensor_scalar(out=zt[:], in0=pr[:], scalar1=fcb[:], scalar2=None,
                                  op0=ALU.is_le))
            VF(v, v.tensor_mul(out=zt[:], in0=zt[:], in1=eqt[:]))
            sV.inc(v.tensor_add(out=mask_t[:], in0=cmpt[:], in1=zt[:]))
    RQ.append(r_mask3)

    def drain_routing(n):
        import os as _os2
        if _os2.environ.get("KDRAIN_MODE", "ileave") == "skip":
            for _ in range(n):
                if RQ:
                    RQ.pop(0)
            return
        for _ in range(n):
            if RQ:
                RQ.pop(0)()

    import os as _os
    _KM = _os.environ.get("KDRAIN_MODE", "ileave")
    if _KM == "all":
        drain_routing(len(RQ))
    elif _KM.isdigit():
        drain_routing(int(_KM))

    def _dr(kind, n):
        if _KM == "skip":
            for _ in range(n):
                if RQ:
                    RQ.pop(0)
            return
        if _KM == "ileave":
            if kind != "head":
                drain_routing(n)
        elif _KM == kind:
            drain_routing(n)


    fb = [pq, pq2, pSA]
    bb = [po[0], po[1], po[2], po[3]]
    PM = {}

    def f_w1(j):
        @blk.tensor
        def _(t, j=j):
            if j == 0:
                t.wait_ge(sS.h, NFM_S)
                t.wait_ge(sW.h, W_ATTN)
                t.wait_ge(sWB.h, W_SMALL)
                t.wait_ge(sV.h, V_RT)
            else:
                t.wait_ge(sV.h, PM[('gbB', j - 1)])
            for og in range(3):
                for kc in range(8):
                    mm = t.matmul(fb[og][:],
                                  nfm[:, kc * T + j * 128:kc * T + j * 128 + 128],
                                  wqkv_sb[:, kc * 3 * D + og * 512:kc * 3 * D + og * 512 + 512],
                                  start=(kc == 0), stop=(kc == 7))
            sM.inc(mm)
        PM[('w1', j)] = sM.n

    def f_ev1(j):
        @blk.vector
        def _(v, j=j):
            v.wait_ge(sM.h, PM[('w1', j)])
            iv = invr_t[:, j:j + 1]
            v.tensor_scalar(out=qk_f32[:, 0:512], in0=pq[:], scalar1=iv, scalar2=None, op0=ALU.mult)
            v.tensor_scalar(out=qk_f32[:, 512:1024], in0=pq2[:], scalar1=iv, scalar2=None, op0=ALU.mult)
            v.reduce_sum(out=stat_a[:], in_=qk_f32.rearrange("p (s k) -> p s k", k=HD), axis=AX.X)
            sV.inc(v.tensor_scalar(out=xn_sc[:, 0:512], in0=pSA[:], scalar1=iv, scalar2=None,
                                   op0=ALU.mult))
        PM[('ev1', j)] = sV.n

    def f_w2(j):
        @blk.tensor
        def _(t, j=j):
            t.wait_ge(sV.h, PM[('ev1', j)])
            for og in range(3, 6):
                for kc in range(8):
                    mm = t.matmul(fb[og - 3][:],
                                  nfm[:, kc * T + j * 128:kc * T + j * 128 + 128],
                                  wqkv_sb[:, kc * 3 * D + og * 512:kc * 3 * D + og * 512 + 512],
                                  start=(kc == 0), stop=(kc == 7))
            sM.inc(mm)
        PM[('w2', j)] = sM.n

    def f_ev2a(j):
        @blk.vector
        def _(v, j=j):
            v.wait_ge(sM.h, PM[('w2', j)])
            iv = invr_t[:, j:j + 1]
            v.tensor_scalar(out=xn_sc[:, 512:1024], in0=pq[:], scalar1=iv, scalar2=None,
                            op0=ALU.mult)
            v.reduce_sum(out=stat_e[:], in_=xn_sc.rearrange("p (s k) -> p s k", k=HD),
                         axis=AX.X)
            v.tensor_mul(out=pq[:], in0=qk_f32[:, 0:512], in1=qk_f32[:, 0:512])
            v.reduce_sum(out=stat_b[:, 0:8], in_=pq.rearrange("p (s k) -> p s k", k=HD), axis=AX.X)
            v.tensor_mul(out=pq[:], in0=qk_f32[:, 512:1024], in1=qk_f32[:, 512:1024])
            v.reduce_sum(out=stat_b[:, 8:16], in_=pq.rearrange("p (s k) -> p s k", k=HD), axis=AX.X)
            v.tensor_mul(out=pq[:], in0=xn_sc[:, 0:512], in1=xn_sc[:, 0:512])
            v.reduce_sum(out=stat_f[:, 0:8], in_=pq.rearrange("p (s k) -> p s k", k=HD), axis=AX.X)
            v.tensor_mul(out=pq[:], in0=xn_sc[:, 512:1024], in1=xn_sc[:, 512:1024])
            VF(v, v.reduce_sum(out=stat_f[:, 8:16], in_=pq.rearrange("p (s k) -> p s k", k=HD),
                               axis=AX.X))
            sV.inc(v.engine_nop())
        PM[('ev2a', j)] = sV.n

    def f_muvar(j):
        @blk.vector
        def _(v, j=j):
            VF(v, v.tensor_scalar(out=stat_a[:], in0=stat_a[:], scalar1=1.0 / HD, scalar2=None,
                                  op0=ALU.mult))
            VF(v, v.tensor_scalar(out=stat_e[:], in0=stat_e[:], scalar1=1.0 / HD, scalar2=None,
                                  op0=ALU.mult))
            VF(v, v.tensor_mul(out=stat_c[:], in0=stat_a[:], in1=stat_a[:]))
            VF(v, v.scalar_tensor_tensor(out=stat_b[:], in0=stat_b[:], scalar=1.0 / HD,
                                         in1=stat_c[:], op0=ALU.mult, op1=ALU.subtract))
            VF(v, v.tensor_mul(out=stat_c[:], in0=stat_e[:], in1=stat_e[:]))
            sV.inc(v.scalar_tensor_tensor(out=stat_f[:], in0=stat_f[:], scalar=1.0 / HD,
                                          in1=stat_c[:], op0=ALU.mult, op1=ALU.subtract))
        PM[('muvar', j)] = sV.n

    def f_sd(j):
        @blk.scalar
        def _(sc, j=j):
            sc.wait_ge(sV.h, PM[('muvar', j)])
            sc.activation(out=stat_b[:], in_=stat_b[:], func=AF.Sqrt, bias=EPS_LN)
            sS.inc(sc.activation(out=stat_f[:], in_=stat_f[:], func=AF.Sqrt, bias=EPS_LN))
        PM[('sd', j)] = sS.n

    def f_hat(j):
        @blk.vector
        def _(v, j=j):
            v.wait_ge(sS.h, PM[('sd', j)])
            if j >= 1:
                v.wait_ge(sM.h, PM[('tpB', j - 1)])
            VF(v, v.reciprocal(out=stat_d[:], in_=stat_b[:]))
            VF(v, v.reciprocal(out=stat_h[:], in_=stat_f[:]))
            for h in range(H):
                v.tensor_scalar(out=qkhat[:, h * HD:(h + 1) * HD],
                                in0=qk_f32[:, h * HD:(h + 1) * HD],
                                scalar1=stat_a[:, h:h + 1], scalar2=stat_d[:, h:h + 1],
                                op0=ALU.subtract, op1=ALU.mult)
            for h in range(H):
                last = v.tensor_scalar(out=qkhat[:, D + h * HD:D + (h + 1) * HD],
                                       in0=xn_sc[:, h * HD:(h + 1) * HD],
                                       scalar1=stat_e[:, h:h + 1], scalar2=stat_h[:, h:h + 1],
                                       op0=ALU.subtract, op1=ALU.mult)
            sV.inc(last)
        PM[('hat', j)] = sV.n

    def f_ev2v(j):
        @blk.vector
        def _(v, j=j):
            if j >= 1:
                v.wait_ge(sM.h, PM[('av', j - 1)])
            iv = invr_t[:, j:j + 1]
            v.tensor_scalar(out=v_bf[:, 0:512], in0=pq2[:], scalar1=iv, scalar2=None, op0=ALU.mult)
            sV.inc(v.tensor_scalar(out=v_bf[:, 512:1024], in0=pSA[:], scalar1=iv, scalar2=None,
                                   op0=ALU.mult))
        PM[('ev2v', j)] = sV.n

    def f_tpA(j):
        @blk.tensor
        def _(t, j=j):
            t.wait_ge(sV.h, PM[('ev2v', j)])
            for c in range(12):
                mm = t.matmul(fb[c % 3][:, (c // 3) * 128:(c // 3) * 128 + 128],
                              qkhat[:, c * 128:(c + 1) * 128], I_sb[:],
                              start=True, stop=True)
            sM.inc(mm)
        PM[('tpA', j)] = sM.n

    def f_gbA(j):
        @blk.vector
        def _(v, j=j):
            v.wait_ge(sM.h, PM[('tpA', j)])
            for c in range(12):
                g_ap = lnq_g[:] if c < 8 else lnk_g[:]
                b_ap = lnq_b[:] if c < 8 else lnk_b[:]
                last = v.tensor_scalar(out=qkT[:, c * 128:(c + 1) * 128],
                                       in0=fb[c % 3][:, (c // 3) * 128:(c // 3) * 128 + 128],
                                       scalar1=g_ap, scalar2=b_ap, op0=ALU.mult, op1=ALU.add)
            sV.inc(last)
        PM[('gbA', j)] = sV.n

    def f_tpB(j):
        @blk.tensor
        def _(t, j=j):
            t.wait_ge(sV.h, PM[('gbA', j)])
            for c in range(12, 16):
                mm = t.matmul(pq[:, (c - 12) * 128:(c - 12) * 128 + 128],
                              qkhat[:, c * 128:(c + 1) * 128], I_sb[:],
                              start=True, stop=True)
            sM.inc(mm)
        PM[('tpB', j)] = sM.n

    def f_gbB(j):
        @blk.vector
        def _(v, j=j):
            v.wait_ge(sM.h, PM[('tpB', j)])
            for c in range(12, 16):
                last = v.tensor_scalar(out=qkT[:, c * 128:(c + 1) * 128],
                                       in0=pq[:, (c - 12) * 128:(c - 12) * 128 + 128],
                                       scalar1=lnk_g[:], scalar2=lnk_b[:],
                                       op0=ALU.mult, op1=ALU.add)
            sV.inc(last)
        PM[('gbB', j)] = sV.n

    def b_smm(j):
        for h in range(H):
            @blk.tensor
            def _(t, j=j, h=h):
                if h == 0:
                    t.wait_ge(sV.h, PM[('gbB', j)])
                    if j >= 1:
                        t.wait_ge(sV.h, PM[('add', j - 1)])
                c = h // 2
                r0 = (h % 2) * HD
                sM.inc(t.matmul(bb[h // 4][:, (h % 4) * 128:(h % 4) * 128 + 128],
                                qkT[r0:r0 + HD, c * 128:(c + 1) * 128],
                                qkT[r0:r0 + HD, D + c * 128:D + (c + 1) * 128],
                                start=True, stop=True))
        PM[('smm', j)] = sM.n

    def b_exp(j):
        @blk.scalar
        def _(sc, j=j):
            sc.wait_ge(sM.h, PM[('smm', j)])
            for h in range(H):
                bank = bb[h // 4]
                qtr = (h % 4) * 128
                sc.activation(out=A_bf[0:64, h * 128:h * 128 + 64],
                              in_=bank[0:64, qtr:qtr + 64], func=AF.Exp, bias=-8.0)
                last = sc.activation(out=A_bf[64:128, h * 128 + 64:h * 128 + 128],
                                     in_=bank[64:128, qtr + 64:qtr + 128], func=AF.Exp,
                                     bias=-8.0)
            sS.inc(last)
        PM[('exp', j)] = sS.n

    def b_norm(j):
        @blk.vector
        def _(v, j=j):
            v.wait_ge(sS.h, PM[('exp', j)])
            VF(v, v.reduce_sum(out=sums16[:], in_=A_bf.rearrange("p (s k) -> p s k", k=128),
                               axis=AX.X))
            VF(v, v.reciprocal(out=invs16[:], in_=sums16[:]))
            for h in range(H):
                last = v.tensor_scalar(out=A_bf[:, h * 128:(h + 1) * 128],
                                       in0=A_bf[:, h * 128:(h + 1) * 128],
                                       scalar1=invs16[:, h:h + 1], scalar2=None, op0=ALU.mult)
            sV.inc(last)
        PM[('norm', j)] = sV.n

    def b_at(j):
        @blk.tensor
        def _(t, j=j):
            t.wait_ge(sV.h, PM[('norm', j)])
            for h in range(H):
                mm = t.matmul(bb[h % 4][:, (h // 4) * 128:(h // 4) * 128 + 128],
                              A_bf[:, h * 128:(h + 1) * 128], I_sb[:],
                              start=True, stop=True)
            sM.inc(mm)
        PM[('at', j)] = sM.n

    def b_atev(j):
        @blk.vector
        def _(v, j=j):
            v.wait_ge(sM.h, PM[('at', j)])
            for h in range(H):
                last = v.tensor_copy(out=AT_bf[:, h * 128:(h + 1) * 128],
                                     in_=bb[h % 4][:, (h // 4) * 128:(h // 4) * 128 + 128])
            sV.inc(last)
        PM[('atev', j)] = sV.n

    def b_av(j):
        @blk.tensor
        def _(t, j=j):
            t.wait_ge(sV.h, PM[('atev', j)])
            for h in range(H):
                c = h // 2
                r0 = (h % 2) * HD
                bank = po[0] if c < 4 else po[1]
                mm = t.matmul(bank[r0:r0 + HD, (c % 4) * 128:(c % 4) * 128 + 128],
                              v_bf[:, h * HD:(h + 1) * HD],
                              AT_bf[:, h * 128:(h + 1) * 128], start=True, stop=True)
            sM.inc(mm)
        PM[('av', j)] = sM.n

    def b_ao(j):
        @blk.scalar
        def _(sc, j=j):
            sc.wait_ge(sM.h, PM[('av', j)])
            for c in range(8):
                bank = po[0] if c < 4 else po[1]
                last = sc.activation(out=ao_fm[:, c * 128:(c + 1) * 128],
                                     in_=bank[:, (c % 4) * 128:(c % 4) * 128 + 128], func=AF.Copy)
            sS.inc(last)
        PM[('ao', j)] = sS.n

    def b_proj(j):
        @blk.tensor
        def _(t, j=j):
            t.wait_ge(sS.h, PM[('ao', j)])
            for of in range(2):
                dst = po[2 + of]
                for dc in range(8):
                    t.matmul(dst[:], ao_fm[:, dc * 128:(dc + 1) * 128],
                             wproj_sb[:, dc * D + of * 512:dc * D + of * 512 + 512],
                             start=(dc == 0), stop=False)
                mm = t.matmul(dst[:], ones1r_b[:], rows_b[0:1, of * 512:of * 512 + 512],
                              start=False, stop=True)
            sM.inc(mm)
        PM[('proj', j)] = sM.n

    def b_add(j):
        @blk.vector
        def _(v, j=j):
            v.wait_ge(sM.h, PM[('proj', j)])
            for of in range(2):
                sl = outacc[:, j * D + of * 512:j * D + of * 512 + 512]
                last = v.tensor_add(out=sl, in0=po[2 + of][:], in1=sl)
            sV.inc(last)
        PM[('add', j)] = sV.n

    def emit_back(j):
        b_smm(j); b_exp(j); b_norm(j); b_at(j); b_atev(j)

    for j in range(NT):
        f_w1(j); f_ev1(j); f_w2(j); f_ev2a(j)
        f_muvar(j)
        _dr('vbf', 2)
        f_sd(j); f_hat(j)
        _dr('vbf', 2)
        if j >= 1:
            b_smm(j - 1); b_exp(j - 1); b_norm(j - 1); b_at(j - 1); b_atev(j - 1)
            b_av(j - 1)
        f_ev2v(j)
        _dr('vbf', 2)
        f_tpA(j); f_gbA(j); f_tpB(j); f_gbB(j)
        if j >= 1:
            b_ao(j - 1); b_proj(j - 1); b_add(j - 1)
            _dr('proj', 3)
    j = NT - 1
    b_smm(j); b_exp(j); b_norm(j); b_at(j); b_atev(j)
    b_av(j); b_ao(j); b_proj(j); b_add(j)
    V_ATTN = sV.n

    wctx.close()
    hctx = ExitStack()
    he = hctx.enter_context
    hl_fm = he(nc.sbuf_tensor("hl_fm", [128, 4 * T], BF16))
    w1h_sb = he(nc.sbuf_tensor("w1h_sb", [128, 8 * 1024], BF16))
    w2h_sb = he(nc.sbuf_tensor("w2h_sb", [128, 8 * 1024], BF16))
    w1l_sb = he(nc.sbuf_tensor("w1l_sb", [128, 8 * LHID], BF16))
    w2l_sb = he(nc.sbuf_tensor("w2l_sb", [128, 4 * D], BF16))

    @blk.gpsimd
    def _(g):
        g.wait_ge(sV.h, V_ATTN)
        for kc in range(8):
            sW.inc(g.dma_start(out=w1l_sb[:, kc * LHID:(kc + 1) * LHID],
                               in_=w1l_d[kc * 128:(kc + 1) * 128, :]), 16)
        for hc in range(4):
            sW.inc(g.dma_start(out=w2l_sb[:, hc * D:(hc + 1) * D],
                               in_=w2l_d[hc * 128:(hc + 1) * 128, :]), 16)
    W_LGT = sW.n

    @blk.vector
    def _(v):
        v.wait_ge(sW.h, W_LGT)
        for kc in range(8):
            last = v.tensor_scalar(out=w1l_sb[:, kc * LHID:(kc + 1) * LHID],
                                   in0=w1l_sb[:, kc * LHID:(kc + 1) * LHID],
                                   scalar1=gl_sb[:, kc:kc + 1], scalar2=None, op0=ALU.mult)
        sV.inc(last)
    V_WL = sV.n

    l_mark = {}
    lt_ready = {}
    lo_add = {}

    def emit_hidden(tg):
        for hc in range(4):
            idx = tg * 4 + hc

            @blk.tensor
            def _(t, hc=hc, tg=tg, idx=idx):
                if idx == 0:
                    t.wait_ge(sV.h, V_WL)
                if idx >= 2:
                    t.wait_ge(sS.h, l_mark[idx - 2])
                dst = pq if idx % 2 == 0 else pq2
                for kc in range(8):
                    mm = t.matmul(dst[:], w1l_sb[:, kc * LHID + hc * 128:kc * LHID + hc * 128 + 128],
                                  nfm[:, kc * T + tg * 512:kc * T + (tg + 1) * 512],
                                  start=(kc == 0), stop=(kc == 7))
                sM.inc(mm)

            @blk.scalar
            def _(sc, hc=hc, tg=tg, idx=idx):
                sc.wait_ge(sM.h, sM.n)
                src = pq if idx % 2 == 0 else pq2
                sS.inc(sc.activation(out=hl_fm[:, hc * T + tg * 512:hc * T + (tg + 1) * 512],
                                     in_=src[:], func=AF.Gelu, bias=b1l_sb[:, hc:hc + 1]))
            l_mark[idx] = sS.n
            _dr('light', min(2, max(0, len(RQ) - 3)))
        lt_ready[tg] = sS.n

    def emit_out(j):
        @blk.tensor
        def _(t, j=j):
            t.wait_ge(sS.h, lt_ready[j // 4])
            if j >= 2:
                t.wait_ge(sV.h, lo_add[j - 2])
            for of in range(2):
                dst = po[2 * (j % 2) + of]
                for hc in range(4):
                    t.matmul(dst[:], hl_fm[:, hc * T + j * 128:hc * T + j * 128 + 128],
                             w2l_sb[:, hc * D + of * 512:hc * D + of * 512 + 512],
                             start=(hc == 0), stop=False)
                mm = t.matmul(dst[:], ones1r_b[:], rows_b[0:1, D + of * 512:D + of * 512 + 512],
                              start=False, stop=True)
            sM.inc(mm)

        @blk.vector
        def _(v, j=j):
            v.wait_ge(sM.h, sM.n)
            for of in range(2):
                sl = outacc[:, j * D + of * 512:j * D + of * 512 + 512]
                last = v.tensor_add(out=sl, in0=po[2 * (j % 2) + of][:], in1=sl)
            sV.inc(last)
        lo_add[j] = sV.n

    emit_hidden(0)
    emit_hidden(1)
    for j in range(4):
        emit_out(j)
    emit_hidden(2)
    for j in range(4, 8):
        emit_out(j)
    emit_hidden(3)
    for j in range(8, 16):
        emit_out(j)
    V_LIGHT = sV.n

    drain_routing(len(RQ))
    V_MASK = sV.n

    if dbg:
        @blk.sync
        def _(sy):
            sy.wait_ge(sV.h, V_MASK)
            sy.wait_ge(sT.h, T_NFM)
            sSt.inc(sy.dma_start(out=dbg_rt[:, 0:NT], in_=s_t[:]), 16)
            sSt.inc(sy.dma_start(out=dbg_rt[:, NT:2 * NT], in_=invr_t[:]), 16)
            sSt.inc(sy.dma_start(out=dbg_rt[:, 2 * NT:3 * NT], in_=mask_t[:]), 16)
            sSt.inc(sy.dma_start(out=dbg_rt[:, 3 * NT:4 * NT], in_=key_t[:]), 16)
            sSt.inc(sy.dma_start(out=dbg_rt[:, 64:65], in_=smax_b[:], allow_slow_non_contiguous=True), 16)
            sSt.inc(sy.dma_start(out=dbg_rt[:, 65:66], in_=Ab[:], allow_slow_non_contiguous=True), 16)
            sSt.inc(sy.dma_start(out=dbg_rt[:, 66:67], in_=tadb[:], allow_slow_non_contiguous=True), 16)
            sSt.inc(sy.dma_start(out=dbg_rt[:, 67:68], in_=lob[:], allow_slow_non_contiguous=True), 16)
            sSt.inc(sy.dma_start(out=dbg_rt[:, 68:69], in_=hib[:], allow_slow_non_contiguous=True), 16)
            sSt.inc(sy.dma_start(out=dbg_rt[:, 69:70], in_=fcb[:], allow_slow_non_contiguous=True), 16)
            sSt.inc(sy.dma_start(out=dbg_rt[0:1, 70:71], in_=gam_t[:], allow_slow_non_contiguous=True), 16)
            sSt.inc(sy.dma_start(out=dbg_rt[0:1, 71:72], in_=tadp[:], allow_slow_non_contiguous=True), 16)
            sSt.inc(sy.dma_start(out=dbg_traj, in_=traj[:]), 16)
            sSt.inc(sy.dma_start(out=dbg_nfm, in_=nfm[:]), 16)


    if dbg:
        @blk.sync
        def _(sy):
            sy.wait_ge(sV.h, V_LIGHT)
            for j in range(NT):
                sSt.inc(sy.dma_start(out=dbg_light[j * 128:(j + 1) * 128, :],
                                     in_=outacc[:, j * D:(j + 1) * D]), 16)

    HV_STATE = {'gm': None, 'am': None}
    for g in range(4):
        @blk.gpsimd
        def _(gg, g=g):
            if g == 0:
                gg.wait_ge(sV.h, V_ATTN)
            else:
                gg.wait_ge(sM.h, sM.n)
            for kc in range(8):
                sWH.inc(gg.dma_start(out=w1h_sb[:, kc * 1024:(kc + 1) * 1024],
                                     in_=w1h_d[kc * 128:(kc + 1) * 128,
                                               g * 1024:(g + 1) * 1024]), 16)
            for hc in range(8):
                sWH.inc(gg.dma_start(out=w2h_sb[:, hc * 1024:(hc + 1) * 1024],
                                     in_=w2h_d[g * 1024 + hc * 128:g * 1024 + (hc + 1) * 128, :]), 16)
            sWH.inc(gg.dma_start(out=b1h_sb[:],
                                 in_=b1h_d[g * 1024:(g + 1) * 1024].rearrange(
                                     "(c p) -> p c", p=128),
                                 allow_slow_non_contiguous=True), 16)
        W_G = sWH.n

        @blk.vector
        def _(v, g=g, W_G=W_G):
            v.wait_ge(sWH.h, W_G)
            if g == 0:
                v.wait_ge(sV.h, V_LIGHT)
            for kc in range(8):
                last = v.tensor_scalar(out=w1h_sb[:, kc * 1024:(kc + 1) * 1024],
                                       in0=w1h_sb[:, kc * 1024:(kc + 1) * 1024],
                                       scalar1=gh_sb[:, kc:kc + 1], scalar2=None, op0=ALU.mult)
            sV.inc(last)
        V_WG = sV.n

        hq = [(pq, 0), (pq, 256), (pq2, 0), (pq2, 256), (pSA, 0), (pSA, 256), (po[0], 0), (po[0], 256)]
        for tg in range(8):
            @blk.tensor
            def _(t, g=g, tg=tg, V_WG=V_WG):
                if tg == 0:
                    t.wait_ge(sV.h, V_WG)
                if HV_STATE['gm'] is not None:
                    t.wait_ge(sS.h, HV_STATE['gm'])
                for hc in range(8):
                    bank, off = hq[hc]
                    for kc in range(8):
                        mm = t.matmul(bank[:, off:off + 256],
                                      w1h_sb[:, kc * 1024 + hc * 128:kc * 1024 + hc * 128 + 128],
                                      nfm[:, kc * T + tg * 256:kc * T + (tg + 1) * 256],
                                      start=(kc == 0), stop=(kc == 7))
                    sM.inc(mm)
                    HV_STATE[('h', hc)] = sM.n

            for hc in range(8):
                @blk.scalar
                def _(sc, g=g, hc=hc):
                    sc.wait_ge(sM.h, HV_STATE[('h', hc | 1)])
                    bank, off = hq[hc]
                    sS.inc(sc.activation(out=A_bf[:, hc * 256:(hc + 1) * 256],
                                         in_=bank[:, off:off + 256], func=AF.Gelu,
                                         bias=b1h_sb[:, hc:hc + 1]))
                HV_STATE[('g', hc)] = sS.n
            HV_STATE['gm'] = sS.n

            for hc in range(8):
                @blk.tensor
                def _(t, g=g, hc=hc):
                    t.wait_ge(sS.h, HV_STATE[('g', hc)])
                    if hc == 0 and HV_STATE['am'] is not None:
                        t.wait_ge(sV.h, HV_STATE['am'])
                    ob = [po[1], po[2], po[3], pmisc]
                    for st in range(2):
                        for of in range(2):
                            mm = t.matmul(ob[st * 2 + of][:],
                                          A_bf[:, hc * 256 + st * 128:hc * 256 + st * 128 + 128],
                                          w2h_sb[:, hc * 1024 + of * 512:hc * 1024 + of * 512 + 512],
                                          start=(hc == 0), stop=(hc == 7 and g < 3))
                    if g == 3 and hc == 7:
                        for st in range(2):
                            for of in range(2):
                                mm = t.matmul(ob[st * 2 + of][:], ones1r_b[:],
                                              rows_b[0:1, 2 * D + of * 512:2 * D + of * 512 + 512],
                                              start=False, stop=True)
                    sM.inc(mm)

            @blk.vector
            def _(v, g=g, tg=tg):
                v.wait_ge(sM.h, sM.n)
                ob = [po[1], po[2], po[3], pmisc]
                for st in range(2):
                    jj = tg * 2 + st
                    for of in range(2):
                        sl = outacc[:, jj * D + of * 512:jj * D + of * 512 + 512]
                        last = v.scalar_tensor_tensor(out=sl, in0=ob[st * 2 + of][:],
                                                      scalar=mask_t[:, jj:jj + 1], in1=sl,
                                                      op0=ALU.mult, op1=ALU.add)
                sV.inc(last)
            HV_STATE['am'] = sV.n

    @blk.sync
    def _(sy):
        sy.wait_ge(sV.h, sV.n)
        for j in range(NT):
            sSt.inc(sy.dma_start(out=out_d[j * 128:(j + 1) * 128, :],
                                 in_=outacc[:, j * D:(j + 1) * D]), 16)
        sy.wait_ge(sSt.h, sSt.n)

    hctx.close()
    ctx.close()
    return nc


_NC_CACHE = []


def kernel(L0_attn_x, L0_attn_W_qkv, L0_attn_ln_gamma, L0_attn_ln_beta, L0_attn_W_proj,
           L0_attn_b_proj, L0_ffn_routing_token, L0_ffn_gamma_light, L0_ffn_w1_light,
           L0_ffn_b1_light, L0_ffn_w2_light, L0_ffn_b2_light, L0_ffn_gamma_heavy,
           L0_ffn_w1_heavy, L0_ffn_b1_heavy, L0_ffn_w2_heavy, L0_ffn_b2_heavy):
    x = np.ascontiguousarray(np.asarray(L0_attn_x, dtype=np.float32))
    shared = {
        "wqkv": np.ascontiguousarray(L0_attn_W_qkv, dtype=np.float32),
        "lng": np.ascontiguousarray(L0_attn_ln_gamma, dtype=np.float32),
        "lnb": np.ascontiguousarray(L0_attn_ln_beta, dtype=np.float32),
        "wproj": np.ascontiguousarray(L0_attn_W_proj, dtype=np.float32),
        "bproj": np.ascontiguousarray(L0_attn_b_proj, dtype=np.float32),
        "rt": np.ascontiguousarray(L0_ffn_routing_token, dtype=np.float32),
        "gl": np.ascontiguousarray(L0_ffn_gamma_light, dtype=np.float32),
        "w1l": np.ascontiguousarray(L0_ffn_w1_light, dtype=np.float32),
        "b1l": np.ascontiguousarray(L0_ffn_b1_light, dtype=np.float32),
        "w2l": np.ascontiguousarray(L0_ffn_w2_light, dtype=np.float32),
        "b2l": np.ascontiguousarray(L0_ffn_b2_light, dtype=np.float32),
        "gh": np.ascontiguousarray(L0_ffn_gamma_heavy, dtype=np.float32),
        "w1h": np.ascontiguousarray(L0_ffn_w1_heavy, dtype=np.float32),
        "b1h": np.ascontiguousarray(L0_ffn_b1_heavy, dtype=np.float32),
        "w2h": np.ascontiguousarray(L0_ffn_w2_heavy, dtype=np.float32),
        "b2h": np.ascontiguousarray(L0_ffn_b2_heavy, dtype=np.float32),
    }
    if not _NC_CACHE:
        _NC_CACHE.append(build_program())
    nc = _NC_CACHE[0]
    in_maps = [dict(shared, x=np.ascontiguousarray(x[b])) for b in range(8)]
    res = run_bass_kernel_spmd(nc, in_maps, core_ids=list(range(8)))
    return np.stack([np.asarray(r["out"]) for r in res.results], axis=0)


# version 14
# speedup vs baseline: 1.1049x; 1.0015x over previous
import numpy as np
from ml_dtypes import bfloat16 as _np_bf16
import concourse.bass as bass
import concourse.mybir as mybir
from concourse.bass_utils import run_bass_kernel_spmd
from contextlib import ExitStack

F32 = mybir.dt.float32
BF16 = mybir.dt.bfloat16
AF = mybir.ActivationFunctionType
ALU = mybir.AluOpType
AX = mybir.AxisListType

T = 2048
D = 1024
NT = 16
H = 16
HD = 64
LHID = 512
HHID = 4096
NHEAVY = 1024
EPS_LN = 1e-5
C0 = float(np.float32(0.1) * np.float32(np.log(8.0)))
T32 = -87.3365478515625
T64 = -745.2
N_CD = 50
N_BIS = 52


class Sem:
    def __init__(self, handle):
        self.h = handle
        self.n = 0

    def inc(self, instr, k=1):
        instr.then_inc(self.h, k)
        self.n += k
        return self.n


def build_program(dbg=False):
    nc = bass.Bass("TRN2", target_bir_lowering=False, detect_race_conditions=False)

    x_d = nc.dram_tensor("x", [T, D], F32, kind="ExternalInput").ap()
    wqkv_d = nc.dram_tensor("wqkv", [D, 3 * D], F32, kind="ExternalInput").ap()
    lng_d = nc.dram_tensor("lng", [HD], F32, kind="ExternalInput").ap()
    lnb_d = nc.dram_tensor("lnb", [HD], F32, kind="ExternalInput").ap()
    wproj_d = nc.dram_tensor("wproj", [D, D], F32, kind="ExternalInput").ap()
    bproj_d = nc.dram_tensor("bproj", [D], F32, kind="ExternalInput").ap()
    rt_d = nc.dram_tensor("rt", [D], F32, kind="ExternalInput").ap()
    gl_d = nc.dram_tensor("gl", [D], F32, kind="ExternalInput").ap()
    w1l_d = nc.dram_tensor("w1l", [D, LHID], F32, kind="ExternalInput").ap()
    b1l_d = nc.dram_tensor("b1l", [LHID], F32, kind="ExternalInput").ap()
    w2l_d = nc.dram_tensor("w2l", [LHID, D], F32, kind="ExternalInput").ap()
    b2l_d = nc.dram_tensor("b2l", [D], F32, kind="ExternalInput").ap()
    gh_d = nc.dram_tensor("gh", [D], F32, kind="ExternalInput").ap()
    w1h_d = nc.dram_tensor("w1h", [D, HHID], F32, kind="ExternalInput").ap()
    b1h_d = nc.dram_tensor("b1h", [HHID], F32, kind="ExternalInput").ap()
    w2h_d = nc.dram_tensor("w2h", [HHID, D], F32, kind="ExternalInput").ap()
    b2h_d = nc.dram_tensor("b2h", [D], F32, kind="ExternalInput").ap()
    out_d = nc.dram_tensor("out", [T, D], F32, kind="ExternalOutput").ap()
    if dbg:
        dbg_rt = nc.dram_tensor("dbg_rt", [128, NT * 4 + 8], F32, kind="ExternalOutput").ap()
        dbg_traj = nc.dram_tensor("dbg_traj", [1, 128], F32, kind="ExternalOutput").ap()
        dbg_attn = nc.dram_tensor("dbg_attn", [T, D], F32, kind="ExternalOutput").ap()
        dbg_light = nc.dram_tensor("dbg_light", [T, D], F32, kind="ExternalOutput").ap()
        dbg_nfm = nc.dram_tensor("dbg_nfm", [128, 8 * T], BF16, kind="ExternalOutput").ap()
        dbg_qkT = nc.dram_tensor("dbg_qkT", [128, 2 * D], BF16, kind="ExternalOutput").ap()
        dbg_A = nc.dram_tensor("dbg_A", [128, H * 128], BF16, kind="ExternalOutput").ap()
        dbg_ao = nc.dram_tensor("dbg_ao", [128, D], BF16, kind="ExternalOutput").ap()
        dbg_qh13 = nc.dram_tensor("dbg_qh13", [128, 2 * D], BF16, kind="ExternalOutput").ap()
        dbg_qkT13 = nc.dram_tensor("dbg_qkT13", [128, 2 * D], BF16, kind="ExternalOutput").ap()
        dbg_A13 = nc.dram_tensor("dbg_A13", [128, H * 128], BF16, kind="ExternalOutput").ap()
        dbg_AT13 = nc.dram_tensor("dbg_AT13", [128, H * 128], BF16, kind="ExternalOutput").ap()
        dbg_ao13 = nc.dram_tensor("dbg_ao13", [128, D], BF16, kind="ExternalOutput").ap()
        dbg_v13 = nc.dram_tensor("dbg_v13", [128, D], BF16, kind="ExternalOutput").ap()
    ds_s = nc.dram_tensor("ds_s", [128], F32, kind="Internal").ap()
    ds_c = nc.dram_tensor("ds_c", [NT], F32, kind="Internal").ap()

    Lmat = np.triu(np.ones((128, 128), dtype=np.float32))
    U16 = np.triu(np.ones((NT, NT), dtype=np.float32), 1)
    L_d = nc.inline_tensor(Lmat, name="Lmat")
    I_d = nc.inline_tensor(np.eye(128).astype(_np_bf16), name="I128")
    U_d = nc.inline_tensor(U16, name="U16")

    ctx = ExitStack()
    e = ctx.enter_context

    outacc = e(nc.sbuf_tensor("outacc", [128, NT * D], F32))
    nfm = e(nc.sbuf_tensor("nfm", [128, 8 * T], BF16))
    s_t = e(nc.sbuf_tensor("s_t", [128, NT], F32))
    invr_t = e(nc.sbuf_tensor("invr_t", [128, NT], F32))
    mask_t = e(nc.sbuf_tensor("mask_t", [128, NT], F32))
    key_t = e(nc.sbuf_tensor("key_t", [128, NT], F32))
    gl_sb = e(nc.sbuf_tensor("gl_sb", [128, 8], F32))
    gh_sb = e(nc.sbuf_tensor("gh_sb", [128, 8], F32))
    b1l_sb = e(nc.sbuf_tensor("b1l_sb", [128, 4], F32))
    b1h_sb = e(nc.sbuf_tensor("b1h_sb", [128, 8], F32))
    rows_b = e(nc.sbuf_tensor("rows_b", [1, 3 * D], BF16))
    lnq_g = e(nc.sbuf_tensor("lnq_g", [128, 1], F32))
    lnq_b = e(nc.sbuf_tensor("lnq_b", [128, 1], F32))
    lnk_g = e(nc.sbuf_tensor("lnk_g", [128, 1], F32))
    lnk_b = e(nc.sbuf_tensor("lnk_b", [128, 1], F32))
    rt_rep = e(nc.sbuf_tensor("rt_rep", [128, D], F32))
    rt_row = e(nc.sbuf_tensor("rt_row", [1, D], F32))
    L_sb = e(nc.sbuf_tensor("L_sb", [128, 128], F32))
    U_sb = e(nc.sbuf_tensor("U_sb", [NT, NT], F32))
    ones128c = e(nc.sbuf_tensor("ones128c", [128, 1], F32))
    ones1r = e(nc.sbuf_tensor("ones1r", [1, 128], F32))
    ones1r_b = e(nc.sbuf_tensor("ones1r_b", [1, 128], BF16))
    smax_b = e(nc.sbuf_tensor("smax_b", [128, 1], F32))
    Ab = e(nc.sbuf_tensor("Ab", [128, 1], F32))
    mmv = e(nc.sbuf_tensor("mmv", [128, 1], F32))
    negmm10 = e(nc.sbuf_tensor("negmm10", [128, 1], F32))
    mmc = e(nc.sbuf_tensor("mmc", [1, 1], F32))
    lnv = e(nc.sbuf_tensor("lnv", [1, 1], F32))
    Asc = e(nc.sbuf_tensor("Asc", [1, 1], F32))
    zt = e(nc.sbuf_tensor("zt", [128, NT], F32))
    rowsum = e(nc.sbuf_tensor("rowsum", [128, 1], F32))
    tr128 = e(nc.sbuf_tensor("tr128", [1, 128], F32))
    tr16b = e(nc.sbuf_tensor("tr16b", [NT, 1], F32))
    tr1 = e(nc.sbuf_tensor("tr1", [1, NT], F32))
    sm1 = e(nc.sbuf_tensor("sm1", [1, 1], F32))
    lo_t = e(nc.sbuf_tensor("lo_t", [1, 1], F32))
    hi_t = e(nc.sbuf_tensor("hi_t", [1, 1], F32))
    mid_t = e(nc.sbuf_tensor("mid_t", [1, 1], F32))
    midb = e(nc.sbuf_tensor("midb", [128, 1], F32))
    gam_t = e(nc.sbuf_tensor("gam_t", [1, 1], F32))
    tmp1 = e(nc.sbuf_tensor("tmp1", [1, 1], F32))
    tmp2 = e(nc.sbuf_tensor("tmp2", [1, 1], F32))
    tmp3 = e(nc.sbuf_tensor("tmp3", [1, 1], F32))
    tadp = e(nc.sbuf_tensor("tadp", [1, 1], F32))
    tadb = e(nc.sbuf_tensor("tadb", [128, 1], F32))
    lob = e(nc.sbuf_tensor("lob", [128, 1], F32))
    hib = e(nc.sbuf_tensor("hib", [128, 1], F32))
    nhi_t = e(nc.sbuf_tensor("nhi_t", [1, 1], F32))
    fcb = e(nc.sbuf_tensor("fcb", [128, 1], F32))
    cmpt = e(nc.sbuf_tensor("cmpt", [128, NT], F32))
    eqt = e(nc.sbuf_tensor("eqt", [128, NT], F32))
    colpre_sb = e(nc.sbuf_tensor("colpre_sb", [1, NT], F32))
    traj = e(nc.sbuf_tensor("traj", [1, 128], F32))
    stat_a = e(nc.sbuf_tensor("stat_a", [128, H], F32))
    stat_b = e(nc.sbuf_tensor("stat_b", [128, H], F32))
    stat_c = e(nc.sbuf_tensor("stat_c", [128, H], F32))
    stat_d = e(nc.sbuf_tensor("stat_d", [128, H], F32))
    stat_e = e(nc.sbuf_tensor("stat_e", [128, H], F32))
    stat_f = e(nc.sbuf_tensor("stat_f", [128, H], F32))
    stat_h = e(nc.sbuf_tensor("stat_h", [128, H], F32))
    col_a = e(nc.sbuf_tensor("col_a", [128, 1], F32))
    col_b = e(nc.sbuf_tensor("col_b", [128, 1], F32))
    col_c = e(nc.sbuf_tensor("col_c", [128, 1], F32))
    col_d = e(nc.sbuf_tensor("col_d", [128, 1], F32))
    col_e = e(nc.sbuf_tensor("col_e", [128, 1], F32))
    col_f = e(nc.sbuf_tensor("col_f", [128, 1], F32))
    col_g = e(nc.sbuf_tensor("col_g", [128, 1], F32))
    ca4 = e(nc.sbuf_tensor("ca4", [128, 4], F32))
    cb4 = e(nc.sbuf_tensor("cb4", [128, 4], F32))
    cc4 = e(nc.sbuf_tensor("cc4", [128, 4], F32))
    cd4 = e(nc.sbuf_tensor("cd4", [128, 4], F32))
    ce4 = e(nc.sbuf_tensor("ce4", [128, 4], F32))
    cf4 = e(nc.sbuf_tensor("cf4", [128, 4], F32))
    cg4 = e(nc.sbuf_tensor("cg4", [128, 4], F32))
    ch4 = e(nc.sbuf_tensor("ch4", [128, 4], F32))
    rsum_c = e(nc.sbuf_tensor("rsum_c", [128, 1], F32))
    nrsum_c = e(nc.sbuf_tensor("nrsum_c", [128, 1], F32))
    I_sb = e(nc.sbuf_tensor("I_sb", [128, 128], BF16))
    cbias = e(nc.sbuf_tensor("cbias", [128, 2], F32))
    nc.const_aps.register(F32, EPS_LN, cbias[:, 0:1])
    nc.const_aps.register(F32, -8.0, cbias[:, 1:2])
    xn_sc = e(nc.sbuf_tensor("xn_sc", [128, D], F32))
    qk_f32 = e(nc.sbuf_tensor("qk_f32", [128, D], F32))
    qkhat = e(nc.sbuf_tensor("qkhat", [128, 2 * D], BF16))
    ntm_sc = qkhat[:, 0:D]
    qkT = e(nc.sbuf_tensor("qkT", [128, 2 * D], BF16))
    v_bf = e(nc.sbuf_tensor("v_bf", [128, D], BF16))
    A_bf = e(nc.sbuf_tensor("A_bf", [128, H * 128], BF16))
    AT_bf = e(nc.sbuf_tensor("AT_bf", [128, H * 128], BF16))
    ao_fm = e(nc.sbuf_tensor("ao_fm", [128, D], BF16))
    sums16 = e(nc.sbuf_tensor("sums16", [128, H], F32))
    invs16 = e(nc.sbuf_tensor("invs16", [128, H], F32))
    h_sc = e(nc.sbuf_tensor("h_sc", [128, 256], BF16))

    pq = e(nc.psum_tensor("pq", [128, 512], F32))
    pq2 = e(nc.psum_tensor("pq2", [128, 512], F32))
    pSA = e(nc.psum_tensor("pSA", [128, 512], F32))
    po = [e(nc.psum_tensor(f"po{i}", [128, 512], F32)) for i in range(4)]
    pmisc = e(nc.psum_tensor("pmisc", [128, 512], F32))
    pS = pSA[:, 0:256]
    pAV = pSA[:, 256:512]
    pr = pmisc[:, 48:48 + NT]
    pb = pmisc[:, 0:1]
    pb2 = pmisc[:, 1:2]
    p1 = pmisc[0:1, 16:32]
    p2 = pmisc[0:1, 32:48]

    sW = Sem(e(nc.semaphore("sW")))
    sX = Sem(e(nc.semaphore("sX")))
    sT = Sem(e(nc.semaphore("sT")))
    sSt = Sem(e(nc.semaphore("sSt")))
    sV = Sem(e(nc.semaphore("sV")))
    sS = Sem(e(nc.semaphore("sS")))
    sM = Sem(e(nc.semaphore("sM")))
    sG = Sem(e(nc.semaphore("sG")))
    sWB = Sem(e(nc.semaphore("sWB")))
    sWH = Sem(e(nc.semaphore("sWH")))

    def VF(v, ins):
        sV.inc(ins)
        v.wait_ge(sV.h, sV.n)

    blk = e(nc.Block())

    @blk.sync
    def _(sy):
        for dma in (
            lambda: sy.dma_start(out=L_sb[:], in_=L_d.ap()),
            lambda: sy.dma_start(out=U_sb[:], in_=U_d.ap()),
            lambda: sy.dma_start(out=rt_row[:], in_=rt_d[None, :]),
            lambda: sy.dma_start(out=col_e[0:64, :], in_=lng_d[:, None]),
            lambda: sy.dma_start(out=col_e[64:128, :], in_=lng_d[:, None]),
            lambda: sy.dma_start(out=col_f[0:64, :], in_=lnb_d[:, None]),
            lambda: sy.dma_start(out=col_f[64:128, :], in_=lnb_d[:, None]),
            lambda: sy.dma_start(out=gl_sb[:], in_=gl_d.rearrange("(c p) -> p c", p=128), allow_slow_non_contiguous=True),
            lambda: sy.dma_start(out=gh_sb[:], in_=gh_d.rearrange("(c p) -> p c", p=128), allow_slow_non_contiguous=True),
            lambda: sy.dma_start(out=b1l_sb[:], in_=b1l_d.rearrange("(c p) -> p c", p=128), allow_slow_non_contiguous=True),
        ):
            sX.inc(dma(), 16)
        X_CONST_ = sX.n
        for j in range(NT):
            sX.inc(sy.dma_start(out=outacc[:, j * D:(j + 1) * D],
                                in_=x_d[j * 128:(j + 1) * 128, :]), 16)
    X_CONST = 16 * 10
    X_ALL = X_CONST + 16 * NT

    @blk.gpsimd
    def _(g):
        sWB.inc(g.dma_start(out=rows_b[:, 0:D], in_=bproj_d[None, :]), 16)
        sWB.inc(g.dma_start(out=rows_b[:, D:2 * D], in_=b2l_d[None, :]), 16)
        sWB.inc(g.dma_start(out=rows_b[:, 2 * D:3 * D], in_=b2h_d[None, :]), 16)
        sG.inc(g.memset(cbias[:, 0:1], EPS_LN))
        sG.inc(g.memset(cbias[:, 1:2], -8.0))
        sG.inc(g.memset(ones128c[:], 1.0))
        sG.inc(g.memset(ones1r[:], 1.0))
        sG.inc(g.memset(ones1r_b[:], 1.0))
        sG.inc(g.memset(Ab[:], 0.0))
        sG.inc(g.memzero(A_bf[:]))
        sG.inc(g.dma_start(out=I_sb[:], in_=I_d.ap()), 16)
    W_SMALL = 16 * 3
    G_CONST = 7 + 16

    wctx = ExitStack()
    we = wctx.enter_context
    wqkv_sb = we(nc.sbuf_tensor("wqkv_sb", [128, 8 * 3 * D], BF16))
    wproj_sb = we(nc.sbuf_tensor("wproj_sb", [128, 8 * D], BF16))

    @blk.gpsimd
    def _(g):
        for kc in range(8):
            sW.inc(g.dma_start(out=wqkv_sb[:, kc * 3 * D:(kc + 1) * 3 * D],
                               in_=wqkv_d[kc * 128:(kc + 1) * 128, :]), 16)
        for kc in range(8):
            sW.inc(g.dma_start(out=wproj_sb[:, kc * D:(kc + 1) * D],
                               in_=wproj_d[kc * 128:(kc + 1) * 128, :]), 16)
    W_ATTN = sW.n

    @blk.vector
    def _(v):
        v.wait_ge(sX.h, X_ALL)
        v.wait_ge(sWB.h, W_SMALL)
        v.tensor_scalar(out=lnq_g[:], in0=col_e[:], scalar1=0.125, scalar2=None, op0=ALU.mult)
        v.tensor_scalar(out=lnq_b[:], in0=col_f[:], scalar1=0.125, scalar2=None, op0=ALU.mult)
        v.tensor_copy(out=lnk_g[:], in_=col_e[:])
        sV.inc(v.tensor_copy(out=lnk_b[:], in_=col_f[:]))

    @blk.tensor
    def _(t):
        t.wait_ge(sG.h, G_CONST)
        t.wait_ge(sX.h, X_ALL)
        t.matmul(pq[:], ones1r[:], rt_row[:, 0:512], start=True, stop=True)
        sM.inc(t.matmul(pq2[:], ones1r[:], rt_row[:, 512:1024], start=True, stop=True))

    @blk.vector
    def _(v):
        v.wait_ge(sM.h, sM.n)
        v.tensor_copy(out=rt_rep[:, 0:512], in_=pq[:])
        sV.inc(v.tensor_copy(out=rt_rep[:, 512:1024], in_=pq2[:]))
    V_RT = sV.n

    ln_c = 1.0 / D

    @blk.vector
    def _(v):
        v.wait_ge(sX.h, X_ALL)
        v.wait_ge(sV.h, V_RT)
        v.wait_ge(sG.h, G_CONST)
        VF(v, v.reduce_sum(out=rsum_c[:], in_=rt_rep[:], axis=AX.X))
        sV.inc(v.tensor_scalar(out=nrsum_c[:], in0=rsum_c[:], scalar1=-1.0, scalar2=None,
                               op0=ALU.mult))

    a_ev = {}
    a_tp = {}
    tp_banks = [(pq, pq2), (pSA, po[0])]
    for gi in range(4):
        j0 = gi * 4

        @blk.vector
        def _(v, j0=j0):
            for jj in range(4):
                base = (j0 + jj) * D
                xa = outacc[:, base:base + D]
                v.tensor_mul(out=qk_f32[:, 0:512], in0=outacc[:, base:base + 512],
                             in1=rt_rep[:, 0:512])
                v.reduce_sum(out=ca4[:, jj:jj + 1], in_=xa, axis=AX.X)
                v.tensor_mul(out=qk_f32[:, 512:1024], in0=outacc[:, base + 512:base + D],
                             in1=rt_rep[:, 512:1024])
                last = v.reduce_sum(out=ch4[:, jj:jj + 1], in_=qk_f32[:], axis=AX.X)
            VF(v, last)
            sV.inc(v.engine_nop())

        @blk.scalar
        def _(sc, j0=j0):
            sc.wait_ge(sV.h, sV.n)
            for jj in range(4):
                base = (j0 + jj) * D
                last = sc.activation(out=xn_sc[:], in_=outacc[:, base:base + D],
                                     func=AF.Square, accum_out=cb4[:, jj:jj + 1])
            sS.inc(last)

        @blk.vector
        def _(v):
            v.wait_ge(sS.h, sS.n)
            VF(v, v.tensor_scalar(out=cc4[:], in0=ca4[:], scalar1=ln_c, scalar2=None,
                                  op0=ALU.mult))
            VF(v, v.tensor_mul(out=cd4[:], in0=cc4[:], in1=cc4[:]))
            sV.inc(v.scalar_tensor_tensor(out=cb4[:], in0=cb4[:], scalar=ln_c,
                                          in1=cd4[:], op0=ALU.mult, op1=ALU.subtract))

        @blk.scalar
        def _(sc):
            sc.wait_ge(sV.h, sV.n)
            sS.inc(sc.activation(out=cd4[:], in_=cb4[:], func=AF.Sqrt, bias=EPS_LN))

        @blk.vector
        def _(v):
            v.wait_ge(sS.h, sS.n)
            VF(v, v.reciprocal(out=ce4[:], in_=cd4[:]))
            VF(v, v.tensor_mul(out=cf4[:], in0=cb4[:], in1=ce4[:]))
            sV.inc(v.tensor_mul(out=cf4[:], in0=cf4[:], in1=ce4[:]))

        @blk.scalar
        def _(sc):
            sc.wait_ge(sV.h, sV.n)
            sS.inc(sc.activation(out=cf4[:], in_=cf4[:], func=AF.Sqrt))

        @blk.vector
        def _(v, j0=j0):
            v.wait_ge(sS.h, sS.n)
            VF(v, v.tensor_scalar_max(cf4[:], cf4[:], 1e-12 / 32.0))
            v.tensor_copy(out=invr_t[:, j0:j0 + 4], in_=cf4[:])
            VF(v, v.reciprocal(out=cg4[:], in_=cf4[:]))
            VF(v, v.tensor_mul(out=cg4[:], in0=cg4[:], in1=ce4[:]))
            VF(v, v.scalar_tensor_tensor(out=cd4[:], in0=cc4[:], scalar=nrsum_c[:],
                                         in1=ch4[:], op0=ALU.mult, op1=ALU.add))
            sV.inc(v.tensor_mul(out=s_t[:, j0:j0 + 4], in0=cd4[:], in1=ce4[:]))

        for jj in range(4):
            j = j0 + jj

            @blk.vector
            def _(v, j=j, jj=jj):
                if j >= 2 and (j - 2) in a_tp:
                    v.wait_ge(sM.h, a_tp[j - 2])
                base = j * D
                sV.inc(v.tensor_scalar(out=qkhat[:, (j % 2) * D:(j % 2) * D + D],
                                       in0=outacc[:, base:base + D],
                                       scalar1=cc4[:, jj:jj + 1], scalar2=cg4[:, jj:jj + 1],
                                       op0=ALU.subtract, op1=ALU.mult))

            @blk.tensor
            def _(t, j=j):
                t.wait_ge(sV.h, sV.n)
                if j >= 2 and (j - 2) in a_ev:
                    t.wait_ge(sS.h, a_ev[j - 2])
                b0, b1 = tp_banks[j % 2]
                for kc in range(8):
                    bank = b0 if kc < 4 else b1
                    qtr = (kc % 4) * 128
                    mm = t.matmul(bank[:, qtr:qtr + 128],
                                  qkhat[(0):(128), (j % 2) * D + kc * 128:(j % 2) * D + kc * 128 + 128],
                                  I_sb[:], start=True, stop=True)
                sM.inc(mm)
            a_tp[j] = sM.n

            @blk.scalar
            def _(sc, j=j):
                sc.wait_ge(sM.h, a_tp[j])
                b0, b1 = tp_banks[j % 2]
                for kc in range(8):
                    bank = b0 if kc < 4 else b1
                    qtr = (kc % 4) * 128
                    last = sc.activation(out=nfm[:, kc * T + j * 128:kc * T + j * 128 + 128],
                                         in_=bank[:, qtr:qtr + 128], func=AF.Copy)
                sS.inc(last)
            a_ev[j] = sS.n
    NFM_S = sS.n
    T_NFM = sT.n
    V_PHA = sV.n

    RQ = []

    def r_prep():
        @blk.vector
        def _(v):
            v.wait_ge(sV.h, V_PHA)
            sV.inc(v.reduce_max(out=col_a[:], in_=s_t[:], axis=AX.X))

        @blk.sync
        def _(sy):
            sy.wait_ge(sV.h, sV.n)
            sX.inc(sy.dma_start(out=ds_s[:, None], in_=col_a[:]), 16)
            sy.wait_ge(sX.h, sX.n)
            sX.inc(sy.dma_start(out=tr128[:], in_=ds_s[None, :]), 16)

        @blk.vector
        def _(v):
            v.wait_ge(sX.h, sX.n)
            VF(v, v.reduce_max(out=sm1[:], in_=tr128[:], axis=AX.X))
            VF(v, v.memset(tmp1[:], 1.0))
            VF(v, v.tensor_scalar(out=tmp2[:], in0=tmp1[:], scalar1=float(2.0 ** -30),
                                  scalar2=None, op0=ALU.add))
            VF(v, v.tensor_sub(out=tmp2[:], in0=tmp2[:], in1=tmp1[:]))
            VF(v, v.tensor_scalar(out=gam_t[:], in0=tmp2[:], scalar1=1e-15, scalar2=None,
                                  op0=ALU.is_gt))
            VF(v, v.memset(tmp1[:], T32))
            VF(v, v.scalar_tensor_tensor(out=tadp[:], in0=gam_t[:], scalar=float(T64 - T32),
                                         in1=tmp1[:], op0=ALU.mult, op1=ALU.add))

        @blk.tensor
        def _(t):
            t.wait_ge(sV.h, sV.n)
            t.matmul(pb[:], ones1r[:], sm1[:], start=True, stop=True)
            sM.inc(t.matmul(pb2[:], ones1r[:], tadp[:], start=True, stop=True))

        @blk.vector
        def _(v):
            v.wait_ge(sM.h, sM.n)
            v.wait_ge(sG.h, G_CONST)
            VF(v, v.tensor_copy(out=smax_b[:], in_=pb[:]))
            VF(v, v.tensor_copy(out=tadb[:], in_=pb2[:]))
    RQ.append(r_prep)

    def r_cd_u1(it):
        def f():
            @blk.vector
            def _(v):
                if it > 0:
                    v.wait_ge(sM.h, sM.n)
                    VF(v, v.tensor_copy(out=Ab[:], in_=pb[:]))
                VF(v, v.tensor_tensor(out=mmv[:], in0=smax_b[:], in1=Ab[:], op=ALU.min))
                v.tensor_scalar(out=negmm10[:], in0=mmv[:], scalar1=-10.0, scalar2=None,
                                op0=ALU.mult)
                VF(v, v.tensor_scalar(out=mmc[:], in0=mmv[0:1, :], scalar1=C0, scalar2=None,
                                      op0=ALU.subtract))
                sV.inc(v.tensor_scalar(out=zt[:], in0=s_t[:], scalar1=Ab[:], scalar2=None,
                                       op0=ALU.min))

            @blk.scalar
            def _(ss):
                ss.wait_ge(sV.h, sV.n)
                sS.inc(ss.activation(out=cmpt[:], in_=zt[:], func=AF.Exp, bias=negmm10[:],
                                     scale=10.0, accum_out=rowsum[:]))
        return f

    def r_cd_u2():
        @blk.tensor
        def _(t):
            t.wait_ge(sS.h, sS.n)
            sM.inc(t.matmul(p1[0:1, 0:1], rowsum[:], ones128c[:], start=True, stop=True))

        @blk.scalar
        def _(ss):
            ss.wait_ge(sM.h, sM.n)
            sS.inc(ss.activation(out=lnv[:], in_=p1[0:1, 0:1], func=AF.Ln))

        @blk.vector
        def _(v):
            v.wait_ge(sS.h, sS.n)
            VF(v, v.scalar_tensor_tensor(out=Asc[:], in0=lnv[:], scalar=0.1,
                                         in1=mmc[:], op0=ALU.mult, op1=ALU.add))

    def r_cd_u3():
        @blk.tensor
        def _(t):
            t.wait_ge(sV.h, sV.n)
            sM.inc(t.matmul(pb[:], ones1r[:], Asc[:], start=True, stop=True))

    for it in range(N_CD):
        RQ.append(r_cd_u1(it))
        RQ.append(r_cd_u2)
        RQ.append(r_cd_u3)

    def r_key():
        @blk.vector
        def _(v):
            v.wait_ge(sM.h, sM.n)
            VF(v, v.tensor_copy(out=Ab[:], in_=pb[:]))
            VF(v, v.memset(hi_t[:], 1.0))
            VF(v, v.memset(mid_t[:], 0.0))
            VF(v, v.tensor_scalar(out=zt[:], in0=s_t[:], scalar1=Ab[:], scalar2=0.0,
                                  op0=ALU.subtract, op1=ALU.min))
            VF(v, v.tensor_scalar(out=key_t[:], in0=zt[:], scalar1=10.0, scalar2=None,
                                  op0=ALU.mult))
            VF(v, v.tensor_scalar(out=key_t[:], in0=key_t[:], scalar1=tadb[:], scalar2=None,
                                  op0=ALU.max))
            VF(v, v.tensor_scalar(out=lo_t[:], in0=tadp[:], scalar1=1.0, scalar2=None,
                                  op0=ALU.subtract))
            VF(v, v.tensor_add(out=mid_t[:], in0=lo_t[:], in1=hi_t[:]))
            VF(v, v.tensor_scalar(out=mid_t[:], in0=mid_t[:], scalar1=0.5, scalar2=None,
                                  op0=ALU.mult))
    RQ.append(r_key)

    def r_bis_u1():
        @blk.tensor
        def _(t):
            t.wait_ge(sV.h, sV.n)
            sM.inc(t.matmul(pb[:], ones1r[:], mid_t[:], start=True, stop=True))

        @blk.vector
        def _(v):
            v.wait_ge(sM.h, sM.n)
            VF(v, v.tensor_copy(out=midb[:], in_=pb[:]))
            VF(v, v.tensor_scalar(out=cmpt[:], in0=key_t[:], scalar1=midb[:], scalar2=None,
                                  op0=ALU.is_gt))
            sV.inc(v.reduce_sum(out=rowsum[:], in_=cmpt[:], axis=AX.X))

    def r_bis_u2():
        @blk.tensor
        def _(t):
            t.wait_ge(sV.h, sV.n)
            sM.inc(t.matmul(p1[0:1, 0:1], rowsum[:], ones128c[:], start=True, stop=True))

        @blk.vector
        def _(v):
            v.wait_ge(sM.h, sM.n)
            VF(v, v.tensor_scalar(out=gam_t[:], in0=p1[0:1, 0:1], scalar1=float(NHEAVY),
                                  scalar2=None, op0=ALU.is_ge))
            VF(v, v.tensor_sub(out=tmp1[:], in0=mid_t[:], in1=lo_t[:]))
            VF(v, v.tensor_mul(out=tmp1[:], in0=tmp1[:], in1=gam_t[:]))
            VF(v, v.tensor_add(out=lo_t[:], in0=lo_t[:], in1=tmp1[:]))
            VF(v, v.tensor_sub(out=tmp2[:], in0=hi_t[:], in1=mid_t[:]))
            VF(v, v.tensor_scalar(out=tmp3[:], in0=gam_t[:], scalar1=-1.0, scalar2=1.0,
                                  op0=ALU.mult, op1=ALU.add))
            VF(v, v.tensor_mul(out=tmp2[:], in0=tmp2[:], in1=tmp3[:]))
            VF(v, v.tensor_sub(out=hi_t[:], in0=hi_t[:], in1=tmp2[:]))
            VF(v, v.tensor_add(out=mid_t[:], in0=lo_t[:], in1=hi_t[:]))
            VF(v, v.tensor_scalar(out=mid_t[:], in0=mid_t[:], scalar1=0.5, scalar2=None,
                                  op0=ALU.mult))

    for it in range(N_BIS):
        RQ.append(r_bis_u1)
        RQ.append(r_bis_u2)

    def r_mask1():
        @blk.tensor
        def _(t):
            t.wait_ge(sV.h, sV.n)
            t.matmul(pb[:], ones1r[:], lo_t[:], start=True, stop=True)
            sM.inc(t.matmul(pb2[:], ones1r[:], hi_t[:], start=True, stop=True))

        @blk.vector
        def _(v):
            v.wait_ge(sM.h, sM.n)
            VF(v, v.tensor_copy(out=lob[:], in_=pb[:]))
            VF(v, v.tensor_copy(out=hib[:], in_=pb2[:]))
            VF(v, v.tensor_scalar(out=cmpt[:], in0=key_t[:], scalar1=hib[:], scalar2=None,
                                  op0=ALU.is_gt))
            VF(v, v.reduce_sum(out=rowsum[:], in_=cmpt[:], axis=AX.X))
            VF(v, v.tensor_scalar(out=eqt[:], in0=key_t[:], scalar1=lob[:], scalar2=None,
                                  op0=ALU.is_gt))
            VF(v, v.tensor_scalar(out=zt[:], in0=key_t[:], scalar1=hib[:], scalar2=None,
                                  op0=ALU.is_le))
            sV.inc(v.tensor_mul(out=eqt[:], in0=eqt[:], in1=zt[:]))
    RQ.append(r_mask1)

    def r_mask2():
        @blk.tensor
        def _(t):
            t.wait_ge(sV.h, sV.n)
            t.matmul(p1[0:1, 0:1], rowsum[:], ones128c[:], start=True, stop=True)
            sM.inc(t.matmul(p2, ones128c[:], eqt[:], start=True, stop=True))

        @blk.vector
        def _(v):
            v.wait_ge(sM.h, sM.n)
            VF(v, v.tensor_copy(out=nhi_t[:], in_=p1[0:1, 0:1]))
            VF(v, v.tensor_copy(out=tr1[:], in_=p2))

        @blk.sync
        def _(sy):
            sy.wait_ge(sV.h, sV.n)
            sX.inc(sy.dma_start(out=ds_c[None, :], in_=tr1[0:1, :]), 16)
            sy.wait_ge(sX.h, sX.n)
            sX.inc(sy.dma_start(out=tr16b[:], in_=ds_c[:, None]), 16)
    RQ.append(r_mask2)

    def r_mask3():
        @blk.tensor
        def _(t):
            t.wait_ge(sX.h, sX.n)
            sM.inc(t.matmul(p1[0:1, 0:NT], tr16b[:], U_sb[:], start=True, stop=True))

        @blk.vector
        def _(v):
            v.wait_ge(sM.h, sM.n)
            VF(v, v.tensor_copy(out=colpre_sb[:], in_=p1[0:1, 0:NT]))
            VF(v, v.tensor_scalar(out=nhi_t[:], in0=nhi_t[:], scalar1=-1.0,
                                  scalar2=float(NHEAVY), op0=ALU.mult, op1=ALU.add))

        @blk.tensor
        def _(t):
            t.wait_ge(sV.h, sV.n)
            t.matmul(pr[:], L_sb[:], eqt[:], start=True, stop=False)
            t.matmul(pr[:], ones1r[:], colpre_sb[:], start=False, stop=True)
            sM.inc(t.matmul(pb[:], ones1r[:], nhi_t[:], start=True, stop=True))

        @blk.vector
        def _(v):
            v.wait_ge(sM.h, sM.n)
            VF(v, v.tensor_copy(out=fcb[:], in_=pb[:]))
            VF(v, v.tensor_scalar(out=zt[:], in0=pr[:], scalar1=fcb[:], scalar2=None,
                                  op0=ALU.is_le))
            VF(v, v.tensor_mul(out=zt[:], in0=zt[:], in1=eqt[:]))
            sV.inc(v.tensor_add(out=mask_t[:], in0=cmpt[:], in1=zt[:]))
    RQ.append(r_mask3)

    def drain_routing(n):
        import os as _os2
        if _os2.environ.get("KDRAIN_MODE", "ileave") == "skip":
            for _ in range(n):
                if RQ:
                    RQ.pop(0)
            return
        for _ in range(n):
            if RQ:
                RQ.pop(0)()

    import os as _os
    _KM = _os.environ.get("KDRAIN_MODE", "ileave")
    if _KM == "all":
        drain_routing(len(RQ))
    elif _KM.isdigit():
        drain_routing(int(_KM))

    def _dr(kind, n):
        if _KM == "skip":
            for _ in range(n):
                if RQ:
                    RQ.pop(0)
            return
        if _KM == "ileave":
            if kind != "head":
                drain_routing(n)
        elif _KM == kind:
            drain_routing(n)


    fb = [pq, pq2, pSA]
    bb = [po[0], po[1], po[2], po[3]]
    PM = {}

    def f_w1(j):
        @blk.tensor
        def _(t, j=j):
            if j == 0:
                t.wait_ge(sS.h, NFM_S)
                t.wait_ge(sW.h, W_ATTN)
                t.wait_ge(sWB.h, W_SMALL)
                t.wait_ge(sV.h, V_RT)
            else:
                t.wait_ge(sV.h, PM[('gbB', j - 1)])
            for og in range(3):
                for kc in range(8):
                    mm = t.matmul(fb[og][:],
                                  nfm[:, kc * T + j * 128:kc * T + j * 128 + 128],
                                  wqkv_sb[:, kc * 3 * D + og * 512:kc * 3 * D + og * 512 + 512],
                                  start=(kc == 0), stop=(kc == 7))
            sM.inc(mm)
        PM[('w1', j)] = sM.n

    def f_ev1(j):
        @blk.vector
        def _(v, j=j):
            v.wait_ge(sM.h, PM[('w1', j)])
            iv = invr_t[:, j:j + 1]
            v.tensor_scalar(out=qk_f32[:, 0:512], in0=pq[:], scalar1=iv, scalar2=None, op0=ALU.mult)
            v.tensor_scalar(out=qk_f32[:, 512:1024], in0=pq2[:], scalar1=iv, scalar2=None, op0=ALU.mult)
            v.reduce_sum(out=stat_a[:], in_=qk_f32.rearrange("p (s k) -> p s k", k=HD), axis=AX.X)
            sV.inc(v.tensor_scalar(out=xn_sc[:, 0:512], in0=pSA[:], scalar1=iv, scalar2=None,
                                   op0=ALU.mult))
        PM[('ev1', j)] = sV.n

    def f_w2(j):
        @blk.tensor
        def _(t, j=j):
            t.wait_ge(sV.h, PM[('ev1', j)])
            for og in range(3, 6):
                for kc in range(8):
                    mm = t.matmul(fb[og - 3][:],
                                  nfm[:, kc * T + j * 128:kc * T + j * 128 + 128],
                                  wqkv_sb[:, kc * 3 * D + og * 512:kc * 3 * D + og * 512 + 512],
                                  start=(kc == 0), stop=(kc == 7))
            sM.inc(mm)
        PM[('w2', j)] = sM.n

    def f_ev2a(j):
        @blk.vector
        def _(v, j=j):
            v.wait_ge(sM.h, PM[('w2', j)])
            iv = invr_t[:, j:j + 1]
            v.tensor_scalar(out=xn_sc[:, 512:1024], in0=pq[:], scalar1=iv, scalar2=None,
                            op0=ALU.mult)
            v.reduce_sum(out=stat_e[:], in_=xn_sc.rearrange("p (s k) -> p s k", k=HD),
                         axis=AX.X)
            v.tensor_mul(out=pq[:], in0=qk_f32[:, 0:512], in1=qk_f32[:, 0:512])
            v.reduce_sum(out=stat_b[:, 0:8], in_=pq.rearrange("p (s k) -> p s k", k=HD), axis=AX.X)
            v.tensor_mul(out=pq[:], in0=qk_f32[:, 512:1024], in1=qk_f32[:, 512:1024])
            v.reduce_sum(out=stat_b[:, 8:16], in_=pq.rearrange("p (s k) -> p s k", k=HD), axis=AX.X)
            v.tensor_mul(out=pq[:], in0=xn_sc[:, 0:512], in1=xn_sc[:, 0:512])
            v.reduce_sum(out=stat_f[:, 0:8], in_=pq.rearrange("p (s k) -> p s k", k=HD), axis=AX.X)
            v.tensor_mul(out=pq[:], in0=xn_sc[:, 512:1024], in1=xn_sc[:, 512:1024])
            VF(v, v.reduce_sum(out=stat_f[:, 8:16], in_=pq.rearrange("p (s k) -> p s k", k=HD),
                               axis=AX.X))
            sV.inc(v.engine_nop())
        PM[('ev2a', j)] = sV.n

    def f_muvar(j):
        @blk.vector
        def _(v, j=j):
            VF(v, v.tensor_scalar(out=stat_a[:], in0=stat_a[:], scalar1=1.0 / HD, scalar2=None,
                                  op0=ALU.mult))
            VF(v, v.tensor_scalar(out=stat_e[:], in0=stat_e[:], scalar1=1.0 / HD, scalar2=None,
                                  op0=ALU.mult))
            VF(v, v.tensor_mul(out=stat_c[:], in0=stat_a[:], in1=stat_a[:]))
            VF(v, v.scalar_tensor_tensor(out=stat_b[:], in0=stat_b[:], scalar=1.0 / HD,
                                         in1=stat_c[:], op0=ALU.mult, op1=ALU.subtract))
            VF(v, v.tensor_mul(out=stat_c[:], in0=stat_e[:], in1=stat_e[:]))
            sV.inc(v.scalar_tensor_tensor(out=stat_f[:], in0=stat_f[:], scalar=1.0 / HD,
                                          in1=stat_c[:], op0=ALU.mult, op1=ALU.subtract))
        PM[('muvar', j)] = sV.n

    def f_sd(j):
        @blk.scalar
        def _(sc, j=j):
            sc.wait_ge(sV.h, PM[('muvar', j)])
            sc.activation(out=stat_b[:], in_=stat_b[:], func=AF.Sqrt, bias=EPS_LN)
            sS.inc(sc.activation(out=stat_f[:], in_=stat_f[:], func=AF.Sqrt, bias=EPS_LN))
        PM[('sd', j)] = sS.n

    def f_hat(j):
        @blk.vector
        def _(v, j=j):
            v.wait_ge(sS.h, PM[('sd', j)])
            if j >= 1:
                v.wait_ge(sM.h, PM[('tpB', j - 1)])
            VF(v, v.reciprocal(out=stat_d[:], in_=stat_b[:]))
            VF(v, v.reciprocal(out=stat_h[:], in_=stat_f[:]))
            for h in range(H):
                v.tensor_scalar(out=qkhat[:, h * HD:(h + 1) * HD],
                                in0=qk_f32[:, h * HD:(h + 1) * HD],
                                scalar1=stat_a[:, h:h + 1], scalar2=stat_d[:, h:h + 1],
                                op0=ALU.subtract, op1=ALU.mult)
            for h in range(H):
                last = v.tensor_scalar(out=qkhat[:, D + h * HD:D + (h + 1) * HD],
                                       in0=xn_sc[:, h * HD:(h + 1) * HD],
                                       scalar1=stat_e[:, h:h + 1], scalar2=stat_h[:, h:h + 1],
                                       op0=ALU.subtract, op1=ALU.mult)
            sV.inc(last)
        PM[('hat', j)] = sV.n

    def f_ev2v(j):
        @blk.vector
        def _(v, j=j):
            if j >= 1:
                v.wait_ge(sM.h, PM[('av', j - 1)])
            iv = invr_t[:, j:j + 1]
            v.tensor_scalar(out=v_bf[:, 0:512], in0=pq2[:], scalar1=iv, scalar2=None, op0=ALU.mult)
            sV.inc(v.tensor_scalar(out=v_bf[:, 512:1024], in0=pSA[:], scalar1=iv, scalar2=None,
                                   op0=ALU.mult))
        PM[('ev2v', j)] = sV.n

    def f_tpA(j):
        @blk.tensor
        def _(t, j=j):
            t.wait_ge(sV.h, PM[('ev2v', j)])
            for c in range(12):
                mm = t.matmul(fb[c % 3][:, (c // 3) * 128:(c // 3) * 128 + 128],
                              qkhat[:, c * 128:(c + 1) * 128], I_sb[:],
                              start=True, stop=True)
            sM.inc(mm)
        PM[('tpA', j)] = sM.n

    def f_gbA(j):
        @blk.vector
        def _(v, j=j):
            v.wait_ge(sM.h, PM[('tpA', j)])
            for c in range(12):
                g_ap = lnq_g[:] if c < 8 else lnk_g[:]
                b_ap = lnq_b[:] if c < 8 else lnk_b[:]
                last = v.tensor_scalar(out=qkT[:, c * 128:(c + 1) * 128],
                                       in0=fb[c % 3][:, (c // 3) * 128:(c // 3) * 128 + 128],
                                       scalar1=g_ap, scalar2=b_ap, op0=ALU.mult, op1=ALU.add)
            sV.inc(last)
        PM[('gbA', j)] = sV.n

    def f_tpB(j):
        @blk.tensor
        def _(t, j=j):
            t.wait_ge(sV.h, PM[('gbA', j)])
            for c in range(12, 16):
                mm = t.matmul(pq[:, (c - 12) * 128:(c - 12) * 128 + 128],
                              qkhat[:, c * 128:(c + 1) * 128], I_sb[:],
                              start=True, stop=True)
            sM.inc(mm)
        PM[('tpB', j)] = sM.n

    def f_gbB(j):
        @blk.vector
        def _(v, j=j):
            v.wait_ge(sM.h, PM[('tpB', j)])
            for c in range(12, 16):
                last = v.tensor_scalar(out=qkT[:, c * 128:(c + 1) * 128],
                                       in0=pq[:, (c - 12) * 128:(c - 12) * 128 + 128],
                                       scalar1=lnk_g[:], scalar2=lnk_b[:],
                                       op0=ALU.mult, op1=ALU.add)
            sV.inc(last)
        PM[('gbB', j)] = sV.n

    def b_smm(j):
        for h in range(H):
            @blk.tensor
            def _(t, j=j, h=h):
                if h == 0:
                    t.wait_ge(sV.h, PM[('gbB', j)])
                    if j >= 1:
                        t.wait_ge(sV.h, PM[('add', j - 1)])
                c = h // 2
                r0 = (h % 2) * HD
                sM.inc(t.matmul(bb[h // 4][:, (h % 4) * 128:(h % 4) * 128 + 128],
                                qkT[r0:r0 + HD, c * 128:(c + 1) * 128],
                                qkT[r0:r0 + HD, D + c * 128:D + (c + 1) * 128],
                                start=True, stop=True))
        PM[('smm', j)] = sM.n

    def b_exp(j):
        @blk.scalar
        def _(sc, j=j):
            sc.wait_ge(sM.h, PM[('smm', j)])
            for h in range(H):
                bank = bb[h // 4]
                qtr = (h % 4) * 128
                sc.activation(out=A_bf[0:64, h * 128:h * 128 + 64],
                              in_=bank[0:64, qtr:qtr + 64], func=AF.Exp, bias=-8.0)
                last = sc.activation(out=A_bf[64:128, h * 128 + 64:h * 128 + 128],
                                     in_=bank[64:128, qtr + 64:qtr + 128], func=AF.Exp,
                                     bias=-8.0)
            sS.inc(last)
        PM[('exp', j)] = sS.n

    def b_norm(j):
        @blk.vector
        def _(v, j=j):
            v.wait_ge(sS.h, PM[('exp', j)])
            VF(v, v.reduce_sum(out=sums16[:], in_=A_bf.rearrange("p (s k) -> p s k", k=128),
                               axis=AX.X))
            VF(v, v.reciprocal(out=invs16[:], in_=sums16[:]))
            for h in range(H):
                last = v.tensor_scalar(out=A_bf[:, h * 128:(h + 1) * 128],
                                       in0=A_bf[:, h * 128:(h + 1) * 128],
                                       scalar1=invs16[:, h:h + 1], scalar2=None, op0=ALU.mult)
            sV.inc(last)
        PM[('norm', j)] = sV.n

    def b_at(j):
        @blk.tensor
        def _(t, j=j):
            t.wait_ge(sV.h, PM[('norm', j)])
            for h in range(H):
                mm = t.matmul(bb[h % 4][:, (h // 4) * 128:(h // 4) * 128 + 128],
                              A_bf[:, h * 128:(h + 1) * 128], I_sb[:],
                              start=True, stop=True)
            sM.inc(mm)
        PM[('at', j)] = sM.n

    def b_atev(j):
        @blk.vector
        def _(v, j=j):
            v.wait_ge(sM.h, PM[('at', j)])
            for h in range(H):
                last = v.tensor_copy(out=AT_bf[:, h * 128:(h + 1) * 128],
                                     in_=bb[h % 4][:, (h // 4) * 128:(h // 4) * 128 + 128])
            sV.inc(last)
        PM[('atev', j)] = sV.n

    def b_av(j):
        @blk.tensor
        def _(t, j=j):
            t.wait_ge(sV.h, PM[('atev', j)])
            for h in range(H):
                c = h // 2
                r0 = (h % 2) * HD
                bank = po[0] if c < 4 else po[1]
                mm = t.matmul(bank[r0:r0 + HD, (c % 4) * 128:(c % 4) * 128 + 128],
                              v_bf[:, h * HD:(h + 1) * HD],
                              AT_bf[:, h * 128:(h + 1) * 128], start=True, stop=True)
            sM.inc(mm)
        PM[('av', j)] = sM.n

    def b_ao(j):
        @blk.scalar
        def _(sc, j=j):
            sc.wait_ge(sM.h, PM[('av', j)])
            for c in range(8):
                bank = po[0] if c < 4 else po[1]
                last = sc.activation(out=ao_fm[:, c * 128:(c + 1) * 128],
                                     in_=bank[:, (c % 4) * 128:(c % 4) * 128 + 128], func=AF.Copy)
            sS.inc(last)
        PM[('ao', j)] = sS.n

    def b_proj(j):
        @blk.tensor
        def _(t, j=j):
            t.wait_ge(sS.h, PM[('ao', j)])
            for of in range(2):
                dst = po[2 + of]
                for dc in range(8):
                    t.matmul(dst[:], ao_fm[:, dc * 128:(dc + 1) * 128],
                             wproj_sb[:, dc * D + of * 512:dc * D + of * 512 + 512],
                             start=(dc == 0), stop=False)
                mm = t.matmul(dst[:], ones1r_b[:], rows_b[0:1, of * 512:of * 512 + 512],
                              start=False, stop=True)
            sM.inc(mm)
        PM[('proj', j)] = sM.n

    def b_add(j):
        @blk.vector
        def _(v, j=j):
            v.wait_ge(sM.h, PM[('proj', j)])
            for of in range(2):
                sl = outacc[:, j * D + of * 512:j * D + of * 512 + 512]
                last = v.tensor_add(out=sl, in0=po[2 + of][:], in1=sl)
            sV.inc(last)
        PM[('add', j)] = sV.n

    def emit_back(j):
        b_smm(j); b_exp(j); b_norm(j); b_at(j); b_atev(j)

    for j in range(NT):
        f_w1(j); f_ev1(j); f_w2(j); f_ev2a(j)
        f_muvar(j)
        _dr('vbf', 2)
        f_sd(j); f_hat(j)
        _dr('vbf', 2)
        if j >= 1:
            b_smm(j - 1); b_exp(j - 1); b_norm(j - 1); b_at(j - 1); b_atev(j - 1)
            b_av(j - 1)
        f_ev2v(j)
        _dr('vbf', 2)
        f_tpA(j); f_gbA(j); f_tpB(j); f_gbB(j)
        if j >= 1:
            b_ao(j - 1); b_proj(j - 1); b_add(j - 1)
            _dr('proj', 3)
    j = NT - 1
    b_smm(j); b_exp(j); b_norm(j); b_at(j); b_atev(j)
    b_av(j); b_ao(j); b_proj(j); b_add(j)
    V_ATTN = sV.n

    wctx.close()
    hctx = ExitStack()
    he = hctx.enter_context
    w1l_sb = he(nc.sbuf_tensor("w1l_sb", [128, 8 * LHID], BF16))
    w2l_sb = he(nc.sbuf_tensor("w2l_sb", [128, 4 * D], BF16))
    w1h_sb = he(nc.sbuf_tensor("w1h_sb", [128, 8 * 1024], BF16))
    w2h_sb = he(nc.sbuf_tensor("w2h_sb", [128, 8 * 1024], BF16))
    hl_fm = he(nc.sbuf_tensor("hl_fm", [128, 4 * T], BF16))

    W_QKV_DONE = PM[('w2', NT - 1)]

    @blk.gpsimd
    def _(g):
        g.wait_ge(sM.h, W_QKV_DONE)
        for kc in range(8):
            sW.inc(g.dma_start(out=w1l_sb[:, kc * LHID:(kc + 1) * LHID],
                               in_=w1l_d[kc * 128:(kc + 1) * 128, :]), 16)
        for hc in range(4):
            sW.inc(g.dma_start(out=w2l_sb[:, hc * D:(hc + 1) * D],
                               in_=w2l_d[hc * 128:(hc + 1) * 128, :]), 16)
    W_LGT = sW.n

    @blk.vector
    def _(v):
        v.wait_ge(sW.h, W_LGT)
        for kc in range(8):
            last = v.tensor_scalar(out=w1l_sb[:, kc * LHID:(kc + 1) * LHID],
                                   in0=w1l_sb[:, kc * LHID:(kc + 1) * LHID],
                                   scalar1=gl_sb[:, kc:kc + 1], scalar2=None, op0=ALU.mult)
        sV.inc(last)
    V_WL = sV.n

    l_mark = {}
    lt_ready = {}
    lo_add = {}

    def emit_hidden(tg):
        for hc in range(4):
            idx = tg * 4 + hc

            @blk.tensor
            def _(t, hc=hc, tg=tg, idx=idx):
                if idx == 0:
                    t.wait_ge(sV.h, max(V_WL, PM[('gbB', NT - 1)]))
                if idx >= 2:
                    t.wait_ge(sS.h, l_mark[idx - 2])
                dst = pq if idx % 2 == 0 else pq2
                for kc in range(8):
                    mm = t.matmul(dst[:], w1l_sb[:, kc * LHID + hc * 128:kc * LHID + hc * 128 + 128],
                                  nfm[:, kc * T + tg * 512:kc * T + (tg + 1) * 512],
                                  start=(kc == 0), stop=(kc == 7))
                sM.inc(mm)

            @blk.scalar
            def _(sc, hc=hc, tg=tg, idx=idx):
                sc.wait_ge(sM.h, sM.n)
                src = pq if idx % 2 == 0 else pq2
                sS.inc(sc.activation(out=hl_fm[:, hc * T + tg * 512:hc * T + (tg + 1) * 512],
                                     in_=src[:], func=AF.Gelu, bias=b1l_sb[:, hc:hc + 1]))
            l_mark[idx] = sS.n
            _dr('light', min(2, max(0, len(RQ) - 3)))
        lt_ready[tg] = sS.n

    def emit_out(j):
        @blk.tensor
        def _(t, j=j):
            t.wait_ge(sS.h, lt_ready[j // 4])
            if j >= 2:
                t.wait_ge(sV.h, lo_add[j - 2])
            for of in range(2):
                dst = po[2 * (j % 2) + of]
                for hc in range(4):
                    t.matmul(dst[:], hl_fm[:, hc * T + j * 128:hc * T + j * 128 + 128],
                             w2l_sb[:, hc * D + of * 512:hc * D + of * 512 + 512],
                             start=(hc == 0), stop=False)
                mm = t.matmul(dst[:], ones1r_b[:], rows_b[0:1, D + of * 512:D + of * 512 + 512],
                              start=False, stop=True)
            sM.inc(mm)

        @blk.vector
        def _(v, j=j):
            v.wait_ge(sM.h, sM.n)
            for of in range(2):
                sl = outacc[:, j * D + of * 512:j * D + of * 512 + 512]
                last = v.tensor_add(out=sl, in0=po[2 * (j % 2) + of][:], in1=sl)
            sV.inc(last)
        lo_add[j] = sV.n

    emit_hidden(0)
    emit_hidden(1)
    for j in range(4):
        emit_out(j)
    emit_hidden(2)
    for j in range(4, 8):
        emit_out(j)
    emit_hidden(3)
    for j in range(8, 16):
        emit_out(j)
    V_LIGHT = sV.n

    drain_routing(len(RQ))
    V_MASK = sV.n

    if dbg:
        @blk.sync
        def _(sy):
            sy.wait_ge(sV.h, V_MASK)
            sy.wait_ge(sT.h, T_NFM)
            sSt.inc(sy.dma_start(out=dbg_rt[:, 0:NT], in_=s_t[:]), 16)
            sSt.inc(sy.dma_start(out=dbg_rt[:, NT:2 * NT], in_=invr_t[:]), 16)
            sSt.inc(sy.dma_start(out=dbg_rt[:, 2 * NT:3 * NT], in_=mask_t[:]), 16)
            sSt.inc(sy.dma_start(out=dbg_rt[:, 3 * NT:4 * NT], in_=key_t[:]), 16)
            sSt.inc(sy.dma_start(out=dbg_rt[:, 64:65], in_=smax_b[:], allow_slow_non_contiguous=True), 16)
            sSt.inc(sy.dma_start(out=dbg_rt[:, 65:66], in_=Ab[:], allow_slow_non_contiguous=True), 16)
            sSt.inc(sy.dma_start(out=dbg_rt[:, 66:67], in_=tadb[:], allow_slow_non_contiguous=True), 16)
            sSt.inc(sy.dma_start(out=dbg_rt[:, 67:68], in_=lob[:], allow_slow_non_contiguous=True), 16)
            sSt.inc(sy.dma_start(out=dbg_rt[:, 68:69], in_=hib[:], allow_slow_non_contiguous=True), 16)
            sSt.inc(sy.dma_start(out=dbg_rt[:, 69:70], in_=fcb[:], allow_slow_non_contiguous=True), 16)
            sSt.inc(sy.dma_start(out=dbg_rt[0:1, 70:71], in_=gam_t[:], allow_slow_non_contiguous=True), 16)
            sSt.inc(sy.dma_start(out=dbg_rt[0:1, 71:72], in_=tadp[:], allow_slow_non_contiguous=True), 16)
            sSt.inc(sy.dma_start(out=dbg_traj, in_=traj[:]), 16)
            sSt.inc(sy.dma_start(out=dbg_nfm, in_=nfm[:]), 16)


    if dbg:
        @blk.sync
        def _(sy):
            sy.wait_ge(sV.h, V_LIGHT)
            for j in range(NT):
                sSt.inc(sy.dma_start(out=dbg_light[j * 128:(j + 1) * 128, :],
                                     in_=outacc[:, j * D:(j + 1) * D]), 16)

    HV_STATE = {'gm': None, 'am': None}
    for g in range(4):
        @blk.gpsimd
        def _(gg, g=g):
            if g == 0:
                gg.wait_ge(sM.h, W_QKV_DONE)
            else:
                gg.wait_ge(sM.h, sM.n)
            for kc in range(8):
                sWH.inc(gg.dma_start(out=w1h_sb[:, kc * 1024:(kc + 1) * 1024],
                                     in_=w1h_d[kc * 128:(kc + 1) * 128,
                                               g * 1024:(g + 1) * 1024]), 16)
            for hc in range(8):
                sWH.inc(gg.dma_start(out=w2h_sb[:, hc * 1024:(hc + 1) * 1024],
                                     in_=w2h_d[g * 1024 + hc * 128:g * 1024 + (hc + 1) * 128, :]), 16)
            sWH.inc(gg.dma_start(out=b1h_sb[:],
                                 in_=b1h_d[g * 1024:(g + 1) * 1024].rearrange(
                                     "(c p) -> p c", p=128),
                                 allow_slow_non_contiguous=True), 16)
        W_G = sWH.n

        @blk.vector
        def _(v, g=g, W_G=W_G):
            v.wait_ge(sWH.h, W_G)
            if g == 0:
                v.wait_ge(sV.h, V_LIGHT)
            for kc in range(8):
                last = v.tensor_scalar(out=w1h_sb[:, kc * 1024:(kc + 1) * 1024],
                                       in0=w1h_sb[:, kc * 1024:(kc + 1) * 1024],
                                       scalar1=gh_sb[:, kc:kc + 1], scalar2=None, op0=ALU.mult)
            sV.inc(last)
        V_WG = sV.n

        hq = [(pq, 0), (pq, 256), (pq2, 0), (pq2, 256), (pSA, 0), (pSA, 256), (po[0], 0), (po[0], 256)]
        for tg in range(8):
            @blk.tensor
            def _(t, g=g, tg=tg, V_WG=V_WG):
                if tg == 0:
                    t.wait_ge(sV.h, V_WG)
                if HV_STATE['gm'] is not None:
                    t.wait_ge(sS.h, HV_STATE['gm'])
                for hc in range(8):
                    bank, off = hq[hc]
                    for kc in range(8):
                        mm = t.matmul(bank[:, off:off + 256],
                                      w1h_sb[:, kc * 1024 + hc * 128:kc * 1024 + hc * 128 + 128],
                                      nfm[:, kc * T + tg * 256:kc * T + (tg + 1) * 256],
                                      start=(kc == 0), stop=(kc == 7))
                    sM.inc(mm)
                    HV_STATE[('h', hc)] = sM.n

            for hc in range(8):
                @blk.scalar
                def _(sc, g=g, hc=hc):
                    sc.wait_ge(sM.h, HV_STATE[('h', hc | 1)])
                    bank, off = hq[hc]
                    sS.inc(sc.activation(out=A_bf[:, hc * 256:(hc + 1) * 256],
                                         in_=bank[:, off:off + 256], func=AF.Gelu,
                                         bias=b1h_sb[:, hc:hc + 1]))
                HV_STATE[('g', hc)] = sS.n
            HV_STATE['gm'] = sS.n

            for hc in range(8):
                @blk.tensor
                def _(t, g=g, hc=hc):
                    t.wait_ge(sS.h, HV_STATE[('g', hc)])
                    if hc == 0 and HV_STATE['am'] is not None:
                        t.wait_ge(sV.h, HV_STATE['am'])
                    ob = [po[1], po[2], po[3], pmisc]
                    for st in range(2):
                        for of in range(2):
                            mm = t.matmul(ob[st * 2 + of][:],
                                          A_bf[:, hc * 256 + st * 128:hc * 256 + st * 128 + 128],
                                          w2h_sb[:, hc * 1024 + of * 512:hc * 1024 + of * 512 + 512],
                                          start=(hc == 0), stop=(hc == 7 and g < 3))
                    if g == 3 and hc == 7:
                        for st in range(2):
                            for of in range(2):
                                mm = t.matmul(ob[st * 2 + of][:], ones1r_b[:],
                                              rows_b[0:1, 2 * D + of * 512:2 * D + of * 512 + 512],
                                              start=False, stop=True)
                    sM.inc(mm)

            @blk.vector
            def _(v, g=g, tg=tg):
                v.wait_ge(sM.h, sM.n)
                ob = [po[1], po[2], po[3], pmisc]
                for st in range(2):
                    jj = tg * 2 + st
                    for of in range(2):
                        sl = outacc[:, jj * D + of * 512:jj * D + of * 512 + 512]
                        last = v.scalar_tensor_tensor(out=sl, in0=ob[st * 2 + of][:],
                                                      scalar=mask_t[:, jj:jj + 1], in1=sl,
                                                      op0=ALU.mult, op1=ALU.add)
                sV.inc(last)
            HV_STATE['am'] = sV.n

    @blk.sync
    def _(sy):
        sy.wait_ge(sV.h, sV.n)
        for j in range(NT):
            sSt.inc(sy.dma_start(out=out_d[j * 128:(j + 1) * 128, :],
                                 in_=outacc[:, j * D:(j + 1) * D]), 16)
        sy.wait_ge(sSt.h, sSt.n)

    hctx.close()
    ctx.close()
    return nc


_NC_CACHE = []


def kernel(L0_attn_x, L0_attn_W_qkv, L0_attn_ln_gamma, L0_attn_ln_beta, L0_attn_W_proj,
           L0_attn_b_proj, L0_ffn_routing_token, L0_ffn_gamma_light, L0_ffn_w1_light,
           L0_ffn_b1_light, L0_ffn_w2_light, L0_ffn_b2_light, L0_ffn_gamma_heavy,
           L0_ffn_w1_heavy, L0_ffn_b1_heavy, L0_ffn_w2_heavy, L0_ffn_b2_heavy):
    x = np.ascontiguousarray(np.asarray(L0_attn_x, dtype=np.float32))
    shared = {
        "wqkv": np.ascontiguousarray(L0_attn_W_qkv, dtype=np.float32),
        "lng": np.ascontiguousarray(L0_attn_ln_gamma, dtype=np.float32),
        "lnb": np.ascontiguousarray(L0_attn_ln_beta, dtype=np.float32),
        "wproj": np.ascontiguousarray(L0_attn_W_proj, dtype=np.float32),
        "bproj": np.ascontiguousarray(L0_attn_b_proj, dtype=np.float32),
        "rt": np.ascontiguousarray(L0_ffn_routing_token, dtype=np.float32),
        "gl": np.ascontiguousarray(L0_ffn_gamma_light, dtype=np.float32),
        "w1l": np.ascontiguousarray(L0_ffn_w1_light, dtype=np.float32),
        "b1l": np.ascontiguousarray(L0_ffn_b1_light, dtype=np.float32),
        "w2l": np.ascontiguousarray(L0_ffn_w2_light, dtype=np.float32),
        "b2l": np.ascontiguousarray(L0_ffn_b2_light, dtype=np.float32),
        "gh": np.ascontiguousarray(L0_ffn_gamma_heavy, dtype=np.float32),
        "w1h": np.ascontiguousarray(L0_ffn_w1_heavy, dtype=np.float32),
        "b1h": np.ascontiguousarray(L0_ffn_b1_heavy, dtype=np.float32),
        "w2h": np.ascontiguousarray(L0_ffn_w2_heavy, dtype=np.float32),
        "b2h": np.ascontiguousarray(L0_ffn_b2_heavy, dtype=np.float32),
    }
    if not _NC_CACHE:
        _NC_CACHE.append(build_program())
    nc = _NC_CACHE[0]
    in_maps = [dict(shared, x=np.ascontiguousarray(x[b])) for b in range(8)]
    res = run_bass_kernel_spmd(nc, in_maps, core_ids=list(range(8)))
    return np.stack([np.asarray(r["out"]) for r in res.results], axis=0)


# version 16
# speedup vs baseline: 1.1440x; 1.0353x over previous
import numpy as np
from ml_dtypes import bfloat16 as _np_bf16
import concourse.bass as bass
import concourse.mybir as mybir
from concourse.bass_utils import run_bass_kernel_spmd
from contextlib import ExitStack

F32 = mybir.dt.float32
BF16 = mybir.dt.bfloat16
AF = mybir.ActivationFunctionType
ALU = mybir.AluOpType
AX = mybir.AxisListType

T = 2048
D = 1024
NT = 16
H = 16
HD = 64
LHID = 512
HHID = 4096
NHEAVY = 1024
EPS_LN = 1e-5
C0 = float(np.float32(0.1) * np.float32(np.log(8.0)))
T32 = -87.3365478515625
T64 = -745.2
N_CD = 50
N_BIS = 52


class Sem:
    def __init__(self, handle):
        self.h = handle
        self.n = 0

    def inc(self, instr, k=1):
        instr.then_inc(self.h, k)
        self.n += k
        return self.n


def build_program(dbg=False):
    nc = bass.Bass("TRN2", target_bir_lowering=False, detect_race_conditions=False)

    x_d = nc.dram_tensor("x", [T, D], F32, kind="ExternalInput").ap()
    wqkv_d = nc.dram_tensor("wqkv", [D, 3 * D], F32, kind="ExternalInput").ap()
    lng_d = nc.dram_tensor("lng", [HD], F32, kind="ExternalInput").ap()
    lnb_d = nc.dram_tensor("lnb", [HD], F32, kind="ExternalInput").ap()
    wproj_d = nc.dram_tensor("wproj", [D, D], F32, kind="ExternalInput").ap()
    bproj_d = nc.dram_tensor("bproj", [D], F32, kind="ExternalInput").ap()
    rt_d = nc.dram_tensor("rt", [D], F32, kind="ExternalInput").ap()
    gl_d = nc.dram_tensor("gl", [D], F32, kind="ExternalInput").ap()
    w1l_d = nc.dram_tensor("w1l", [D, LHID], F32, kind="ExternalInput").ap()
    b1l_d = nc.dram_tensor("b1l", [LHID], F32, kind="ExternalInput").ap()
    w2l_d = nc.dram_tensor("w2l", [LHID, D], F32, kind="ExternalInput").ap()
    b2l_d = nc.dram_tensor("b2l", [D], F32, kind="ExternalInput").ap()
    gh_d = nc.dram_tensor("gh", [D], F32, kind="ExternalInput").ap()
    w1h_d = nc.dram_tensor("w1h", [D, HHID], F32, kind="ExternalInput").ap()
    b1h_d = nc.dram_tensor("b1h", [HHID], F32, kind="ExternalInput").ap()
    w2h_d = nc.dram_tensor("w2h", [HHID, D], F32, kind="ExternalInput").ap()
    b2h_d = nc.dram_tensor("b2h", [D], F32, kind="ExternalInput").ap()
    out_d = nc.dram_tensor("out", [T, D], F32, kind="ExternalOutput").ap()
    if dbg:
        dbg_rt = nc.dram_tensor("dbg_rt", [128, NT * 4 + 8], F32, kind="ExternalOutput").ap()
        dbg_traj = nc.dram_tensor("dbg_traj", [1, 128], F32, kind="ExternalOutput").ap()
        dbg_attn = nc.dram_tensor("dbg_attn", [T, D], F32, kind="ExternalOutput").ap()
        dbg_light = nc.dram_tensor("dbg_light", [T, D], F32, kind="ExternalOutput").ap()
        dbg_nfm = nc.dram_tensor("dbg_nfm", [128, 8 * T], BF16, kind="ExternalOutput").ap()
        dbg_qkT = nc.dram_tensor("dbg_qkT", [128, 2 * D], BF16, kind="ExternalOutput").ap()
        dbg_A = nc.dram_tensor("dbg_A", [128, H * 128], BF16, kind="ExternalOutput").ap()
        dbg_ao = nc.dram_tensor("dbg_ao", [128, D], BF16, kind="ExternalOutput").ap()
        dbg_qh13 = nc.dram_tensor("dbg_qh13", [128, 2 * D], BF16, kind="ExternalOutput").ap()
        dbg_qkT13 = nc.dram_tensor("dbg_qkT13", [128, 2 * D], BF16, kind="ExternalOutput").ap()
        dbg_A13 = nc.dram_tensor("dbg_A13", [128, H * 128], BF16, kind="ExternalOutput").ap()
        dbg_AT13 = nc.dram_tensor("dbg_AT13", [128, H * 128], BF16, kind="ExternalOutput").ap()
        dbg_ao13 = nc.dram_tensor("dbg_ao13", [128, D], BF16, kind="ExternalOutput").ap()
        dbg_v13 = nc.dram_tensor("dbg_v13", [128, D], BF16, kind="ExternalOutput").ap()
    ds_s = nc.dram_tensor("ds_s", [128], F32, kind="Internal").ap()
    ds_c = nc.dram_tensor("ds_c", [NT], F32, kind="Internal").ap()

    Lmat = np.triu(np.ones((128, 128), dtype=np.float32))
    U16 = np.triu(np.ones((NT, NT), dtype=np.float32), 1)
    L_d = nc.inline_tensor(Lmat, name="Lmat")
    I_d = nc.inline_tensor(np.eye(128).astype(_np_bf16), name="I128")
    U_d = nc.inline_tensor(U16, name="U16")

    ctx = ExitStack()
    e = ctx.enter_context

    outacc = e(nc.sbuf_tensor("outacc", [128, NT * D], F32))
    nfm = e(nc.sbuf_tensor("nfm", [128, 8 * T], BF16))
    s_t = e(nc.sbuf_tensor("s_t", [128, NT], F32))
    invr_t = e(nc.sbuf_tensor("invr_t", [128, NT], F32))
    mask_t = e(nc.sbuf_tensor("mask_t", [128, NT], F32))
    key_t = e(nc.sbuf_tensor("key_t", [128, NT], F32))
    gl_sb = e(nc.sbuf_tensor("gl_sb", [128, 8], F32))
    gh_sb = e(nc.sbuf_tensor("gh_sb", [128, 8], F32))
    b1l_sb = e(nc.sbuf_tensor("b1l_sb", [128, 4], F32))
    b1h_sb = e(nc.sbuf_tensor("b1h_sb", [128, 8], F32))
    rows_b = e(nc.sbuf_tensor("rows_b", [1, 3 * D], BF16))
    lnq_g = e(nc.sbuf_tensor("lnq_g", [128, 1], F32))
    lnq_b = e(nc.sbuf_tensor("lnq_b", [128, 1], F32))
    lnk_g = e(nc.sbuf_tensor("lnk_g", [128, 1], F32))
    lnk_b = e(nc.sbuf_tensor("lnk_b", [128, 1], F32))
    rt_rep = e(nc.sbuf_tensor("rt_rep", [128, D], F32))
    rt_row = e(nc.sbuf_tensor("rt_row", [1, D], F32))
    L_sb = e(nc.sbuf_tensor("L_sb", [128, 128], F32))
    U_sb = e(nc.sbuf_tensor("U_sb", [NT, NT], F32))
    ones128c = e(nc.sbuf_tensor("ones128c", [128, 1], F32))
    ones1r = e(nc.sbuf_tensor("ones1r", [1, 128], F32))
    ones1r_b = e(nc.sbuf_tensor("ones1r_b", [1, 128], BF16))
    smax_b = e(nc.sbuf_tensor("smax_b", [128, 1], F32))
    Ab = e(nc.sbuf_tensor("Ab", [128, 1], F32))
    mmv = e(nc.sbuf_tensor("mmv", [128, 1], F32))
    negmm10 = e(nc.sbuf_tensor("negmm10", [128, 1], F32))
    mmc = e(nc.sbuf_tensor("mmc", [1, 1], F32))
    lnv = e(nc.sbuf_tensor("lnv", [1, 1], F32))
    Asc = e(nc.sbuf_tensor("Asc", [1, 1], F32))
    zt = e(nc.sbuf_tensor("zt", [128, NT], F32))
    rowsum = e(nc.sbuf_tensor("rowsum", [128, 1], F32))
    tr128 = e(nc.sbuf_tensor("tr128", [1, 128], F32))
    tr16b = e(nc.sbuf_tensor("tr16b", [NT, 1], F32))
    tr1 = e(nc.sbuf_tensor("tr1", [1, NT], F32))
    sm1 = e(nc.sbuf_tensor("sm1", [1, 1], F32))
    lo_t = e(nc.sbuf_tensor("lo_t", [1, 1], F32))
    hi_t = e(nc.sbuf_tensor("hi_t", [1, 1], F32))
    mid_t = e(nc.sbuf_tensor("mid_t", [1, 1], F32))
    midb = e(nc.sbuf_tensor("midb", [128, 1], F32))
    gam_t = e(nc.sbuf_tensor("gam_t", [1, 1], F32))
    tmp1 = e(nc.sbuf_tensor("tmp1", [1, 1], F32))
    tmp2 = e(nc.sbuf_tensor("tmp2", [1, 1], F32))
    tmp3 = e(nc.sbuf_tensor("tmp3", [1, 1], F32))
    tadp = e(nc.sbuf_tensor("tadp", [1, 1], F32))
    tadb = e(nc.sbuf_tensor("tadb", [128, 1], F32))
    lob = e(nc.sbuf_tensor("lob", [128, 1], F32))
    hib = e(nc.sbuf_tensor("hib", [128, 1], F32))
    nhi_t = e(nc.sbuf_tensor("nhi_t", [1, 1], F32))
    fcb = e(nc.sbuf_tensor("fcb", [128, 1], F32))
    cmpt = e(nc.sbuf_tensor("cmpt", [128, NT], F32))
    eqt = e(nc.sbuf_tensor("eqt", [128, NT], F32))
    colpre_sb = e(nc.sbuf_tensor("colpre_sb", [1, NT], F32))
    traj = e(nc.sbuf_tensor("traj", [1, 128], F32))
    stat_a = e(nc.sbuf_tensor("stat_a", [128, H], F32))
    stat_b = e(nc.sbuf_tensor("stat_b", [128, H], F32))
    stat_c = e(nc.sbuf_tensor("stat_c", [128, H], F32))
    stat_d = e(nc.sbuf_tensor("stat_d", [128, H], F32))
    stat_e = e(nc.sbuf_tensor("stat_e", [128, H], F32))
    stat_f = e(nc.sbuf_tensor("stat_f", [128, H], F32))
    stat_h = e(nc.sbuf_tensor("stat_h", [128, H], F32))
    col_a = e(nc.sbuf_tensor("col_a", [128, 1], F32))
    col_b = e(nc.sbuf_tensor("col_b", [128, 1], F32))
    col_c = e(nc.sbuf_tensor("col_c", [128, 1], F32))
    col_d = e(nc.sbuf_tensor("col_d", [128, 1], F32))
    col_e = e(nc.sbuf_tensor("col_e", [128, 1], F32))
    col_f = e(nc.sbuf_tensor("col_f", [128, 1], F32))
    col_g = e(nc.sbuf_tensor("col_g", [128, 1], F32))
    ca4 = e(nc.sbuf_tensor("ca4", [128, 4], F32))
    cb4 = e(nc.sbuf_tensor("cb4", [128, 4], F32))
    cc4 = e(nc.sbuf_tensor("cc4", [128, 4], F32))
    cd4 = e(nc.sbuf_tensor("cd4", [128, 4], F32))
    ce4 = e(nc.sbuf_tensor("ce4", [128, 4], F32))
    cf4 = e(nc.sbuf_tensor("cf4", [128, 4], F32))
    cg4 = e(nc.sbuf_tensor("cg4", [128, 4], F32))
    ch4 = e(nc.sbuf_tensor("ch4", [128, 4], F32))
    rsum_c = e(nc.sbuf_tensor("rsum_c", [128, 1], F32))
    nrsum_c = e(nc.sbuf_tensor("nrsum_c", [128, 1], F32))
    I_sb = e(nc.sbuf_tensor("I_sb", [128, 128], BF16))
    cbias = e(nc.sbuf_tensor("cbias", [128, 2], F32))
    nc.const_aps.register(F32, EPS_LN, cbias[:, 0:1])
    nc.const_aps.register(F32, -8.0, cbias[:, 1:2])
    xn_sc = e(nc.sbuf_tensor("xn_sc", [128, D], F32))
    qk_f32 = e(nc.sbuf_tensor("qk_f32", [128, D], F32))
    qkhat = e(nc.sbuf_tensor("qkhat", [128, 2 * D], BF16))
    ntm_sc = qkhat[:, 0:D]
    qkT = e(nc.sbuf_tensor("qkT", [128, 2 * D], BF16))
    v_bf = e(nc.sbuf_tensor("v_bf", [128, D], BF16))
    A_bf = e(nc.sbuf_tensor("A_bf", [128, H * 128], BF16))
    AT_bf = e(nc.sbuf_tensor("AT_bf", [128, H * 128], BF16))
    ao_fm = e(nc.sbuf_tensor("ao_fm", [128, D], BF16))
    sums16 = e(nc.sbuf_tensor("sums16", [128, H], F32))
    invs16 = e(nc.sbuf_tensor("invs16", [128, H], F32))
    h_sc = e(nc.sbuf_tensor("h_sc", [128, 256], BF16))

    pq = e(nc.psum_tensor("pq", [128, 512], F32))
    pq2 = e(nc.psum_tensor("pq2", [128, 512], F32))
    pSA = e(nc.psum_tensor("pSA", [128, 512], F32))
    po = [e(nc.psum_tensor(f"po{i}", [128, 512], F32)) for i in range(4)]
    pmisc = e(nc.psum_tensor("pmisc", [128, 512], F32))
    pS = pSA[:, 0:256]
    pAV = pSA[:, 256:512]
    pr = pmisc[:, 48:48 + NT]
    pb = pmisc[:, 0:1]
    pb2 = pmisc[:, 1:2]
    p1 = pmisc[0:1, 16:32]
    p2 = pmisc[0:1, 32:48]

    sW = Sem(e(nc.semaphore("sW")))
    sX = Sem(e(nc.semaphore("sX")))
    sT = Sem(e(nc.semaphore("sT")))
    sSt = Sem(e(nc.semaphore("sSt")))
    sV = Sem(e(nc.semaphore("sV")))
    sS = Sem(e(nc.semaphore("sS")))
    sM = Sem(e(nc.semaphore("sM")))
    sG = Sem(e(nc.semaphore("sG")))
    sI = Sem(e(nc.semaphore("sI")))
    sWB = Sem(e(nc.semaphore("sWB")))
    sWH = Sem(e(nc.semaphore("sWH")))

    def VF(v, ins):
        sV.inc(ins)
        v.wait_ge(sV.h, sV.n)

    blk = e(nc.Block())

    @blk.sync
    def _(sy):
        for dma in (
            lambda: sy.dma_start(out=L_sb[:], in_=L_d.ap()),
            lambda: sy.dma_start(out=U_sb[:], in_=U_d.ap()),
            lambda: sy.dma_start(out=rt_row[:], in_=rt_d[None, :]),
            lambda: sy.dma_start(out=col_e[0:64, :], in_=lng_d[:, None]),
            lambda: sy.dma_start(out=col_e[64:128, :], in_=lng_d[:, None]),
            lambda: sy.dma_start(out=col_f[0:64, :], in_=lnb_d[:, None]),
            lambda: sy.dma_start(out=col_f[64:128, :], in_=lnb_d[:, None]),
            lambda: sy.dma_start(out=gl_sb[:], in_=gl_d.rearrange("(c p) -> p c", p=128), allow_slow_non_contiguous=True),
            lambda: sy.dma_start(out=gh_sb[:], in_=gh_d.rearrange("(c p) -> p c", p=128), allow_slow_non_contiguous=True),
            lambda: sy.dma_start(out=b1l_sb[:], in_=b1l_d.rearrange("(c p) -> p c", p=128), allow_slow_non_contiguous=True),
        ):
            sX.inc(dma(), 16)
        X_CONST_ = sX.n
        for j in range(NT):
            sX.inc(sy.dma_start(out=outacc[:, j * D:(j + 1) * D],
                                in_=x_d[j * 128:(j + 1) * 128, :]), 16)
    X_CONST = 16 * 10
    X_ALL = X_CONST + 16 * NT

    @blk.gpsimd
    def _(g):
        sWB.inc(g.dma_start(out=rows_b[:, 0:D], in_=bproj_d[None, :]), 16)
        sWB.inc(g.dma_start(out=rows_b[:, D:2 * D], in_=b2l_d[None, :]), 16)
        sWB.inc(g.dma_start(out=rows_b[:, 2 * D:3 * D], in_=b2h_d[None, :]), 16)
        sG.inc(g.memset(cbias[:, 0:1], EPS_LN))
        sG.inc(g.memset(cbias[:, 1:2], -8.0))
        sG.inc(g.memset(ones128c[:], 1.0))
        sG.inc(g.memset(ones1r[:], 1.0))
        sG.inc(g.memset(ones1r_b[:], 1.0))
        sG.inc(g.memset(Ab[:], 0.0))
        sG.inc(g.memzero(A_bf[:]))
        sI.inc(g.dma_start(out=I_sb[:], in_=I_d.ap()), 16)
    W_SMALL = 16 * 3
    G_CONST = 7

    wctx = ExitStack()
    we = wctx.enter_context
    wqkv_sb = we(nc.sbuf_tensor("wqkv_sb", [128, 8 * 3 * D], BF16))
    wproj_sb = we(nc.sbuf_tensor("wproj_sb", [128, 8 * D], BF16))

    @blk.gpsimd
    def _(g):
        for kc in range(8):
            sW.inc(g.dma_start(out=wqkv_sb[:, kc * 3 * D:(kc + 1) * 3 * D],
                               in_=wqkv_d[kc * 128:(kc + 1) * 128, :]), 16)
        for kc in range(8):
            sW.inc(g.dma_start(out=wproj_sb[:, kc * D:(kc + 1) * D],
                               in_=wproj_d[kc * 128:(kc + 1) * 128, :]), 16)
    W_ATTN = sW.n

    @blk.vector
    def _(v):
        v.wait_ge(sX.h, X_ALL)
        v.wait_ge(sWB.h, W_SMALL)
        v.tensor_scalar(out=lnq_g[:], in0=col_e[:], scalar1=0.125, scalar2=None, op0=ALU.mult)
        v.tensor_scalar(out=lnq_b[:], in0=col_f[:], scalar1=0.125, scalar2=None, op0=ALU.mult)
        v.tensor_copy(out=lnk_g[:], in_=col_e[:])
        sV.inc(v.tensor_copy(out=lnk_b[:], in_=col_f[:]))

    @blk.tensor
    def _(t):
        t.wait_ge(sG.h, G_CONST)
        t.wait_ge(sX.h, X_ALL)
        t.matmul(pq[:], ones1r[:], rt_row[:, 0:512], start=True, stop=True)
        sM.inc(t.matmul(pq2[:], ones1r[:], rt_row[:, 512:1024], start=True, stop=True))

    @blk.vector
    def _(v):
        v.wait_ge(sM.h, sM.n)
        v.tensor_copy(out=rt_rep[:, 0:512], in_=pq[:])
        sV.inc(v.tensor_copy(out=rt_rep[:, 512:1024], in_=pq2[:]))
    V_RT = sV.n

    ln_c = 1.0 / D

    @blk.vector
    def _(v):
        v.wait_ge(sX.h, X_ALL)
        v.wait_ge(sV.h, V_RT)
        v.wait_ge(sG.h, G_CONST)
        v.wait_ge(sI.h, 16)
        VF(v, v.reduce_sum(out=rsum_c[:], in_=rt_rep[:], axis=AX.X))
        sV.inc(v.tensor_scalar(out=nrsum_c[:], in0=rsum_c[:], scalar1=-1.0, scalar2=None,
                               op0=ALU.mult))

    a_ev = {}
    a_tp = {}
    tp_banks = [(pq, pq2), (pSA, po[0])]
    for gi in range(4):
        j0 = gi * 4

        @blk.vector
        def _(v, j0=j0):
            for jj in range(4):
                base = (j0 + jj) * D
                xa = outacc[:, base:base + D]
                v.tensor_mul(out=qk_f32[:, 0:512], in0=outacc[:, base:base + 512],
                             in1=rt_rep[:, 0:512])
                v.reduce_sum(out=ca4[:, jj:jj + 1], in_=xa, axis=AX.X)
                v.tensor_mul(out=qk_f32[:, 512:1024], in0=outacc[:, base + 512:base + D],
                             in1=rt_rep[:, 512:1024])
                last = v.reduce_sum(out=ch4[:, jj:jj + 1], in_=qk_f32[:], axis=AX.X)
            VF(v, last)
            sV.inc(v.engine_nop())

        @blk.scalar
        def _(sc, j0=j0):
            sc.wait_ge(sV.h, sV.n)
            for jj in range(4):
                base = (j0 + jj) * D
                last = sc.activation(out=xn_sc[:], in_=outacc[:, base:base + D],
                                     func=AF.Square, accum_out=cb4[:, jj:jj + 1])
            sS.inc(last)

        @blk.vector
        def _(v):
            v.wait_ge(sS.h, sS.n)
            VF(v, v.tensor_scalar(out=cc4[:], in0=ca4[:], scalar1=ln_c, scalar2=None,
                                  op0=ALU.mult))
            VF(v, v.tensor_mul(out=cd4[:], in0=cc4[:], in1=cc4[:]))
            sV.inc(v.scalar_tensor_tensor(out=cb4[:], in0=cb4[:], scalar=ln_c,
                                          in1=cd4[:], op0=ALU.mult, op1=ALU.subtract))

        @blk.scalar
        def _(sc):
            sc.wait_ge(sV.h, sV.n)
            sS.inc(sc.activation(out=cd4[:], in_=cb4[:], func=AF.Sqrt, bias=EPS_LN))

        @blk.vector
        def _(v):
            v.wait_ge(sS.h, sS.n)
            VF(v, v.reciprocal(out=ce4[:], in_=cd4[:]))
            VF(v, v.tensor_mul(out=cf4[:], in0=cb4[:], in1=ce4[:]))
            sV.inc(v.tensor_mul(out=cf4[:], in0=cf4[:], in1=ce4[:]))

        @blk.scalar
        def _(sc):
            sc.wait_ge(sV.h, sV.n)
            sS.inc(sc.activation(out=cf4[:], in_=cf4[:], func=AF.Sqrt))

        @blk.vector
        def _(v, j0=j0):
            v.wait_ge(sS.h, sS.n)
            VF(v, v.tensor_scalar_max(cf4[:], cf4[:], 1e-12 / 32.0))
            v.tensor_copy(out=invr_t[:, j0:j0 + 4], in_=cf4[:])
            VF(v, v.reciprocal(out=cg4[:], in_=cf4[:]))
            VF(v, v.tensor_mul(out=cg4[:], in0=cg4[:], in1=ce4[:]))
            VF(v, v.scalar_tensor_tensor(out=cd4[:], in0=cc4[:], scalar=nrsum_c[:],
                                         in1=ch4[:], op0=ALU.mult, op1=ALU.add))
            sV.inc(v.tensor_mul(out=s_t[:, j0:j0 + 4], in0=cd4[:], in1=ce4[:]))

        for jj in range(4):
            j = j0 + jj

            @blk.vector
            def _(v, j=j, jj=jj):
                if j >= 2 and (j - 2) in a_tp:
                    v.wait_ge(sM.h, a_tp[j - 2])
                base = j * D
                sV.inc(v.tensor_scalar(out=qkhat[:, (j % 2) * D:(j % 2) * D + D],
                                       in0=outacc[:, base:base + D],
                                       scalar1=cc4[:, jj:jj + 1], scalar2=cg4[:, jj:jj + 1],
                                       op0=ALU.subtract, op1=ALU.mult))

            @blk.tensor
            def _(t, j=j):
                t.wait_ge(sV.h, sV.n)
                if j >= 2 and (j - 2) in a_ev:
                    t.wait_ge(sS.h, a_ev[j - 2])
                b0, b1 = tp_banks[j % 2]
                for kc in range(8):
                    bank = b0 if kc < 4 else b1
                    qtr = (kc % 4) * 128
                    mm = t.matmul(bank[:, qtr:qtr + 128],
                                  qkhat[(0):(128), (j % 2) * D + kc * 128:(j % 2) * D + kc * 128 + 128],
                                  I_sb[:], start=True, stop=True)
                sM.inc(mm)
            a_tp[j] = sM.n

            @blk.scalar
            def _(sc, j=j):
                sc.wait_ge(sM.h, a_tp[j])
                b0, b1 = tp_banks[j % 2]
                for kc in range(8):
                    bank = b0 if kc < 4 else b1
                    qtr = (kc % 4) * 128
                    last = sc.activation(out=nfm[:, kc * T + j * 128:kc * T + j * 128 + 128],
                                         in_=bank[:, qtr:qtr + 128], func=AF.Copy)
                sS.inc(last)
            a_ev[j] = sS.n
    NFM_S = sS.n
    T_NFM = sT.n
    V_PHA = sV.n

    RQ = []

    def r_prep():
        @blk.vector
        def _(v):
            v.wait_ge(sV.h, V_PHA)
            sV.inc(v.reduce_max(out=col_a[:], in_=s_t[:], axis=AX.X))

        @blk.sync
        def _(sy):
            sy.wait_ge(sV.h, sV.n)
            sX.inc(sy.dma_start(out=ds_s[:, None], in_=col_a[:]), 16)
            sy.wait_ge(sX.h, sX.n)
            sX.inc(sy.dma_start(out=tr128[:], in_=ds_s[None, :]), 16)

        @blk.vector
        def _(v):
            v.wait_ge(sX.h, sX.n)
            VF(v, v.reduce_max(out=sm1[:], in_=tr128[:], axis=AX.X))
            VF(v, v.memset(tmp1[:], 1.0))
            VF(v, v.tensor_scalar(out=tmp2[:], in0=tmp1[:], scalar1=float(2.0 ** -30),
                                  scalar2=None, op0=ALU.add))
            VF(v, v.tensor_sub(out=tmp2[:], in0=tmp2[:], in1=tmp1[:]))
            VF(v, v.tensor_scalar(out=gam_t[:], in0=tmp2[:], scalar1=1e-15, scalar2=None,
                                  op0=ALU.is_gt))
            VF(v, v.memset(tmp1[:], T32))
            VF(v, v.scalar_tensor_tensor(out=tadp[:], in0=gam_t[:], scalar=float(T64 - T32),
                                         in1=tmp1[:], op0=ALU.mult, op1=ALU.add))

        @blk.tensor
        def _(t):
            t.wait_ge(sV.h, sV.n)
            t.matmul(pb[:], ones1r[:], sm1[:], start=True, stop=True)
            sM.inc(t.matmul(pb2[:], ones1r[:], tadp[:], start=True, stop=True))

        @blk.vector
        def _(v):
            v.wait_ge(sM.h, sM.n)
            v.wait_ge(sG.h, G_CONST)
            VF(v, v.tensor_copy(out=smax_b[:], in_=pb[:]))
            VF(v, v.tensor_copy(out=tadb[:], in_=pb2[:]))
    RQ.append(r_prep)

    def r_cd_u1(it):
        def f():
            @blk.vector
            def _(v):
                if it > 0:
                    v.wait_ge(sM.h, sM.n)
                    VF(v, v.tensor_copy(out=Ab[:], in_=pb[:]))
                VF(v, v.tensor_tensor(out=mmv[:], in0=smax_b[:], in1=Ab[:], op=ALU.min))
                v.tensor_scalar(out=negmm10[:], in0=mmv[:], scalar1=-10.0, scalar2=None,
                                op0=ALU.mult)
                VF(v, v.tensor_scalar(out=mmc[:], in0=mmv[0:1, :], scalar1=C0, scalar2=None,
                                      op0=ALU.subtract))
                sV.inc(v.tensor_scalar(out=zt[:], in0=s_t[:], scalar1=Ab[:], scalar2=None,
                                       op0=ALU.min))

            @blk.scalar
            def _(ss):
                ss.wait_ge(sV.h, sV.n)
                sS.inc(ss.activation(out=cmpt[:], in_=zt[:], func=AF.Exp, bias=negmm10[:],
                                     scale=10.0, accum_out=rowsum[:]))
        return f

    def r_cd_u2():
        @blk.tensor
        def _(t):
            t.wait_ge(sS.h, sS.n)
            sM.inc(t.matmul(p1[0:1, 0:1], rowsum[:], ones128c[:], start=True, stop=True))

        @blk.scalar
        def _(ss):
            ss.wait_ge(sM.h, sM.n)
            sS.inc(ss.activation(out=lnv[:], in_=p1[0:1, 0:1], func=AF.Ln))

        @blk.vector
        def _(v):
            v.wait_ge(sS.h, sS.n)
            VF(v, v.scalar_tensor_tensor(out=Asc[:], in0=lnv[:], scalar=0.1,
                                         in1=mmc[:], op0=ALU.mult, op1=ALU.add))

    def r_cd_u3():
        @blk.tensor
        def _(t):
            t.wait_ge(sV.h, sV.n)
            sM.inc(t.matmul(pb[:], ones1r[:], Asc[:], start=True, stop=True))

    for it in range(N_CD):
        RQ.append(r_cd_u1(it))
        RQ.append(r_cd_u2)
        RQ.append(r_cd_u3)

    def r_key():
        @blk.vector
        def _(v):
            v.wait_ge(sM.h, sM.n)
            VF(v, v.tensor_copy(out=Ab[:], in_=pb[:]))
            VF(v, v.memset(hi_t[:], 1.0))
            VF(v, v.memset(mid_t[:], 0.0))
            VF(v, v.tensor_scalar(out=zt[:], in0=s_t[:], scalar1=Ab[:], scalar2=0.0,
                                  op0=ALU.subtract, op1=ALU.min))
            VF(v, v.tensor_scalar(out=key_t[:], in0=zt[:], scalar1=10.0, scalar2=None,
                                  op0=ALU.mult))
            VF(v, v.tensor_scalar(out=key_t[:], in0=key_t[:], scalar1=tadb[:], scalar2=None,
                                  op0=ALU.max))
            VF(v, v.tensor_scalar(out=lo_t[:], in0=tadp[:], scalar1=1.0, scalar2=None,
                                  op0=ALU.subtract))
            VF(v, v.tensor_add(out=mid_t[:], in0=lo_t[:], in1=hi_t[:]))
            VF(v, v.tensor_scalar(out=mid_t[:], in0=mid_t[:], scalar1=0.5, scalar2=None,
                                  op0=ALU.mult))
    RQ.append(r_key)

    def r_bis_u1():
        @blk.tensor
        def _(t):
            t.wait_ge(sV.h, sV.n)
            sM.inc(t.matmul(pb[:], ones1r[:], mid_t[:], start=True, stop=True))

        @blk.vector
        def _(v):
            v.wait_ge(sM.h, sM.n)
            VF(v, v.tensor_copy(out=midb[:], in_=pb[:]))
            VF(v, v.tensor_scalar(out=cmpt[:], in0=key_t[:], scalar1=midb[:], scalar2=None,
                                  op0=ALU.is_gt))
            sV.inc(v.reduce_sum(out=rowsum[:], in_=cmpt[:], axis=AX.X))

    def r_bis_u2():
        @blk.tensor
        def _(t):
            t.wait_ge(sV.h, sV.n)
            sM.inc(t.matmul(p1[0:1, 0:1], rowsum[:], ones128c[:], start=True, stop=True))

        @blk.vector
        def _(v):
            v.wait_ge(sM.h, sM.n)
            VF(v, v.tensor_scalar(out=gam_t[:], in0=p1[0:1, 0:1], scalar1=float(NHEAVY),
                                  scalar2=None, op0=ALU.is_ge))
            VF(v, v.tensor_sub(out=tmp1[:], in0=mid_t[:], in1=lo_t[:]))
            VF(v, v.tensor_mul(out=tmp1[:], in0=tmp1[:], in1=gam_t[:]))
            VF(v, v.tensor_add(out=lo_t[:], in0=lo_t[:], in1=tmp1[:]))
            VF(v, v.tensor_sub(out=tmp2[:], in0=hi_t[:], in1=mid_t[:]))
            VF(v, v.tensor_scalar(out=tmp3[:], in0=gam_t[:], scalar1=-1.0, scalar2=1.0,
                                  op0=ALU.mult, op1=ALU.add))
            VF(v, v.tensor_mul(out=tmp2[:], in0=tmp2[:], in1=tmp3[:]))
            VF(v, v.tensor_sub(out=hi_t[:], in0=hi_t[:], in1=tmp2[:]))
            VF(v, v.tensor_add(out=mid_t[:], in0=lo_t[:], in1=hi_t[:]))
            VF(v, v.tensor_scalar(out=mid_t[:], in0=mid_t[:], scalar1=0.5, scalar2=None,
                                  op0=ALU.mult))

    for it in range(N_BIS):
        RQ.append(r_bis_u1)
        RQ.append(r_bis_u2)

    def r_mask1():
        @blk.tensor
        def _(t):
            t.wait_ge(sV.h, sV.n)
            t.matmul(pb[:], ones1r[:], lo_t[:], start=True, stop=True)
            sM.inc(t.matmul(pb2[:], ones1r[:], hi_t[:], start=True, stop=True))

        @blk.vector
        def _(v):
            v.wait_ge(sM.h, sM.n)
            VF(v, v.tensor_copy(out=lob[:], in_=pb[:]))
            VF(v, v.tensor_copy(out=hib[:], in_=pb2[:]))
            VF(v, v.tensor_scalar(out=cmpt[:], in0=key_t[:], scalar1=hib[:], scalar2=None,
                                  op0=ALU.is_gt))
            VF(v, v.reduce_sum(out=rowsum[:], in_=cmpt[:], axis=AX.X))
            VF(v, v.tensor_scalar(out=eqt[:], in0=key_t[:], scalar1=lob[:], scalar2=None,
                                  op0=ALU.is_gt))
            VF(v, v.tensor_scalar(out=zt[:], in0=key_t[:], scalar1=hib[:], scalar2=None,
                                  op0=ALU.is_le))
            sV.inc(v.tensor_mul(out=eqt[:], in0=eqt[:], in1=zt[:]))
    RQ.append(r_mask1)

    def r_mask2():
        @blk.tensor
        def _(t):
            t.wait_ge(sV.h, sV.n)
            t.matmul(p1[0:1, 0:1], rowsum[:], ones128c[:], start=True, stop=True)
            sM.inc(t.matmul(p2, ones128c[:], eqt[:], start=True, stop=True))

        @blk.vector
        def _(v):
            v.wait_ge(sM.h, sM.n)
            VF(v, v.tensor_copy(out=nhi_t[:], in_=p1[0:1, 0:1]))
            VF(v, v.tensor_copy(out=tr1[:], in_=p2))

        @blk.sync
        def _(sy):
            sy.wait_ge(sV.h, sV.n)
            sX.inc(sy.dma_start(out=ds_c[None, :], in_=tr1[0:1, :]), 16)
            sy.wait_ge(sX.h, sX.n)
            sX.inc(sy.dma_start(out=tr16b[:], in_=ds_c[:, None]), 16)
    RQ.append(r_mask2)

    def r_mask3():
        @blk.tensor
        def _(t):
            t.wait_ge(sX.h, sX.n)
            sM.inc(t.matmul(p1[0:1, 0:NT], tr16b[:], U_sb[:], start=True, stop=True))

        @blk.vector
        def _(v):
            v.wait_ge(sM.h, sM.n)
            VF(v, v.tensor_copy(out=colpre_sb[:], in_=p1[0:1, 0:NT]))
            VF(v, v.tensor_scalar(out=nhi_t[:], in0=nhi_t[:], scalar1=-1.0,
                                  scalar2=float(NHEAVY), op0=ALU.mult, op1=ALU.add))

        @blk.tensor
        def _(t):
            t.wait_ge(sV.h, sV.n)
            t.matmul(pr[:], L_sb[:], eqt[:], start=True, stop=False)
            t.matmul(pr[:], ones1r[:], colpre_sb[:], start=False, stop=True)
            sM.inc(t.matmul(pb[:], ones1r[:], nhi_t[:], start=True, stop=True))

        @blk.vector
        def _(v):
            v.wait_ge(sM.h, sM.n)
            VF(v, v.tensor_copy(out=fcb[:], in_=pb[:]))
            VF(v, v.tensor_scalar(out=zt[:], in0=pr[:], scalar1=fcb[:], scalar2=None,
                                  op0=ALU.is_le))
            VF(v, v.tensor_mul(out=zt[:], in0=zt[:], in1=eqt[:]))
            sV.inc(v.tensor_add(out=mask_t[:], in0=cmpt[:], in1=zt[:]))
    RQ.append(r_mask3)

    def drain_routing(n):
        import os as _os2
        if _os2.environ.get("KDRAIN_MODE", "ileave") == "skip":
            for _ in range(n):
                if RQ:
                    RQ.pop(0)
            return
        for _ in range(n):
            if RQ:
                RQ.pop(0)()

    import os as _os
    _KM = _os.environ.get("KDRAIN_MODE", "ileave")
    if _KM == "all":
        drain_routing(len(RQ))
    elif _KM.isdigit():
        drain_routing(int(_KM))

    def _dr(kind, n):
        if _KM == "skip":
            for _ in range(n):
                if RQ:
                    RQ.pop(0)
            return
        if _KM == "ileave":
            if kind != "head":
                drain_routing(n)
        elif _KM == kind:
            drain_routing(n)


    fb = [pq, pq2, pSA]
    bb = [po[0], po[1], po[2], po[3]]
    PM = {}

    def f_w1(j):
        @blk.tensor
        def _(t, j=j):
            if j == 0:
                t.wait_ge(sS.h, NFM_S)
                t.wait_ge(sW.h, W_ATTN)
                t.wait_ge(sWB.h, W_SMALL)
                t.wait_ge(sV.h, V_RT)
            else:
                t.wait_ge(sV.h, PM[('gbB', j - 1)])
            for og in range(3):
                for kc in range(8):
                    mm = t.matmul(fb[og][:],
                                  nfm[:, kc * T + j * 128:kc * T + j * 128 + 128],
                                  wqkv_sb[:, kc * 3 * D + og * 512:kc * 3 * D + og * 512 + 512],
                                  start=(kc == 0), stop=(kc == 7))
            sM.inc(mm)
        PM[('w1', j)] = sM.n

    def f_ev1(j):
        @blk.vector
        def _(v, j=j):
            v.wait_ge(sM.h, PM[('w1', j)])
            iv = invr_t[:, j:j + 1]
            v.tensor_scalar(out=qk_f32[:, 0:512], in0=pq[:], scalar1=iv, scalar2=None, op0=ALU.mult)
            v.tensor_scalar(out=qk_f32[:, 512:1024], in0=pq2[:], scalar1=iv, scalar2=None, op0=ALU.mult)
            v.reduce_sum(out=stat_a[:], in_=qk_f32.rearrange("p (s k) -> p s k", k=HD), axis=AX.X)
            sV.inc(v.tensor_scalar(out=xn_sc[:, 0:512], in0=pSA[:], scalar1=iv, scalar2=None,
                                   op0=ALU.mult))
        PM[('ev1', j)] = sV.n

    def f_w2(j):
        @blk.tensor
        def _(t, j=j):
            t.wait_ge(sV.h, PM[('ev1', j)])
            for og in range(3, 6):
                for kc in range(8):
                    mm = t.matmul(fb[og - 3][:],
                                  nfm[:, kc * T + j * 128:kc * T + j * 128 + 128],
                                  wqkv_sb[:, kc * 3 * D + og * 512:kc * 3 * D + og * 512 + 512],
                                  start=(kc == 0), stop=(kc == 7))
            sM.inc(mm)
        PM[('w2', j)] = sM.n

    def f_ev2a(j):
        @blk.vector
        def _(v, j=j):
            v.wait_ge(sM.h, PM[('w2', j)])
            iv = invr_t[:, j:j + 1]
            v.tensor_scalar(out=xn_sc[:, 512:1024], in0=pq[:], scalar1=iv, scalar2=None,
                            op0=ALU.mult)
            v.reduce_sum(out=stat_e[:], in_=xn_sc.rearrange("p (s k) -> p s k", k=HD),
                         axis=AX.X)
            v.tensor_mul(out=pq[:], in0=qk_f32[:, 0:512], in1=qk_f32[:, 0:512])
            v.reduce_sum(out=stat_b[:, 0:8], in_=pq.rearrange("p (s k) -> p s k", k=HD), axis=AX.X)
            v.tensor_mul(out=pq[:], in0=qk_f32[:, 512:1024], in1=qk_f32[:, 512:1024])
            v.reduce_sum(out=stat_b[:, 8:16], in_=pq.rearrange("p (s k) -> p s k", k=HD), axis=AX.X)
            v.tensor_mul(out=pq[:], in0=xn_sc[:, 0:512], in1=xn_sc[:, 0:512])
            v.reduce_sum(out=stat_f[:, 0:8], in_=pq.rearrange("p (s k) -> p s k", k=HD), axis=AX.X)
            v.tensor_mul(out=pq[:], in0=xn_sc[:, 512:1024], in1=xn_sc[:, 512:1024])
            VF(v, v.reduce_sum(out=stat_f[:, 8:16], in_=pq.rearrange("p (s k) -> p s k", k=HD),
                               axis=AX.X))
            sV.inc(v.engine_nop())
        PM[('ev2a', j)] = sV.n

    def f_muvar(j):
        @blk.vector
        def _(v, j=j):
            VF(v, v.tensor_scalar(out=stat_a[:], in0=stat_a[:], scalar1=1.0 / HD, scalar2=None,
                                  op0=ALU.mult))
            VF(v, v.tensor_scalar(out=stat_e[:], in0=stat_e[:], scalar1=1.0 / HD, scalar2=None,
                                  op0=ALU.mult))
            VF(v, v.tensor_mul(out=stat_c[:], in0=stat_a[:], in1=stat_a[:]))
            VF(v, v.scalar_tensor_tensor(out=stat_b[:], in0=stat_b[:], scalar=1.0 / HD,
                                         in1=stat_c[:], op0=ALU.mult, op1=ALU.subtract))
            VF(v, v.tensor_mul(out=stat_c[:], in0=stat_e[:], in1=stat_e[:]))
            sV.inc(v.scalar_tensor_tensor(out=stat_f[:], in0=stat_f[:], scalar=1.0 / HD,
                                          in1=stat_c[:], op0=ALU.mult, op1=ALU.subtract))
        PM[('muvar', j)] = sV.n

    def f_sd(j):
        @blk.scalar
        def _(sc, j=j):
            sc.wait_ge(sV.h, PM[('muvar', j)])
            sc.activation(out=stat_b[:], in_=stat_b[:], func=AF.Sqrt, bias=EPS_LN)
            sS.inc(sc.activation(out=stat_f[:], in_=stat_f[:], func=AF.Sqrt, bias=EPS_LN))
        PM[('sd', j)] = sS.n

    def f_hat(j):
        @blk.vector
        def _(v, j=j):
            v.wait_ge(sS.h, PM[('sd', j)])
            if j >= 1:
                v.wait_ge(sM.h, PM[('tpB', j - 1)])
            VF(v, v.reciprocal(out=stat_d[:], in_=stat_b[:]))
            VF(v, v.reciprocal(out=stat_h[:], in_=stat_f[:]))
            for h in range(H):
                v.tensor_scalar(out=qkhat[:, h * HD:(h + 1) * HD],
                                in0=qk_f32[:, h * HD:(h + 1) * HD],
                                scalar1=stat_a[:, h:h + 1], scalar2=stat_d[:, h:h + 1],
                                op0=ALU.subtract, op1=ALU.mult)
            for h in range(H):
                last = v.tensor_scalar(out=qkhat[:, D + h * HD:D + (h + 1) * HD],
                                       in0=xn_sc[:, h * HD:(h + 1) * HD],
                                       scalar1=stat_e[:, h:h + 1], scalar2=stat_h[:, h:h + 1],
                                       op0=ALU.subtract, op1=ALU.mult)
            sV.inc(last)
        PM[('hat', j)] = sV.n

    def f_ev2v(j):
        @blk.vector
        def _(v, j=j):
            if j >= 1:
                v.wait_ge(sM.h, PM[('av', j - 1)])
            iv = invr_t[:, j:j + 1]
            v.tensor_scalar(out=v_bf[:, 0:512], in0=pq2[:], scalar1=iv, scalar2=None, op0=ALU.mult)
            sV.inc(v.tensor_scalar(out=v_bf[:, 512:1024], in0=pSA[:], scalar1=iv, scalar2=None,
                                   op0=ALU.mult))
        PM[('ev2v', j)] = sV.n

    def f_tpA(j):
        @blk.tensor
        def _(t, j=j):
            t.wait_ge(sV.h, PM[('ev2v', j)])
            for c in range(12):
                mm = t.matmul(fb[c % 3][:, (c // 3) * 128:(c // 3) * 128 + 128],
                              qkhat[:, c * 128:(c + 1) * 128], I_sb[:],
                              start=True, stop=True)
            sM.inc(mm)
        PM[('tpA', j)] = sM.n

    def f_gbA(j):
        @blk.vector
        def _(v, j=j):
            v.wait_ge(sM.h, PM[('tpA', j)])
            for c in range(12):
                g_ap = lnq_g[:] if c < 8 else lnk_g[:]
                b_ap = lnq_b[:] if c < 8 else lnk_b[:]
                last = v.tensor_scalar(out=qkT[:, c * 128:(c + 1) * 128],
                                       in0=fb[c % 3][:, (c // 3) * 128:(c // 3) * 128 + 128],
                                       scalar1=g_ap, scalar2=b_ap, op0=ALU.mult, op1=ALU.add)
            sV.inc(last)
        PM[('gbA', j)] = sV.n

    def f_tpB(j):
        @blk.tensor
        def _(t, j=j):
            t.wait_ge(sV.h, PM[('gbA', j)])
            for c in range(12, 16):
                mm = t.matmul(pq[:, (c - 12) * 128:(c - 12) * 128 + 128],
                              qkhat[:, c * 128:(c + 1) * 128], I_sb[:],
                              start=True, stop=True)
            sM.inc(mm)
        PM[('tpB', j)] = sM.n

    def f_gbB(j):
        @blk.vector
        def _(v, j=j):
            v.wait_ge(sM.h, PM[('tpB', j)])
            for c in range(12, 16):
                last = v.tensor_scalar(out=qkT[:, c * 128:(c + 1) * 128],
                                       in0=pq[:, (c - 12) * 128:(c - 12) * 128 + 128],
                                       scalar1=lnk_g[:], scalar2=lnk_b[:],
                                       op0=ALU.mult, op1=ALU.add)
            sV.inc(last)
        PM[('gbB', j)] = sV.n

    def b_smm(j):
        for h in range(H):
            @blk.tensor
            def _(t, j=j, h=h):
                if h == 0:
                    t.wait_ge(sV.h, PM[('gbB', j)])
                    if j >= 1:
                        t.wait_ge(sV.h, PM[('add', j - 1)])
                c = h // 2
                r0 = (h % 2) * HD
                sM.inc(t.matmul(bb[h // 4][:, (h % 4) * 128:(h % 4) * 128 + 128],
                                qkT[r0:r0 + HD, c * 128:(c + 1) * 128],
                                qkT[r0:r0 + HD, D + c * 128:D + (c + 1) * 128],
                                start=True, stop=True))
            if h == 7:
                PM[('smm1', j)] = sM.n
        PM[('smm', j)] = sM.n

    def b_exp(j, w):
        @blk.scalar
        def _(sc, j=j, w=w):
            sc.wait_ge(sM.h, PM[('smm1', j)] if w == 0 else PM[('smm', j)])
            for h in range(8 * w, 8 * w + 8):
                bank = bb[h // 4]
                qtr = (h % 4) * 128
                sc.activation(out=A_bf[0:64, h * 128:h * 128 + 64],
                              in_=bank[0:64, qtr:qtr + 64], func=AF.Exp, bias=-8.0)
                last = sc.activation(out=A_bf[64:128, h * 128 + 64:h * 128 + 128],
                                     in_=bank[64:128, qtr + 64:qtr + 128], func=AF.Exp,
                                     bias=-8.0)
            sS.inc(last)
        PM[('exp', j, w)] = sS.n

    def b_norm(j, w):
        @blk.vector
        def _(v, j=j, w=w):
            v.wait_ge(sS.h, PM[('exp', j, w)])
            VF(v, v.reduce_sum(out=sums16[:, 8 * w:8 * w + 8],
                               in_=A_bf[:, 1024 * w:1024 * w + 1024].rearrange(
                                   "p (s k) -> p s k", k=128),
                               axis=AX.X))
            VF(v, v.reciprocal(out=invs16[:, 8 * w:8 * w + 8], in_=sums16[:, 8 * w:8 * w + 8]))
            for h in range(8 * w, 8 * w + 8):
                last = v.tensor_scalar(out=A_bf[:, h * 128:(h + 1) * 128],
                                       in0=A_bf[:, h * 128:(h + 1) * 128],
                                       scalar1=invs16[:, h:h + 1], scalar2=None, op0=ALU.mult)
            sV.inc(last)
        PM[('norm', j, w)] = sV.n

    def b_at(j, w):
        @blk.tensor
        def _(t, j=j, w=w):
            t.wait_ge(sV.h, PM[('norm', j, w)])
            for h in range(8 * w, 8 * w + 8):
                mm = t.matmul(bb[2 * w + h % 2][:, ((h - 8 * w) // 2) * 128:((h - 8 * w) // 2) * 128 + 128],
                              A_bf[:, h * 128:(h + 1) * 128], I_sb[:],
                              start=True, stop=True)
            sM.inc(mm)
        PM[('at', j, w)] = sM.n

    def b_atev(j, w):
        @blk.vector
        def _(v, j=j, w=w):
            v.wait_ge(sM.h, PM[('at', j, w)])
            for h in range(8 * w, 8 * w + 8):
                last = v.tensor_copy(out=AT_bf[:, h * 128:(h + 1) * 128],
                                     in_=bb[2 * w + h % 2][:, ((h - 8 * w) // 2) * 128:((h - 8 * w) // 2) * 128 + 128])
            sV.inc(last)
        PM[('atev', j, w)] = sV.n

    def b_av(j, w):
        @blk.tensor
        def _(t, j=j, w=w):
            t.wait_ge(sV.h, PM[('atev', j, w)])
            for h in range(8 * w, 8 * w + 8):
                c = h // 2
                r0 = (h % 2) * HD
                bank = po[2 * w + c % 2]
                mm = t.matmul(bank[r0:r0 + HD, ((c - 4 * w) // 2) * 128:((c - 4 * w) // 2) * 128 + 128],
                              v_bf[:, h * HD:(h + 1) * HD],
                              AT_bf[:, h * 128:(h + 1) * 128], start=True, stop=True)
            sM.inc(mm)
        PM[('av', j, w)] = sM.n
        if w == 1:
            PM[('av', j)] = sM.n

    def b_ao(j, w):
        @blk.scalar
        def _(sc, j=j, w=w):
            sc.wait_ge(sM.h, PM[('av', j, w)])
            for c in range(4 * w, 4 * w + 4):
                bank = po[2 * w + c % 2]
                last = sc.activation(out=ao_fm[:, c * 128:(c + 1) * 128],
                                     in_=bank[:, ((c - 4 * w) // 2) * 128:((c - 4 * w) // 2) * 128 + 128],
                                     func=AF.Copy)
            sS.inc(last)
        PM[('ao', j)] = sS.n

    def b_proj(j):
        @blk.tensor
        def _(t, j=j):
            t.wait_ge(sS.h, PM[('ao', j)])
            for of in range(2):
                dst = po[2 + of]
                for dc in range(8):
                    t.matmul(dst[:], ao_fm[:, dc * 128:(dc + 1) * 128],
                             wproj_sb[:, dc * D + of * 512:dc * D + of * 512 + 512],
                             start=(dc == 0), stop=False)
                mm = t.matmul(dst[:], ones1r_b[:], rows_b[0:1, of * 512:of * 512 + 512],
                              start=False, stop=True)
            sM.inc(mm)
        PM[('proj', j)] = sM.n

    def b_add(j):
        @blk.vector
        def _(v, j=j):
            v.wait_ge(sM.h, PM[('proj', j)])
            for of in range(2):
                sl = outacc[:, j * D + of * 512:j * D + of * 512 + 512]
                last = v.tensor_add(out=sl, in0=po[2 + of][:], in1=sl)
            sV.inc(last)
        PM[('add', j)] = sV.n

    for j in range(NT):
        f_w1(j); f_ev1(j); f_w2(j); f_ev2a(j)
        f_muvar(j)
        _dr('vbf', 2)
        f_sd(j); f_hat(j)
        _dr('vbf', 2)
        if j >= 1:
            b_smm(j - 1)
            b_exp(j - 1, 0); b_norm(j - 1, 0); b_at(j - 1, 0); b_atev(j - 1, 0); b_av(j - 1, 0)
            b_exp(j - 1, 1); b_norm(j - 1, 1); b_at(j - 1, 1); b_atev(j - 1, 1); b_av(j - 1, 1)
        f_ev2v(j)
        _dr('vbf', 2)
        f_tpA(j); f_gbA(j); f_tpB(j); f_gbB(j)
        if j >= 1:
            b_ao(j - 1, 0); b_ao(j - 1, 1); b_proj(j - 1); b_add(j - 1)
            _dr('proj', 3)
    j = NT - 1
    b_smm(j)
    b_exp(j, 0); b_norm(j, 0); b_at(j, 0); b_atev(j, 0); b_av(j, 0)
    b_exp(j, 1); b_norm(j, 1); b_at(j, 1); b_atev(j, 1); b_av(j, 1)
    b_ao(j, 0); b_ao(j, 1); b_proj(j); b_add(j)
    V_ATTN = sV.n

    wctx.close()
    hctx = ExitStack()
    he = hctx.enter_context
    w1l_sb = he(nc.sbuf_tensor("w1l_sb", [128, 8 * LHID], BF16))
    w2l_sb = he(nc.sbuf_tensor("w2l_sb", [128, 4 * D], BF16))
    w1h_sb = he(nc.sbuf_tensor("w1h_sb", [128, 8 * 1024], BF16))
    w2h_sb = he(nc.sbuf_tensor("w2h_sb", [128, 8 * 1024], BF16))
    hl_fm = he(nc.sbuf_tensor("hl_fm", [128, 4 * T], BF16))

    W_QKV_DONE = PM[('w2', NT - 1)]

    @blk.gpsimd
    def _(g):
        g.wait_ge(sM.h, W_QKV_DONE)
        for kc in range(8):
            sW.inc(g.dma_start(out=w1l_sb[:, kc * LHID:(kc + 1) * LHID],
                               in_=w1l_d[kc * 128:(kc + 1) * 128, :]), 16)
        for hc in range(4):
            sW.inc(g.dma_start(out=w2l_sb[:, hc * D:(hc + 1) * D],
                               in_=w2l_d[hc * 128:(hc + 1) * 128, :]), 16)
    W_LGT = sW.n

    @blk.vector
    def _(v):
        v.wait_ge(sW.h, W_LGT)
        for kc in range(8):
            last = v.tensor_scalar(out=w1l_sb[:, kc * LHID:(kc + 1) * LHID],
                                   in0=w1l_sb[:, kc * LHID:(kc + 1) * LHID],
                                   scalar1=gl_sb[:, kc:kc + 1], scalar2=None, op0=ALU.mult)
        sV.inc(last)
    V_WL = sV.n

    l_mark = {}
    lt_ready = {}
    lo_add = {}

    def emit_hidden(tg):
        for hc in range(4):
            idx = tg * 4 + hc

            @blk.tensor
            def _(t, hc=hc, tg=tg, idx=idx):
                if idx == 0:
                    t.wait_ge(sV.h, max(V_WL, PM[('gbB', NT - 1)]))
                if idx >= 2:
                    t.wait_ge(sS.h, l_mark[idx - 2])
                dst = pq if idx % 2 == 0 else pq2
                for kc in range(8):
                    mm = t.matmul(dst[:], w1l_sb[:, kc * LHID + hc * 128:kc * LHID + hc * 128 + 128],
                                  nfm[:, kc * T + tg * 512:kc * T + (tg + 1) * 512],
                                  start=(kc == 0), stop=(kc == 7))
                sM.inc(mm)

            @blk.scalar
            def _(sc, hc=hc, tg=tg, idx=idx):
                sc.wait_ge(sM.h, sM.n)
                src = pq if idx % 2 == 0 else pq2
                sS.inc(sc.activation(out=hl_fm[:, hc * T + tg * 512:hc * T + (tg + 1) * 512],
                                     in_=src[:], func=AF.Gelu, bias=b1l_sb[:, hc:hc + 1]))
            l_mark[idx] = sS.n
            _dr('light', min(2, max(0, len(RQ) - 3)))
        lt_ready[tg] = sS.n

    def emit_out(j):
        @blk.tensor
        def _(t, j=j):
            t.wait_ge(sS.h, lt_ready[j // 4])
            if j >= 2:
                t.wait_ge(sV.h, lo_add[j - 2])
            for of in range(2):
                dst = po[2 * (j % 2) + of]
                for hc in range(4):
                    t.matmul(dst[:], hl_fm[:, hc * T + j * 128:hc * T + j * 128 + 128],
                             w2l_sb[:, hc * D + of * 512:hc * D + of * 512 + 512],
                             start=(hc == 0), stop=False)
                mm = t.matmul(dst[:], ones1r_b[:], rows_b[0:1, D + of * 512:D + of * 512 + 512],
                              start=False, stop=True)
            sM.inc(mm)

        @blk.vector
        def _(v, j=j):
            v.wait_ge(sM.h, sM.n)
            for of in range(2):
                sl = outacc[:, j * D + of * 512:j * D + of * 512 + 512]
                last = v.tensor_add(out=sl, in0=po[2 * (j % 2) + of][:], in1=sl)
            sV.inc(last)
        lo_add[j] = sV.n

    emit_hidden(0)
    emit_hidden(1)
    for j in range(4):
        emit_out(j)
    emit_hidden(2)
    for j in range(4, 8):
        emit_out(j)
    emit_hidden(3)
    for j in range(8, 16):
        emit_out(j)
    V_LIGHT = sV.n

    drain_routing(len(RQ))
    V_MASK = sV.n

    if dbg:
        @blk.sync
        def _(sy):
            sy.wait_ge(sV.h, V_MASK)
            sy.wait_ge(sT.h, T_NFM)
            sSt.inc(sy.dma_start(out=dbg_rt[:, 0:NT], in_=s_t[:]), 16)
            sSt.inc(sy.dma_start(out=dbg_rt[:, NT:2 * NT], in_=invr_t[:]), 16)
            sSt.inc(sy.dma_start(out=dbg_rt[:, 2 * NT:3 * NT], in_=mask_t[:]), 16)
            sSt.inc(sy.dma_start(out=dbg_rt[:, 3 * NT:4 * NT], in_=key_t[:]), 16)
            sSt.inc(sy.dma_start(out=dbg_rt[:, 64:65], in_=smax_b[:], allow_slow_non_contiguous=True), 16)
            sSt.inc(sy.dma_start(out=dbg_rt[:, 65:66], in_=Ab[:], allow_slow_non_contiguous=True), 16)
            sSt.inc(sy.dma_start(out=dbg_rt[:, 66:67], in_=tadb[:], allow_slow_non_contiguous=True), 16)
            sSt.inc(sy.dma_start(out=dbg_rt[:, 67:68], in_=lob[:], allow_slow_non_contiguous=True), 16)
            sSt.inc(sy.dma_start(out=dbg_rt[:, 68:69], in_=hib[:], allow_slow_non_contiguous=True), 16)
            sSt.inc(sy.dma_start(out=dbg_rt[:, 69:70], in_=fcb[:], allow_slow_non_contiguous=True), 16)
            sSt.inc(sy.dma_start(out=dbg_rt[0:1, 70:71], in_=gam_t[:], allow_slow_non_contiguous=True), 16)
            sSt.inc(sy.dma_start(out=dbg_rt[0:1, 71:72], in_=tadp[:], allow_slow_non_contiguous=True), 16)
            sSt.inc(sy.dma_start(out=dbg_traj, in_=traj[:]), 16)
            sSt.inc(sy.dma_start(out=dbg_nfm, in_=nfm[:]), 16)


    if dbg:
        @blk.sync
        def _(sy):
            sy.wait_ge(sV.h, V_LIGHT)
            for j in range(NT):
                sSt.inc(sy.dma_start(out=dbg_light[j * 128:(j + 1) * 128, :],
                                     in_=outacc[:, j * D:(j + 1) * D]), 16)

    HV_STATE = {'gm': None, 'am': None}
    for g in range(4):
        @blk.gpsimd
        def _(gg, g=g):
            if g == 0:
                gg.wait_ge(sM.h, W_QKV_DONE)
            else:
                gg.wait_ge(sM.h, sM.n)
            for kc in range(8):
                sWH.inc(gg.dma_start(out=w1h_sb[:, kc * 1024:(kc + 1) * 1024],
                                     in_=w1h_d[kc * 128:(kc + 1) * 128,
                                               g * 1024:(g + 1) * 1024]), 16)
            for hc in range(8):
                sWH.inc(gg.dma_start(out=w2h_sb[:, hc * 1024:(hc + 1) * 1024],
                                     in_=w2h_d[g * 1024 + hc * 128:g * 1024 + (hc + 1) * 128, :]), 16)
            sWH.inc(gg.dma_start(out=b1h_sb[:],
                                 in_=b1h_d[g * 1024:(g + 1) * 1024].rearrange(
                                     "(c p) -> p c", p=128),
                                 allow_slow_non_contiguous=True), 16)
        W_G = sWH.n

        @blk.vector
        def _(v, g=g, W_G=W_G):
            v.wait_ge(sWH.h, W_G)
            if g == 0:
                v.wait_ge(sV.h, V_LIGHT)
            for kc in range(8):
                last = v.tensor_scalar(out=w1h_sb[:, kc * 1024:(kc + 1) * 1024],
                                       in0=w1h_sb[:, kc * 1024:(kc + 1) * 1024],
                                       scalar1=gh_sb[:, kc:kc + 1], scalar2=None, op0=ALU.mult)
            sV.inc(last)
        V_WG = sV.n

        hq = [(pq, 0), (pq, 256), (pq2, 0), (pq2, 256), (pSA, 0), (pSA, 256), (po[0], 0), (po[0], 256)]
        for tg in range(8):
            @blk.tensor
            def _(t, g=g, tg=tg, V_WG=V_WG):
                if tg == 0:
                    t.wait_ge(sV.h, V_WG)
                if HV_STATE['gm'] is not None:
                    t.wait_ge(sS.h, HV_STATE['gm'])
                for hc in range(8):
                    bank, off = hq[hc]
                    for kc in range(8):
                        mm = t.matmul(bank[:, off:off + 256],
                                      w1h_sb[:, kc * 1024 + hc * 128:kc * 1024 + hc * 128 + 128],
                                      nfm[:, kc * T + tg * 256:kc * T + (tg + 1) * 256],
                                      start=(kc == 0), stop=(kc == 7))
                    sM.inc(mm)
                    HV_STATE[('h', hc)] = sM.n

            for hc in range(8):
                @blk.scalar
                def _(sc, g=g, hc=hc):
                    sc.wait_ge(sM.h, HV_STATE[('h', hc | 1)])
                    bank, off = hq[hc]
                    sS.inc(sc.activation(out=A_bf[:, hc * 256:(hc + 1) * 256],
                                         in_=bank[:, off:off + 256], func=AF.Gelu,
                                         bias=b1h_sb[:, hc:hc + 1]))
                HV_STATE[('g', hc)] = sS.n
            HV_STATE['gm'] = sS.n

            for hc in range(8):
                @blk.tensor
                def _(t, g=g, hc=hc):
                    t.wait_ge(sS.h, HV_STATE[('g', hc)])
                    if hc == 0 and HV_STATE['am'] is not None:
                        t.wait_ge(sV.h, HV_STATE['am'])
                    ob = [po[1], po[2], po[3], pmisc]
                    for st in range(2):
                        for of in range(2):
                            mm = t.matmul(ob[st * 2 + of][:],
                                          A_bf[:, hc * 256 + st * 128:hc * 256 + st * 128 + 128],
                                          w2h_sb[:, hc * 1024 + of * 512:hc * 1024 + of * 512 + 512],
                                          start=(hc == 0), stop=(hc == 7 and g < 3))
                    if g == 3 and hc == 7:
                        for st in range(2):
                            for of in range(2):
                                mm = t.matmul(ob[st * 2 + of][:], ones1r_b[:],
                                              rows_b[0:1, 2 * D + of * 512:2 * D + of * 512 + 512],
                                              start=False, stop=True)
                    sM.inc(mm)

            @blk.vector
            def _(v, g=g, tg=tg):
                v.wait_ge(sM.h, sM.n)
                ob = [po[1], po[2], po[3], pmisc]
                for st in range(2):
                    jj = tg * 2 + st
                    for of in range(2):
                        sl = outacc[:, jj * D + of * 512:jj * D + of * 512 + 512]
                        last = v.scalar_tensor_tensor(out=sl, in0=ob[st * 2 + of][:],
                                                      scalar=mask_t[:, jj:jj + 1], in1=sl,
                                                      op0=ALU.mult, op1=ALU.add)
                sV.inc(last)
            HV_STATE['am'] = sV.n

    @blk.sync
    def _(sy):
        sy.wait_ge(sV.h, sV.n)
        for j in range(NT):
            sSt.inc(sy.dma_start(out=out_d[j * 128:(j + 1) * 128, :],
                                 in_=outacc[:, j * D:(j + 1) * D]), 16)
        sy.wait_ge(sSt.h, sSt.n)

    hctx.close()
    ctx.close()
    return nc


_NC_CACHE = []


def kernel(L0_attn_x, L0_attn_W_qkv, L0_attn_ln_gamma, L0_attn_ln_beta, L0_attn_W_proj,
           L0_attn_b_proj, L0_ffn_routing_token, L0_ffn_gamma_light, L0_ffn_w1_light,
           L0_ffn_b1_light, L0_ffn_w2_light, L0_ffn_b2_light, L0_ffn_gamma_heavy,
           L0_ffn_w1_heavy, L0_ffn_b1_heavy, L0_ffn_w2_heavy, L0_ffn_b2_heavy):
    x = np.ascontiguousarray(np.asarray(L0_attn_x, dtype=np.float32))
    shared = {
        "wqkv": np.ascontiguousarray(L0_attn_W_qkv, dtype=np.float32),
        "lng": np.ascontiguousarray(L0_attn_ln_gamma, dtype=np.float32),
        "lnb": np.ascontiguousarray(L0_attn_ln_beta, dtype=np.float32),
        "wproj": np.ascontiguousarray(L0_attn_W_proj, dtype=np.float32),
        "bproj": np.ascontiguousarray(L0_attn_b_proj, dtype=np.float32),
        "rt": np.ascontiguousarray(L0_ffn_routing_token, dtype=np.float32),
        "gl": np.ascontiguousarray(L0_ffn_gamma_light, dtype=np.float32),
        "w1l": np.ascontiguousarray(L0_ffn_w1_light, dtype=np.float32),
        "b1l": np.ascontiguousarray(L0_ffn_b1_light, dtype=np.float32),
        "w2l": np.ascontiguousarray(L0_ffn_w2_light, dtype=np.float32),
        "b2l": np.ascontiguousarray(L0_ffn_b2_light, dtype=np.float32),
        "gh": np.ascontiguousarray(L0_ffn_gamma_heavy, dtype=np.float32),
        "w1h": np.ascontiguousarray(L0_ffn_w1_heavy, dtype=np.float32),
        "b1h": np.ascontiguousarray(L0_ffn_b1_heavy, dtype=np.float32),
        "w2h": np.ascontiguousarray(L0_ffn_w2_heavy, dtype=np.float32),
        "b2h": np.ascontiguousarray(L0_ffn_b2_heavy, dtype=np.float32),
    }
    if not _NC_CACHE:
        _NC_CACHE.append(build_program())
    nc = _NC_CACHE[0]
    in_maps = [dict(shared, x=np.ascontiguousarray(x[b])) for b in range(8)]
    res = run_bass_kernel_spmd(nc, in_maps, core_ids=list(range(8)))
    return np.stack([np.asarray(r["out"]) for r in res.results], axis=0)


# version 17
# speedup vs baseline: 1.1520x; 1.0070x over previous
import numpy as np
from ml_dtypes import bfloat16 as _np_bf16
import concourse.bass as bass
import concourse.mybir as mybir
from concourse.bass_utils import run_bass_kernel_spmd
from contextlib import ExitStack

F32 = mybir.dt.float32
BF16 = mybir.dt.bfloat16
AF = mybir.ActivationFunctionType
ALU = mybir.AluOpType
AX = mybir.AxisListType

T = 2048
D = 1024
NT = 16
H = 16
HD = 64
LHID = 512
HHID = 4096
NHEAVY = 1024
EPS_LN = 1e-5
C0 = float(np.float32(0.1) * np.float32(np.log(8.0)))
T32 = -87.3365478515625
T64 = -745.2
N_CD = 50
N_BIS = 52


class Sem:
    def __init__(self, handle):
        self.h = handle
        self.n = 0

    def inc(self, instr, k=1):
        instr.then_inc(self.h, k)
        self.n += k
        return self.n


def build_program(dbg=False):
    nc = bass.Bass("TRN2", target_bir_lowering=False, detect_race_conditions=False)

    x_d = nc.dram_tensor("x", [T, D], F32, kind="ExternalInput").ap()
    wqkv_d = nc.dram_tensor("wqkv", [D, 3 * D], F32, kind="ExternalInput").ap()
    lng_d = nc.dram_tensor("lng", [HD], F32, kind="ExternalInput").ap()
    lnb_d = nc.dram_tensor("lnb", [HD], F32, kind="ExternalInput").ap()
    wproj_d = nc.dram_tensor("wproj", [D, D], F32, kind="ExternalInput").ap()
    bproj_d = nc.dram_tensor("bproj", [D], F32, kind="ExternalInput").ap()
    rt_d = nc.dram_tensor("rt", [D], F32, kind="ExternalInput").ap()
    gl_d = nc.dram_tensor("gl", [D], F32, kind="ExternalInput").ap()
    w1l_d = nc.dram_tensor("w1l", [D, LHID], F32, kind="ExternalInput").ap()
    b1l_d = nc.dram_tensor("b1l", [LHID], F32, kind="ExternalInput").ap()
    w2l_d = nc.dram_tensor("w2l", [LHID, D], F32, kind="ExternalInput").ap()
    b2l_d = nc.dram_tensor("b2l", [D], F32, kind="ExternalInput").ap()
    gh_d = nc.dram_tensor("gh", [D], F32, kind="ExternalInput").ap()
    w1h_d = nc.dram_tensor("w1h", [D, HHID], F32, kind="ExternalInput").ap()
    b1h_d = nc.dram_tensor("b1h", [HHID], F32, kind="ExternalInput").ap()
    w2h_d = nc.dram_tensor("w2h", [HHID, D], F32, kind="ExternalInput").ap()
    b2h_d = nc.dram_tensor("b2h", [D], F32, kind="ExternalInput").ap()
    out_d = nc.dram_tensor("out", [T, D], F32, kind="ExternalOutput").ap()
    if dbg:
        dbg_rt = nc.dram_tensor("dbg_rt", [128, NT * 4 + 8], F32, kind="ExternalOutput").ap()
        dbg_traj = nc.dram_tensor("dbg_traj", [1, 128], F32, kind="ExternalOutput").ap()
        dbg_attn = nc.dram_tensor("dbg_attn", [T, D], F32, kind="ExternalOutput").ap()
        dbg_light = nc.dram_tensor("dbg_light", [T, D], F32, kind="ExternalOutput").ap()
        dbg_nfm = nc.dram_tensor("dbg_nfm", [128, 8 * T], BF16, kind="ExternalOutput").ap()
        dbg_qkT = nc.dram_tensor("dbg_qkT", [128, 2 * D], BF16, kind="ExternalOutput").ap()
        dbg_A = nc.dram_tensor("dbg_A", [128, H * 128], BF16, kind="ExternalOutput").ap()
        dbg_ao = nc.dram_tensor("dbg_ao", [128, D], BF16, kind="ExternalOutput").ap()
        dbg_qh13 = nc.dram_tensor("dbg_qh13", [128, 2 * D], BF16, kind="ExternalOutput").ap()
        dbg_qkT13 = nc.dram_tensor("dbg_qkT13", [128, 2 * D], BF16, kind="ExternalOutput").ap()
        dbg_A13 = nc.dram_tensor("dbg_A13", [128, H * 128], BF16, kind="ExternalOutput").ap()
        dbg_AT13 = nc.dram_tensor("dbg_AT13", [128, H * 128], BF16, kind="ExternalOutput").ap()
        dbg_ao13 = nc.dram_tensor("dbg_ao13", [128, D], BF16, kind="ExternalOutput").ap()
        dbg_v13 = nc.dram_tensor("dbg_v13", [128, D], BF16, kind="ExternalOutput").ap()
    ds_s = nc.dram_tensor("ds_s", [128], F32, kind="Internal").ap()
    ds_c = nc.dram_tensor("ds_c", [NT], F32, kind="Internal").ap()

    Lmat = np.triu(np.ones((128, 128), dtype=np.float32))
    U16 = np.triu(np.ones((NT, NT), dtype=np.float32), 1)
    L_d = nc.inline_tensor(Lmat, name="Lmat")
    I_d = nc.inline_tensor(np.eye(128).astype(_np_bf16), name="I128")
    U_d = nc.inline_tensor(U16, name="U16")

    ctx = ExitStack()
    e = ctx.enter_context

    outacc = e(nc.sbuf_tensor("outacc", [128, NT * D], F32))
    nfm = e(nc.sbuf_tensor("nfm", [128, 8 * T], BF16))
    s_t = e(nc.sbuf_tensor("s_t", [128, NT], F32))
    invr_t = e(nc.sbuf_tensor("invr_t", [128, NT], F32))
    mask_t = e(nc.sbuf_tensor("mask_t", [128, NT], F32))
    key_t = e(nc.sbuf_tensor("key_t", [128, NT], F32))
    gl_sb = e(nc.sbuf_tensor("gl_sb", [128, 8], F32))
    gh_sb = e(nc.sbuf_tensor("gh_sb", [128, 8], F32))
    b1l_sb = e(nc.sbuf_tensor("b1l_sb", [128, 4], F32))
    b1h_sb = e(nc.sbuf_tensor("b1h_sb", [128, 8], F32))
    rows_b = e(nc.sbuf_tensor("rows_b", [1, 3 * D], BF16))
    lnq_g = e(nc.sbuf_tensor("lnq_g", [128, 1], F32))
    lnq_b = e(nc.sbuf_tensor("lnq_b", [128, 1], F32))
    lnk_g = e(nc.sbuf_tensor("lnk_g", [128, 1], F32))
    lnk_b = e(nc.sbuf_tensor("lnk_b", [128, 1], F32))
    rt_rep = e(nc.sbuf_tensor("rt_rep", [128, D], F32))
    rt_row = e(nc.sbuf_tensor("rt_row", [1, D], F32))
    L_sb = e(nc.sbuf_tensor("L_sb", [128, 128], F32))
    U_sb = e(nc.sbuf_tensor("U_sb", [NT, NT], F32))
    ones128c = e(nc.sbuf_tensor("ones128c", [128, 1], F32))
    ones1r = e(nc.sbuf_tensor("ones1r", [1, 128], F32))
    ones1r_b = e(nc.sbuf_tensor("ones1r_b", [1, 128], BF16))
    smax_b = e(nc.sbuf_tensor("smax_b", [128, 1], F32))
    Ab = e(nc.sbuf_tensor("Ab", [128, 1], F32))
    mmv = e(nc.sbuf_tensor("mmv", [128, 1], F32))
    negmm10 = e(nc.sbuf_tensor("negmm10", [128, 1], F32))
    mmc = e(nc.sbuf_tensor("mmc", [1, 1], F32))
    lnv = e(nc.sbuf_tensor("lnv", [1, 1], F32))
    Asc = e(nc.sbuf_tensor("Asc", [1, 1], F32))
    zt = e(nc.sbuf_tensor("zt", [128, NT], F32))
    rowsum = e(nc.sbuf_tensor("rowsum", [128, 1], F32))
    tr128 = e(nc.sbuf_tensor("tr128", [1, 128], F32))
    tr16b = e(nc.sbuf_tensor("tr16b", [NT, 1], F32))
    tr1 = e(nc.sbuf_tensor("tr1", [1, NT], F32))
    sm1 = e(nc.sbuf_tensor("sm1", [1, 1], F32))
    lo_t = e(nc.sbuf_tensor("lo_t", [1, 1], F32))
    hi_t = e(nc.sbuf_tensor("hi_t", [1, 1], F32))
    mid_t = e(nc.sbuf_tensor("mid_t", [1, 1], F32))
    midb = e(nc.sbuf_tensor("midb", [128, 1], F32))
    gam_t = e(nc.sbuf_tensor("gam_t", [1, 1], F32))
    tmp1 = e(nc.sbuf_tensor("tmp1", [1, 1], F32))
    tmp2 = e(nc.sbuf_tensor("tmp2", [1, 1], F32))
    tmp3 = e(nc.sbuf_tensor("tmp3", [1, 1], F32))
    tadp = e(nc.sbuf_tensor("tadp", [1, 1], F32))
    tadb = e(nc.sbuf_tensor("tadb", [128, 1], F32))
    lob = e(nc.sbuf_tensor("lob", [128, 1], F32))
    hib = e(nc.sbuf_tensor("hib", [128, 1], F32))
    nhi_t = e(nc.sbuf_tensor("nhi_t", [1, 1], F32))
    fcb = e(nc.sbuf_tensor("fcb", [128, 1], F32))
    cmpt = e(nc.sbuf_tensor("cmpt", [128, NT], F32))
    eqt = e(nc.sbuf_tensor("eqt", [128, NT], F32))
    colpre_sb = e(nc.sbuf_tensor("colpre_sb", [1, NT], F32))
    traj = e(nc.sbuf_tensor("traj", [1, 128], F32))
    stat_a = e(nc.sbuf_tensor("stat_a", [128, H], F32))
    stat_b = e(nc.sbuf_tensor("stat_b", [128, H], F32))
    stat_c = e(nc.sbuf_tensor("stat_c", [128, H], F32))
    stat_d = e(nc.sbuf_tensor("stat_d", [128, H], F32))
    stat_e = e(nc.sbuf_tensor("stat_e", [128, H], F32))
    stat_f = e(nc.sbuf_tensor("stat_f", [128, H], F32))
    stat_h = e(nc.sbuf_tensor("stat_h", [128, H], F32))
    col_a = e(nc.sbuf_tensor("col_a", [128, 1], F32))
    col_b = e(nc.sbuf_tensor("col_b", [128, 1], F32))
    col_c = e(nc.sbuf_tensor("col_c", [128, 1], F32))
    col_d = e(nc.sbuf_tensor("col_d", [128, 1], F32))
    col_e = e(nc.sbuf_tensor("col_e", [128, 1], F32))
    col_f = e(nc.sbuf_tensor("col_f", [128, 1], F32))
    col_g = e(nc.sbuf_tensor("col_g", [128, 1], F32))
    ca4 = e(nc.sbuf_tensor("ca4", [128, 4], F32))
    cb4 = e(nc.sbuf_tensor("cb4", [128, 4], F32))
    cc4 = e(nc.sbuf_tensor("cc4", [128, 4], F32))
    cd4 = e(nc.sbuf_tensor("cd4", [128, 4], F32))
    ce4 = e(nc.sbuf_tensor("ce4", [128, 4], F32))
    cf4 = e(nc.sbuf_tensor("cf4", [128, 4], F32))
    cg4 = e(nc.sbuf_tensor("cg4", [128, 4], F32))
    ch4 = e(nc.sbuf_tensor("ch4", [128, 4], F32))
    rsum_c = e(nc.sbuf_tensor("rsum_c", [128, 1], F32))
    nrsum_c = e(nc.sbuf_tensor("nrsum_c", [128, 1], F32))
    I_sb = e(nc.sbuf_tensor("I_sb", [128, 128], BF16))
    cbias = e(nc.sbuf_tensor("cbias", [128, 2], F32))
    nc.const_aps.register(F32, EPS_LN, cbias[:, 0:1])
    nc.const_aps.register(F32, -8.0, cbias[:, 1:2])
    xn_sc = e(nc.sbuf_tensor("xn_sc", [128, D], F32))
    qk_f32 = e(nc.sbuf_tensor("qk_f32", [128, D], F32))
    qkhat = e(nc.sbuf_tensor("qkhat", [128, 2 * D], BF16))
    ntm_sc = qkhat[:, 0:D]
    qkT = e(nc.sbuf_tensor("qkT", [128, 2 * D], BF16))
    v_bf = e(nc.sbuf_tensor("v_bf", [128, D], BF16))
    A_bf = e(nc.sbuf_tensor("A_bf", [128, H * 128], BF16))
    AT_bf = e(nc.sbuf_tensor("AT_bf", [128, H * 128], BF16))
    ao_fm = e(nc.sbuf_tensor("ao_fm", [128, D], BF16))
    sums16 = e(nc.sbuf_tensor("sums16", [128, H], F32))
    invs16 = e(nc.sbuf_tensor("invs16", [128, H], F32))
    h_sc = e(nc.sbuf_tensor("h_sc", [128, 256], BF16))

    pq = e(nc.psum_tensor("pq", [128, 512], F32))
    pq2 = e(nc.psum_tensor("pq2", [128, 512], F32))
    pSA = e(nc.psum_tensor("pSA", [128, 512], F32))
    po = [e(nc.psum_tensor(f"po{i}", [128, 512], F32)) for i in range(4)]
    pmisc = e(nc.psum_tensor("pmisc", [128, 512], F32))
    pS = pSA[:, 0:256]
    pAV = pSA[:, 256:512]
    pr = pmisc[:, 48:48 + NT]
    pb = pmisc[:, 0:1]
    pb2 = pmisc[:, 1:2]
    p1 = pmisc[0:1, 16:32]
    p2 = pmisc[0:1, 32:48]

    sW = Sem(e(nc.semaphore("sW")))
    sX = Sem(e(nc.semaphore("sX")))
    sT = Sem(e(nc.semaphore("sT")))
    sSt = Sem(e(nc.semaphore("sSt")))
    sV = Sem(e(nc.semaphore("sV")))
    sS = Sem(e(nc.semaphore("sS")))
    sM = Sem(e(nc.semaphore("sM")))
    sG = Sem(e(nc.semaphore("sG")))
    sI = Sem(e(nc.semaphore("sI")))
    sXg = [Sem(e(nc.semaphore(f"sXg{i}"))) for i in range(4)]
    sWB = Sem(e(nc.semaphore("sWB")))
    sWH = Sem(e(nc.semaphore("sWH")))

    def VF(v, ins):
        sV.inc(ins)
        v.wait_ge(sV.h, sV.n)

    blk = e(nc.Block())

    @blk.sync
    def _(sy):
        for dma in (
            lambda: sy.dma_start(out=L_sb[:], in_=L_d.ap()),
            lambda: sy.dma_start(out=U_sb[:], in_=U_d.ap()),
            lambda: sy.dma_start(out=rt_row[:], in_=rt_d[None, :]),
            lambda: sy.dma_start(out=col_e[0:64, :], in_=lng_d[:, None]),
            lambda: sy.dma_start(out=col_e[64:128, :], in_=lng_d[:, None]),
            lambda: sy.dma_start(out=col_f[0:64, :], in_=lnb_d[:, None]),
            lambda: sy.dma_start(out=col_f[64:128, :], in_=lnb_d[:, None]),
            lambda: sy.dma_start(out=gl_sb[:], in_=gl_d.rearrange("(c p) -> p c", p=128), allow_slow_non_contiguous=True),
            lambda: sy.dma_start(out=gh_sb[:], in_=gh_d.rearrange("(c p) -> p c", p=128), allow_slow_non_contiguous=True),
            lambda: sy.dma_start(out=b1l_sb[:], in_=b1l_d.rearrange("(c p) -> p c", p=128), allow_slow_non_contiguous=True),
        ):
            sX.inc(dma(), 16)
        for gi in range(2):
            for j in range(gi * 4, gi * 4 + 4):
                sXg[gi].inc(sy.dma_start(out=outacc[:, j * D:(j + 1) * D],
                                         in_=x_d[j * 128:(j + 1) * 128, :]), 16)

    @blk.scalar
    def _(sc):
        for gi in range(2, 4):
            for j in range(gi * 4, gi * 4 + 4):
                sXg[gi].inc(sc.dma_start(out=outacc[:, j * D:(j + 1) * D],
                                         in_=x_d[j * 128:(j + 1) * 128, :]), 16)
    X_CONST = 16 * 10
    X_ALL = X_CONST

    @blk.gpsimd
    def _(g):
        sWB.inc(g.dma_start(out=rows_b[:, 0:D], in_=bproj_d[None, :]), 16)
        sWB.inc(g.dma_start(out=rows_b[:, D:2 * D], in_=b2l_d[None, :]), 16)
        sWB.inc(g.dma_start(out=rows_b[:, 2 * D:3 * D], in_=b2h_d[None, :]), 16)
        sG.inc(g.memset(cbias[:, 0:1], EPS_LN))
        sG.inc(g.memset(cbias[:, 1:2], -8.0))
        sG.inc(g.memset(ones128c[:], 1.0))
        sG.inc(g.memset(ones1r[:], 1.0))
        sG.inc(g.memset(ones1r_b[:], 1.0))
        sG.inc(g.memset(Ab[:], 0.0))
        sG.inc(g.memzero(A_bf[:]))
        sI.inc(g.dma_start(out=I_sb[:], in_=I_d.ap()), 16)
    W_SMALL = 16 * 3
    G_CONST = 7

    wctx = ExitStack()
    we = wctx.enter_context
    wqkv_sb = we(nc.sbuf_tensor("wqkv_sb", [128, 8 * 3 * D], BF16))
    wproj_sb = we(nc.sbuf_tensor("wproj_sb", [128, 8 * D], BF16))

    @blk.gpsimd
    def _(g):
        for kc in range(8):
            sW.inc(g.dma_start(out=wqkv_sb[:, kc * 3 * D:(kc + 1) * 3 * D],
                               in_=wqkv_d[kc * 128:(kc + 1) * 128, :]), 16)
        for kc in range(8):
            sW.inc(g.dma_start(out=wproj_sb[:, kc * D:(kc + 1) * D],
                               in_=wproj_d[kc * 128:(kc + 1) * 128, :]), 16)
    W_ATTN = sW.n

    @blk.vector
    def _(v):
        v.wait_ge(sX.h, X_CONST)
        v.wait_ge(sWB.h, W_SMALL)
        v.tensor_scalar(out=lnq_g[:], in0=col_e[:], scalar1=0.125, scalar2=None, op0=ALU.mult)
        v.tensor_scalar(out=lnq_b[:], in0=col_f[:], scalar1=0.125, scalar2=None, op0=ALU.mult)
        v.tensor_copy(out=lnk_g[:], in_=col_e[:])
        sV.inc(v.tensor_copy(out=lnk_b[:], in_=col_f[:]))

    @blk.tensor
    def _(t):
        t.wait_ge(sG.h, G_CONST)
        t.wait_ge(sX.h, X_CONST)
        t.matmul(pq[:], ones1r[:], rt_row[:, 0:512], start=True, stop=True)
        sM.inc(t.matmul(pq2[:], ones1r[:], rt_row[:, 512:1024], start=True, stop=True))

    @blk.vector
    def _(v):
        v.wait_ge(sM.h, sM.n)
        v.tensor_copy(out=rt_rep[:, 0:512], in_=pq[:])
        sV.inc(v.tensor_copy(out=rt_rep[:, 512:1024], in_=pq2[:]))
    V_RT = sV.n

    ln_c = 1.0 / D

    @blk.vector
    def _(v):
        v.wait_ge(sX.h, X_CONST)
        v.wait_ge(sV.h, V_RT)
        v.wait_ge(sG.h, G_CONST)
        v.wait_ge(sI.h, 16)
        VF(v, v.reduce_sum(out=rsum_c[:], in_=rt_rep[:], axis=AX.X))
        sV.inc(v.tensor_scalar(out=nrsum_c[:], in0=rsum_c[:], scalar1=-1.0, scalar2=None,
                               op0=ALU.mult))

    a_ev = {}
    a_tp = {}
    tp_banks = [(pq, pq2), (pSA, po[0])]
    for gi in range(4):
        j0 = gi * 4

        @blk.vector
        def _(v, j0=j0, gi=gi):
            v.wait_ge(sXg[gi].h, 64)
            for jj in range(4):
                base = (j0 + jj) * D
                xa = outacc[:, base:base + D]
                v.tensor_mul(out=qk_f32[:, 0:512], in0=outacc[:, base:base + 512],
                             in1=rt_rep[:, 0:512])
                v.reduce_sum(out=ca4[:, jj:jj + 1], in_=xa, axis=AX.X)
                v.tensor_mul(out=qk_f32[:, 512:1024], in0=outacc[:, base + 512:base + D],
                             in1=rt_rep[:, 512:1024])
                last = v.reduce_sum(out=ch4[:, jj:jj + 1], in_=qk_f32[:], axis=AX.X)
            VF(v, last)
            sV.inc(v.engine_nop())

        @blk.scalar
        def _(sc, j0=j0):
            sc.wait_ge(sV.h, sV.n)
            for jj in range(4):
                base = (j0 + jj) * D
                last = sc.activation(out=xn_sc[:], in_=outacc[:, base:base + D],
                                     func=AF.Square, accum_out=cb4[:, jj:jj + 1])
            sS.inc(last)

        @blk.vector
        def _(v):
            v.wait_ge(sS.h, sS.n)
            VF(v, v.tensor_scalar(out=cc4[:], in0=ca4[:], scalar1=ln_c, scalar2=None,
                                  op0=ALU.mult))
            VF(v, v.tensor_mul(out=cd4[:], in0=cc4[:], in1=cc4[:]))
            sV.inc(v.scalar_tensor_tensor(out=cb4[:], in0=cb4[:], scalar=ln_c,
                                          in1=cd4[:], op0=ALU.mult, op1=ALU.subtract))

        @blk.scalar
        def _(sc):
            sc.wait_ge(sV.h, sV.n)
            sS.inc(sc.activation(out=cd4[:], in_=cb4[:], func=AF.Sqrt, bias=EPS_LN))

        @blk.vector
        def _(v):
            v.wait_ge(sS.h, sS.n)
            VF(v, v.reciprocal(out=ce4[:], in_=cd4[:]))
            VF(v, v.tensor_mul(out=cf4[:], in0=cb4[:], in1=ce4[:]))
            sV.inc(v.tensor_mul(out=cf4[:], in0=cf4[:], in1=ce4[:]))

        @blk.scalar
        def _(sc):
            sc.wait_ge(sV.h, sV.n)
            sS.inc(sc.activation(out=cf4[:], in_=cf4[:], func=AF.Sqrt))

        @blk.vector
        def _(v, j0=j0):
            v.wait_ge(sS.h, sS.n)
            VF(v, v.tensor_scalar_max(cf4[:], cf4[:], 1e-12 / 32.0))
            v.tensor_copy(out=invr_t[:, j0:j0 + 4], in_=cf4[:])
            VF(v, v.reciprocal(out=cg4[:], in_=cf4[:]))
            VF(v, v.tensor_mul(out=cg4[:], in0=cg4[:], in1=ce4[:]))
            VF(v, v.scalar_tensor_tensor(out=cd4[:], in0=cc4[:], scalar=nrsum_c[:],
                                         in1=ch4[:], op0=ALU.mult, op1=ALU.add))
            sV.inc(v.tensor_mul(out=s_t[:, j0:j0 + 4], in0=cd4[:], in1=ce4[:]))

        for jj in range(4):
            j = j0 + jj

            @blk.vector
            def _(v, j=j, jj=jj):
                if j >= 2 and (j - 2) in a_tp:
                    v.wait_ge(sM.h, a_tp[j - 2])
                base = j * D
                sV.inc(v.tensor_scalar(out=qkhat[:, (j % 2) * D:(j % 2) * D + D],
                                       in0=outacc[:, base:base + D],
                                       scalar1=cc4[:, jj:jj + 1], scalar2=cg4[:, jj:jj + 1],
                                       op0=ALU.subtract, op1=ALU.mult))

            @blk.tensor
            def _(t, j=j):
                t.wait_ge(sV.h, sV.n)
                if j >= 2 and (j - 2) in a_ev:
                    t.wait_ge(sS.h, a_ev[j - 2])
                b0, b1 = tp_banks[j % 2]
                for kc in range(8):
                    bank = b0 if kc < 4 else b1
                    qtr = (kc % 4) * 128
                    mm = t.matmul(bank[:, qtr:qtr + 128],
                                  qkhat[(0):(128), (j % 2) * D + kc * 128:(j % 2) * D + kc * 128 + 128],
                                  I_sb[:], start=True, stop=True)
                sM.inc(mm)
            a_tp[j] = sM.n

            @blk.scalar
            def _(sc, j=j):
                sc.wait_ge(sM.h, a_tp[j])
                b0, b1 = tp_banks[j % 2]
                for kc in range(8):
                    bank = b0 if kc < 4 else b1
                    qtr = (kc % 4) * 128
                    last = sc.activation(out=nfm[:, kc * T + j * 128:kc * T + j * 128 + 128],
                                         in_=bank[:, qtr:qtr + 128], func=AF.Copy)
                sS.inc(last)
            a_ev[j] = sS.n
    NFM_S = sS.n
    T_NFM = sT.n
    V_PHA = sV.n

    RQ = []

    def r_prep():
        @blk.vector
        def _(v):
            v.wait_ge(sV.h, V_PHA)
            sV.inc(v.reduce_max(out=col_a[:], in_=s_t[:], axis=AX.X))

        @blk.sync
        def _(sy):
            sy.wait_ge(sV.h, sV.n)
            sX.inc(sy.dma_start(out=ds_s[:, None], in_=col_a[:]), 16)
            sy.wait_ge(sX.h, sX.n)
            sX.inc(sy.dma_start(out=tr128[:], in_=ds_s[None, :]), 16)

        @blk.vector
        def _(v):
            v.wait_ge(sX.h, sX.n)
            VF(v, v.reduce_max(out=sm1[:], in_=tr128[:], axis=AX.X))
            VF(v, v.memset(tmp1[:], 1.0))
            VF(v, v.tensor_scalar(out=tmp2[:], in0=tmp1[:], scalar1=float(2.0 ** -30),
                                  scalar2=None, op0=ALU.add))
            VF(v, v.tensor_sub(out=tmp2[:], in0=tmp2[:], in1=tmp1[:]))
            VF(v, v.tensor_scalar(out=gam_t[:], in0=tmp2[:], scalar1=1e-15, scalar2=None,
                                  op0=ALU.is_gt))
            VF(v, v.memset(tmp1[:], T32))
            VF(v, v.scalar_tensor_tensor(out=tadp[:], in0=gam_t[:], scalar=float(T64 - T32),
                                         in1=tmp1[:], op0=ALU.mult, op1=ALU.add))

        @blk.tensor
        def _(t):
            t.wait_ge(sV.h, sV.n)
            t.matmul(pb[:], ones1r[:], sm1[:], start=True, stop=True)
            sM.inc(t.matmul(pb2[:], ones1r[:], tadp[:], start=True, stop=True))

        @blk.vector
        def _(v):
            v.wait_ge(sM.h, sM.n)
            v.wait_ge(sG.h, G_CONST)
            VF(v, v.tensor_copy(out=smax_b[:], in_=pb[:]))
            VF(v, v.tensor_copy(out=tadb[:], in_=pb2[:]))
    RQ.append(r_prep)

    def r_cd_u1(it):
        def f():
            @blk.vector
            def _(v):
                if it > 0:
                    v.wait_ge(sM.h, sM.n)
                    VF(v, v.tensor_copy(out=Ab[:], in_=pb[:]))
                VF(v, v.tensor_tensor(out=mmv[:], in0=smax_b[:], in1=Ab[:], op=ALU.min))
                v.tensor_scalar(out=negmm10[:], in0=mmv[:], scalar1=-10.0, scalar2=None,
                                op0=ALU.mult)
                VF(v, v.tensor_scalar(out=mmc[:], in0=mmv[0:1, :], scalar1=C0, scalar2=None,
                                      op0=ALU.subtract))
                sV.inc(v.tensor_scalar(out=zt[:], in0=s_t[:], scalar1=Ab[:], scalar2=None,
                                       op0=ALU.min))

            @blk.scalar
            def _(ss):
                ss.wait_ge(sV.h, sV.n)
                sS.inc(ss.activation(out=cmpt[:], in_=zt[:], func=AF.Exp, bias=negmm10[:],
                                     scale=10.0, accum_out=rowsum[:]))
        return f

    def r_cd_u2():
        @blk.tensor
        def _(t):
            t.wait_ge(sS.h, sS.n)
            sM.inc(t.matmul(p1[0:1, 0:1], rowsum[:], ones128c[:], start=True, stop=True))

        @blk.scalar
        def _(ss):
            ss.wait_ge(sM.h, sM.n)
            sS.inc(ss.activation(out=lnv[:], in_=p1[0:1, 0:1], func=AF.Ln))

        @blk.vector
        def _(v):
            v.wait_ge(sS.h, sS.n)
            VF(v, v.scalar_tensor_tensor(out=Asc[:], in0=lnv[:], scalar=0.1,
                                         in1=mmc[:], op0=ALU.mult, op1=ALU.add))

    def r_cd_u3():
        @blk.tensor
        def _(t):
            t.wait_ge(sV.h, sV.n)
            sM.inc(t.matmul(pb[:], ones1r[:], Asc[:], start=True, stop=True))

    for it in range(N_CD):
        RQ.append(r_cd_u1(it))
        RQ.append(r_cd_u2)
        RQ.append(r_cd_u3)

    def r_key():
        @blk.vector
        def _(v):
            v.wait_ge(sM.h, sM.n)
            VF(v, v.tensor_copy(out=Ab[:], in_=pb[:]))
            VF(v, v.memset(hi_t[:], 1.0))
            VF(v, v.memset(mid_t[:], 0.0))
            VF(v, v.tensor_scalar(out=zt[:], in0=s_t[:], scalar1=Ab[:], scalar2=0.0,
                                  op0=ALU.subtract, op1=ALU.min))
            VF(v, v.tensor_scalar(out=key_t[:], in0=zt[:], scalar1=10.0, scalar2=None,
                                  op0=ALU.mult))
            VF(v, v.tensor_scalar(out=key_t[:], in0=key_t[:], scalar1=tadb[:], scalar2=None,
                                  op0=ALU.max))
            VF(v, v.tensor_scalar(out=lo_t[:], in0=tadp[:], scalar1=1.0, scalar2=None,
                                  op0=ALU.subtract))
            VF(v, v.tensor_add(out=mid_t[:], in0=lo_t[:], in1=hi_t[:]))
            VF(v, v.tensor_scalar(out=mid_t[:], in0=mid_t[:], scalar1=0.5, scalar2=None,
                                  op0=ALU.mult))
    RQ.append(r_key)

    def r_bis_u1():
        @blk.tensor
        def _(t):
            t.wait_ge(sV.h, sV.n)
            sM.inc(t.matmul(pb[:], ones1r[:], mid_t[:], start=True, stop=True))

        @blk.vector
        def _(v):
            v.wait_ge(sM.h, sM.n)
            VF(v, v.tensor_copy(out=midb[:], in_=pb[:]))
            VF(v, v.tensor_scalar(out=cmpt[:], in0=key_t[:], scalar1=midb[:], scalar2=None,
                                  op0=ALU.is_gt))
            sV.inc(v.reduce_sum(out=rowsum[:], in_=cmpt[:], axis=AX.X))

    def r_bis_u2():
        @blk.tensor
        def _(t):
            t.wait_ge(sV.h, sV.n)
            sM.inc(t.matmul(p1[0:1, 0:1], rowsum[:], ones128c[:], start=True, stop=True))

        @blk.vector
        def _(v):
            v.wait_ge(sM.h, sM.n)
            VF(v, v.tensor_scalar(out=gam_t[:], in0=p1[0:1, 0:1], scalar1=float(NHEAVY),
                                  scalar2=None, op0=ALU.is_ge))
            VF(v, v.tensor_sub(out=tmp1[:], in0=mid_t[:], in1=lo_t[:]))
            VF(v, v.tensor_mul(out=tmp1[:], in0=tmp1[:], in1=gam_t[:]))
            VF(v, v.tensor_add(out=lo_t[:], in0=lo_t[:], in1=tmp1[:]))
            VF(v, v.tensor_sub(out=tmp2[:], in0=hi_t[:], in1=mid_t[:]))
            VF(v, v.tensor_scalar(out=tmp3[:], in0=gam_t[:], scalar1=-1.0, scalar2=1.0,
                                  op0=ALU.mult, op1=ALU.add))
            VF(v, v.tensor_mul(out=tmp2[:], in0=tmp2[:], in1=tmp3[:]))
            VF(v, v.tensor_sub(out=hi_t[:], in0=hi_t[:], in1=tmp2[:]))
            VF(v, v.tensor_add(out=mid_t[:], in0=lo_t[:], in1=hi_t[:]))
            VF(v, v.tensor_scalar(out=mid_t[:], in0=mid_t[:], scalar1=0.5, scalar2=None,
                                  op0=ALU.mult))

    for it in range(N_BIS):
        RQ.append(r_bis_u1)
        RQ.append(r_bis_u2)

    def r_mask1():
        @blk.tensor
        def _(t):
            t.wait_ge(sV.h, sV.n)
            t.matmul(pb[:], ones1r[:], lo_t[:], start=True, stop=True)
            sM.inc(t.matmul(pb2[:], ones1r[:], hi_t[:], start=True, stop=True))

        @blk.vector
        def _(v):
            v.wait_ge(sM.h, sM.n)
            VF(v, v.tensor_copy(out=lob[:], in_=pb[:]))
            VF(v, v.tensor_copy(out=hib[:], in_=pb2[:]))
            VF(v, v.tensor_scalar(out=cmpt[:], in0=key_t[:], scalar1=hib[:], scalar2=None,
                                  op0=ALU.is_gt))
            VF(v, v.reduce_sum(out=rowsum[:], in_=cmpt[:], axis=AX.X))
            VF(v, v.tensor_scalar(out=eqt[:], in0=key_t[:], scalar1=lob[:], scalar2=None,
                                  op0=ALU.is_gt))
            VF(v, v.tensor_scalar(out=zt[:], in0=key_t[:], scalar1=hib[:], scalar2=None,
                                  op0=ALU.is_le))
            sV.inc(v.tensor_mul(out=eqt[:], in0=eqt[:], in1=zt[:]))
    RQ.append(r_mask1)

    def r_mask2():
        @blk.tensor
        def _(t):
            t.wait_ge(sV.h, sV.n)
            t.matmul(p1[0:1, 0:1], rowsum[:], ones128c[:], start=True, stop=True)
            sM.inc(t.matmul(p2, ones128c[:], eqt[:], start=True, stop=True))

        @blk.vector
        def _(v):
            v.wait_ge(sM.h, sM.n)
            VF(v, v.tensor_copy(out=nhi_t[:], in_=p1[0:1, 0:1]))
            VF(v, v.tensor_copy(out=tr1[:], in_=p2))

        @blk.sync
        def _(sy):
            sy.wait_ge(sV.h, sV.n)
            sX.inc(sy.dma_start(out=ds_c[None, :], in_=tr1[0:1, :]), 16)
            sy.wait_ge(sX.h, sX.n)
            sX.inc(sy.dma_start(out=tr16b[:], in_=ds_c[:, None]), 16)
    RQ.append(r_mask2)

    def r_mask3():
        @blk.tensor
        def _(t):
            t.wait_ge(sX.h, sX.n)
            sM.inc(t.matmul(p1[0:1, 0:NT], tr16b[:], U_sb[:], start=True, stop=True))

        @blk.vector
        def _(v):
            v.wait_ge(sM.h, sM.n)
            VF(v, v.tensor_copy(out=colpre_sb[:], in_=p1[0:1, 0:NT]))
            VF(v, v.tensor_scalar(out=nhi_t[:], in0=nhi_t[:], scalar1=-1.0,
                                  scalar2=float(NHEAVY), op0=ALU.mult, op1=ALU.add))

        @blk.tensor
        def _(t):
            t.wait_ge(sV.h, sV.n)
            t.matmul(pr[:], L_sb[:], eqt[:], start=True, stop=False)
            t.matmul(pr[:], ones1r[:], colpre_sb[:], start=False, stop=True)
            sM.inc(t.matmul(pb[:], ones1r[:], nhi_t[:], start=True, stop=True))

        @blk.vector
        def _(v):
            v.wait_ge(sM.h, sM.n)
            VF(v, v.tensor_copy(out=fcb[:], in_=pb[:]))
            VF(v, v.tensor_scalar(out=zt[:], in0=pr[:], scalar1=fcb[:], scalar2=None,
                                  op0=ALU.is_le))
            VF(v, v.tensor_mul(out=zt[:], in0=zt[:], in1=eqt[:]))
            sV.inc(v.tensor_add(out=mask_t[:], in0=cmpt[:], in1=zt[:]))
    RQ.append(r_mask3)

    def drain_routing(n):
        import os as _os2
        if _os2.environ.get("KDRAIN_MODE", "ileave") == "skip":
            for _ in range(n):
                if RQ:
                    RQ.pop(0)
            return
        for _ in range(n):
            if RQ:
                RQ.pop(0)()

    import os as _os
    _KM = _os.environ.get("KDRAIN_MODE", "ileave")
    if _KM == "all":
        drain_routing(len(RQ))
    elif _KM.isdigit():
        drain_routing(int(_KM))

    def _dr(kind, n):
        if _KM == "skip":
            for _ in range(n):
                if RQ:
                    RQ.pop(0)
            return
        if _KM == "ileave":
            if kind != "head":
                drain_routing(n)
        elif _KM == kind:
            drain_routing(n)


    fb = [pq, pq2, pSA]
    bb = [po[0], po[1], po[2], po[3]]
    PM = {}

    def f_w1(j):
        @blk.tensor
        def _(t, j=j):
            if j == 0:
                t.wait_ge(sS.h, NFM_S)
                t.wait_ge(sW.h, W_ATTN)
                t.wait_ge(sWB.h, W_SMALL)
                t.wait_ge(sV.h, V_RT)
            else:
                t.wait_ge(sV.h, PM[('gbB', j - 1)])
            for og in range(3):
                for kc in range(8):
                    mm = t.matmul(fb[og][:],
                                  nfm[:, kc * T + j * 128:kc * T + j * 128 + 128],
                                  wqkv_sb[:, kc * 3 * D + og * 512:kc * 3 * D + og * 512 + 512],
                                  start=(kc == 0), stop=(kc == 7))
            sM.inc(mm)
        PM[('w1', j)] = sM.n

    def f_ev1(j):
        @blk.vector
        def _(v, j=j):
            v.wait_ge(sM.h, PM[('w1', j)])
            iv = invr_t[:, j:j + 1]
            v.tensor_scalar(out=qk_f32[:, 0:512], in0=pq[:], scalar1=iv, scalar2=None, op0=ALU.mult)
            v.tensor_scalar(out=qk_f32[:, 512:1024], in0=pq2[:], scalar1=iv, scalar2=None, op0=ALU.mult)
            v.reduce_sum(out=stat_a[:], in_=qk_f32.rearrange("p (s k) -> p s k", k=HD), axis=AX.X)
            sV.inc(v.tensor_scalar(out=xn_sc[:, 0:512], in0=pSA[:], scalar1=iv, scalar2=None,
                                   op0=ALU.mult))
        PM[('ev1', j)] = sV.n

    def f_w2(j):
        @blk.tensor
        def _(t, j=j):
            t.wait_ge(sV.h, PM[('ev1', j)])
            for og in range(3, 6):
                for kc in range(8):
                    mm = t.matmul(fb[og - 3][:],
                                  nfm[:, kc * T + j * 128:kc * T + j * 128 + 128],
                                  wqkv_sb[:, kc * 3 * D + og * 512:kc * 3 * D + og * 512 + 512],
                                  start=(kc == 0), stop=(kc == 7))
            sM.inc(mm)
        PM[('w2', j)] = sM.n

    def f_ev2a(j):
        @blk.vector
        def _(v, j=j):
            v.wait_ge(sM.h, PM[('w2', j)])
            iv = invr_t[:, j:j + 1]
            v.tensor_scalar(out=xn_sc[:, 512:1024], in0=pq[:], scalar1=iv, scalar2=None,
                            op0=ALU.mult)
            v.reduce_sum(out=stat_e[:], in_=xn_sc.rearrange("p (s k) -> p s k", k=HD),
                         axis=AX.X)
            v.tensor_mul(out=pq[:], in0=qk_f32[:, 0:512], in1=qk_f32[:, 0:512])
            v.reduce_sum(out=stat_b[:, 0:8], in_=pq.rearrange("p (s k) -> p s k", k=HD), axis=AX.X)
            v.tensor_mul(out=pq[:], in0=qk_f32[:, 512:1024], in1=qk_f32[:, 512:1024])
            v.reduce_sum(out=stat_b[:, 8:16], in_=pq.rearrange("p (s k) -> p s k", k=HD), axis=AX.X)
            v.tensor_mul(out=pq[:], in0=xn_sc[:, 0:512], in1=xn_sc[:, 0:512])
            v.reduce_sum(out=stat_f[:, 0:8], in_=pq.rearrange("p (s k) -> p s k", k=HD), axis=AX.X)
            v.tensor_mul(out=pq[:], in0=xn_sc[:, 512:1024], in1=xn_sc[:, 512:1024])
            VF(v, v.reduce_sum(out=stat_f[:, 8:16], in_=pq.rearrange("p (s k) -> p s k", k=HD),
                               axis=AX.X))
            sV.inc(v.engine_nop())
        PM[('ev2a', j)] = sV.n

    def f_muvar(j):
        @blk.vector
        def _(v, j=j):
            VF(v, v.tensor_scalar(out=stat_a[:], in0=stat_a[:], scalar1=1.0 / HD, scalar2=None,
                                  op0=ALU.mult))
            VF(v, v.tensor_scalar(out=stat_e[:], in0=stat_e[:], scalar1=1.0 / HD, scalar2=None,
                                  op0=ALU.mult))
            VF(v, v.tensor_mul(out=stat_c[:], in0=stat_a[:], in1=stat_a[:]))
            VF(v, v.scalar_tensor_tensor(out=stat_b[:], in0=stat_b[:], scalar=1.0 / HD,
                                         in1=stat_c[:], op0=ALU.mult, op1=ALU.subtract))
            VF(v, v.tensor_mul(out=stat_c[:], in0=stat_e[:], in1=stat_e[:]))
            sV.inc(v.scalar_tensor_tensor(out=stat_f[:], in0=stat_f[:], scalar=1.0 / HD,
                                          in1=stat_c[:], op0=ALU.mult, op1=ALU.subtract))
        PM[('muvar', j)] = sV.n

    def f_sd(j):
        @blk.scalar
        def _(sc, j=j):
            sc.wait_ge(sV.h, PM[('muvar', j)])
            sc.activation(out=stat_b[:], in_=stat_b[:], func=AF.Sqrt, bias=EPS_LN)
            sS.inc(sc.activation(out=stat_f[:], in_=stat_f[:], func=AF.Sqrt, bias=EPS_LN))
        PM[('sd', j)] = sS.n

    def f_hat(j):
        @blk.vector
        def _(v, j=j):
            v.wait_ge(sS.h, PM[('sd', j)])
            if j >= 1:
                v.wait_ge(sM.h, PM[('tpB', j - 1)])
            VF(v, v.reciprocal(out=stat_d[:], in_=stat_b[:]))
            VF(v, v.reciprocal(out=stat_h[:], in_=stat_f[:]))
            for h in range(H):
                v.tensor_scalar(out=qkhat[:, h * HD:(h + 1) * HD],
                                in0=qk_f32[:, h * HD:(h + 1) * HD],
                                scalar1=stat_a[:, h:h + 1], scalar2=stat_d[:, h:h + 1],
                                op0=ALU.subtract, op1=ALU.mult)
            for h in range(H):
                last = v.tensor_scalar(out=qkhat[:, D + h * HD:D + (h + 1) * HD],
                                       in0=xn_sc[:, h * HD:(h + 1) * HD],
                                       scalar1=stat_e[:, h:h + 1], scalar2=stat_h[:, h:h + 1],
                                       op0=ALU.subtract, op1=ALU.mult)
            sV.inc(last)
        PM[('hat', j)] = sV.n

    def f_ev2v(j):
        @blk.vector
        def _(v, j=j):
            if j >= 1:
                v.wait_ge(sM.h, PM[('av', j - 1)])
            iv = invr_t[:, j:j + 1]
            v.tensor_scalar(out=v_bf[:, 0:512], in0=pq2[:], scalar1=iv, scalar2=None, op0=ALU.mult)
            sV.inc(v.tensor_scalar(out=v_bf[:, 512:1024], in0=pSA[:], scalar1=iv, scalar2=None,
                                   op0=ALU.mult))
        PM[('ev2v', j)] = sV.n

    def f_tpA(j):
        @blk.tensor
        def _(t, j=j):
            t.wait_ge(sV.h, PM[('ev2v', j)])
            for c in range(12):
                mm = t.matmul(fb[c % 3][:, (c // 3) * 128:(c // 3) * 128 + 128],
                              qkhat[:, c * 128:(c + 1) * 128], I_sb[:],
                              start=True, stop=True)
            sM.inc(mm)
        PM[('tpA', j)] = sM.n

    def f_gbA(j):
        @blk.vector
        def _(v, j=j):
            v.wait_ge(sM.h, PM[('tpA', j)])
            for c in range(12):
                g_ap = lnq_g[:] if c < 8 else lnk_g[:]
                b_ap = lnq_b[:] if c < 8 else lnk_b[:]
                last = v.tensor_scalar(out=qkT[:, c * 128:(c + 1) * 128],
                                       in0=fb[c % 3][:, (c // 3) * 128:(c // 3) * 128 + 128],
                                       scalar1=g_ap, scalar2=b_ap, op0=ALU.mult, op1=ALU.add)
            sV.inc(last)
        PM[('gbA', j)] = sV.n

    def f_tpB(j):
        @blk.tensor
        def _(t, j=j):
            t.wait_ge(sV.h, PM[('gbA', j)])
            for c in range(12, 16):
                mm = t.matmul(pq[:, (c - 12) * 128:(c - 12) * 128 + 128],
                              qkhat[:, c * 128:(c + 1) * 128], I_sb[:],
                              start=True, stop=True)
            sM.inc(mm)
        PM[('tpB', j)] = sM.n

    def f_gbB(j):
        @blk.vector
        def _(v, j=j):
            v.wait_ge(sM.h, PM[('tpB', j)])
            for c in range(12, 16):
                last = v.tensor_scalar(out=qkT[:, c * 128:(c + 1) * 128],
                                       in0=pq[:, (c - 12) * 128:(c - 12) * 128 + 128],
                                       scalar1=lnk_g[:], scalar2=lnk_b[:],
                                       op0=ALU.mult, op1=ALU.add)
            sV.inc(last)
        PM[('gbB', j)] = sV.n

    def b_smm(j):
        for h in range(H):
            @blk.tensor
            def _(t, j=j, h=h):
                if h == 0:
                    t.wait_ge(sV.h, PM[('gbB', j)])
                    if j >= 1:
                        t.wait_ge(sV.h, PM[('add', j - 1)])
                c = h // 2
                r0 = (h % 2) * HD
                sM.inc(t.matmul(bb[h // 4][:, (h % 4) * 128:(h % 4) * 128 + 128],
                                qkT[r0:r0 + HD, c * 128:(c + 1) * 128],
                                qkT[r0:r0 + HD, D + c * 128:D + (c + 1) * 128],
                                start=True, stop=True))
            if h == 7:
                PM[('smm1', j)] = sM.n
        PM[('smm', j)] = sM.n

    def b_exp(j, w):
        @blk.scalar
        def _(sc, j=j, w=w):
            sc.wait_ge(sM.h, PM[('smm1', j)] if w == 0 else PM[('smm', j)])
            for h in range(8 * w, 8 * w + 8):
                bank = bb[h // 4]
                qtr = (h % 4) * 128
                sc.activation(out=A_bf[0:64, h * 128:h * 128 + 64],
                              in_=bank[0:64, qtr:qtr + 64], func=AF.Exp, bias=-8.0)
                last = sc.activation(out=A_bf[64:128, h * 128 + 64:h * 128 + 128],
                                     in_=bank[64:128, qtr + 64:qtr + 128], func=AF.Exp,
                                     bias=-8.0)
            sS.inc(last)
        PM[('exp', j, w)] = sS.n

    def b_norm(j, w):
        @blk.vector
        def _(v, j=j, w=w):
            v.wait_ge(sS.h, PM[('exp', j, w)])
            VF(v, v.reduce_sum(out=sums16[:, 8 * w:8 * w + 8],
                               in_=A_bf[:, 1024 * w:1024 * w + 1024].rearrange(
                                   "p (s k) -> p s k", k=128),
                               axis=AX.X))
            VF(v, v.reciprocal(out=invs16[:, 8 * w:8 * w + 8], in_=sums16[:, 8 * w:8 * w + 8]))
            for h in range(8 * w, 8 * w + 8):
                last = v.tensor_scalar(out=A_bf[:, h * 128:(h + 1) * 128],
                                       in0=A_bf[:, h * 128:(h + 1) * 128],
                                       scalar1=invs16[:, h:h + 1], scalar2=None, op0=ALU.mult)
            sV.inc(last)
        PM[('norm', j, w)] = sV.n

    def b_at(j, w):
        @blk.tensor
        def _(t, j=j, w=w):
            t.wait_ge(sV.h, PM[('norm', j, w)])
            for h in range(8 * w, 8 * w + 8):
                mm = t.matmul(bb[2 * w + h % 2][:, ((h - 8 * w) // 2) * 128:((h - 8 * w) // 2) * 128 + 128],
                              A_bf[:, h * 128:(h + 1) * 128], I_sb[:],
                              start=True, stop=True)
            sM.inc(mm)
        PM[('at', j, w)] = sM.n

    def b_atev(j, w):
        @blk.vector
        def _(v, j=j, w=w):
            v.wait_ge(sM.h, PM[('at', j, w)])
            for h in range(8 * w, 8 * w + 8):
                last = v.tensor_copy(out=AT_bf[:, h * 128:(h + 1) * 128],
                                     in_=bb[2 * w + h % 2][:, ((h - 8 * w) // 2) * 128:((h - 8 * w) // 2) * 128 + 128])
            sV.inc(last)
        PM[('atev', j, w)] = sV.n

    def b_av(j, w):
        @blk.tensor
        def _(t, j=j, w=w):
            t.wait_ge(sV.h, PM[('atev', j, w)])
            for h in range(8 * w, 8 * w + 8):
                c = h // 2
                r0 = (h % 2) * HD
                bank = po[2 * w + c % 2]
                mm = t.matmul(bank[r0:r0 + HD, ((c - 4 * w) // 2) * 128:((c - 4 * w) // 2) * 128 + 128],
                              v_bf[:, h * HD:(h + 1) * HD],
                              AT_bf[:, h * 128:(h + 1) * 128], start=True, stop=True)
            sM.inc(mm)
        PM[('av', j, w)] = sM.n
        if w == 1:
            PM[('av', j)] = sM.n

    def b_ao(j, w):
        @blk.scalar
        def _(sc, j=j, w=w):
            sc.wait_ge(sM.h, PM[('av', j, w)])
            for c in range(4 * w, 4 * w + 4):
                bank = po[2 * w + c % 2]
                last = sc.activation(out=ao_fm[:, c * 128:(c + 1) * 128],
                                     in_=bank[:, ((c - 4 * w) // 2) * 128:((c - 4 * w) // 2) * 128 + 128],
                                     func=AF.Copy)
            sS.inc(last)
        PM[('ao', j)] = sS.n

    def b_proj(j):
        @blk.tensor
        def _(t, j=j):
            t.wait_ge(sS.h, PM[('ao', j)])
            for of in range(2):
                dst = po[2 + of]
                for dc in range(8):
                    t.matmul(dst[:], ao_fm[:, dc * 128:(dc + 1) * 128],
                             wproj_sb[:, dc * D + of * 512:dc * D + of * 512 + 512],
                             start=(dc == 0), stop=False)
                mm = t.matmul(dst[:], ones1r_b[:], rows_b[0:1, of * 512:of * 512 + 512],
                              start=False, stop=True)
            sM.inc(mm)
        PM[('proj', j)] = sM.n

    def b_add(j):
        @blk.vector
        def _(v, j=j):
            v.wait_ge(sM.h, PM[('proj', j)])
            for of in range(2):
                sl = outacc[:, j * D + of * 512:j * D + of * 512 + 512]
                last = v.tensor_add(out=sl, in0=po[2 + of][:], in1=sl)
            sV.inc(last)
        PM[('add', j)] = sV.n

    for j in range(NT):
        f_w1(j); f_ev1(j); f_w2(j); f_ev2a(j)
        f_muvar(j)
        _dr('vbf', 2)
        f_sd(j); f_hat(j)
        _dr('vbf', 2)
        if j >= 1:
            b_smm(j - 1)
            b_exp(j - 1, 0); b_norm(j - 1, 0); b_at(j - 1, 0); b_atev(j - 1, 0); b_av(j - 1, 0)
            b_exp(j - 1, 1); b_norm(j - 1, 1); b_at(j - 1, 1); b_atev(j - 1, 1); b_av(j - 1, 1)
        f_ev2v(j)
        _dr('vbf', 2)
        f_tpA(j); f_gbA(j); f_tpB(j); f_gbB(j)
        if j >= 1:
            b_ao(j - 1, 0); b_ao(j - 1, 1); b_proj(j - 1); b_add(j - 1)
            _dr('proj', 3)
    j = NT - 1
    b_smm(j)
    b_exp(j, 0); b_norm(j, 0); b_at(j, 0); b_atev(j, 0); b_av(j, 0)
    b_exp(j, 1); b_norm(j, 1); b_at(j, 1); b_atev(j, 1); b_av(j, 1)
    b_ao(j, 0); b_ao(j, 1); b_proj(j); b_add(j)
    V_ATTN = sV.n

    wctx.close()
    hctx = ExitStack()
    he = hctx.enter_context
    w1l_sb = he(nc.sbuf_tensor("w1l_sb", [128, 8 * LHID], BF16))
    w2l_sb = he(nc.sbuf_tensor("w2l_sb", [128, 4 * D], BF16))
    w1h_sb = he(nc.sbuf_tensor("w1h_sb", [128, 8 * 1024], BF16))
    w2h_sb = he(nc.sbuf_tensor("w2h_sb", [128, 8 * 1024], BF16))
    hl_fm = he(nc.sbuf_tensor("hl_fm", [128, 4 * T], BF16))

    W_QKV_DONE = PM[('w2', NT - 1)]

    @blk.gpsimd
    def _(g):
        g.wait_ge(sM.h, W_QKV_DONE)
        for kc in range(8):
            sW.inc(g.dma_start(out=w1l_sb[:, kc * LHID:(kc + 1) * LHID],
                               in_=w1l_d[kc * 128:(kc + 1) * 128, :]), 16)
        for hc in range(4):
            sW.inc(g.dma_start(out=w2l_sb[:, hc * D:(hc + 1) * D],
                               in_=w2l_d[hc * 128:(hc + 1) * 128, :]), 16)
    W_LGT = sW.n

    @blk.vector
    def _(v):
        v.wait_ge(sW.h, W_LGT)
        for kc in range(8):
            last = v.tensor_scalar(out=w1l_sb[:, kc * LHID:(kc + 1) * LHID],
                                   in0=w1l_sb[:, kc * LHID:(kc + 1) * LHID],
                                   scalar1=gl_sb[:, kc:kc + 1], scalar2=None, op0=ALU.mult)
        sV.inc(last)
    V_WL = sV.n

    l_mark = {}
    lt_ready = {}
    lo_add = {}

    def emit_hidden(tg):
        for hc in range(4):
            idx = tg * 4 + hc

            @blk.tensor
            def _(t, hc=hc, tg=tg, idx=idx):
                if idx == 0:
                    t.wait_ge(sV.h, max(V_WL, PM[('gbB', NT - 1)]))
                if idx >= 2:
                    t.wait_ge(sS.h, l_mark[idx - 2])
                dst = pq if idx % 2 == 0 else pq2
                for kc in range(8):
                    mm = t.matmul(dst[:], w1l_sb[:, kc * LHID + hc * 128:kc * LHID + hc * 128 + 128],
                                  nfm[:, kc * T + tg * 512:kc * T + (tg + 1) * 512],
                                  start=(kc == 0), stop=(kc == 7))
                sM.inc(mm)

            @blk.scalar
            def _(sc, hc=hc, tg=tg, idx=idx):
                sc.wait_ge(sM.h, sM.n)
                src = pq if idx % 2 == 0 else pq2
                sS.inc(sc.activation(out=hl_fm[:, hc * T + tg * 512:hc * T + (tg + 1) * 512],
                                     in_=src[:], func=AF.Gelu, bias=b1l_sb[:, hc:hc + 1]))
            l_mark[idx] = sS.n
            _dr('light', min(2, max(0, len(RQ) - 3)))
        lt_ready[tg] = sS.n

    def emit_out(j):
        @blk.tensor
        def _(t, j=j):
            t.wait_ge(sS.h, lt_ready[j // 4])
            if j >= 2:
                t.wait_ge(sV.h, lo_add[j - 2])
            for of in range(2):
                dst = po[2 * (j % 2) + of]
                for hc in range(4):
                    t.matmul(dst[:], hl_fm[:, hc * T + j * 128:hc * T + j * 128 + 128],
                             w2l_sb[:, hc * D + of * 512:hc * D + of * 512 + 512],
                             start=(hc == 0), stop=False)
                mm = t.matmul(dst[:], ones1r_b[:], rows_b[0:1, D + of * 512:D + of * 512 + 512],
                              start=False, stop=True)
            sM.inc(mm)

        @blk.vector
        def _(v, j=j):
            v.wait_ge(sM.h, sM.n)
            for of in range(2):
                sl = outacc[:, j * D + of * 512:j * D + of * 512 + 512]
                last = v.tensor_add(out=sl, in0=po[2 * (j % 2) + of][:], in1=sl)
            sV.inc(last)
        lo_add[j] = sV.n

    emit_hidden(0)
    emit_hidden(1)
    for j in range(4):
        emit_out(j)
    emit_hidden(2)
    for j in range(4, 8):
        emit_out(j)
    emit_hidden(3)
    for j in range(8, 16):
        emit_out(j)
    V_LIGHT = sV.n

    drain_routing(len(RQ))
    V_MASK = sV.n

    if dbg:
        @blk.sync
        def _(sy):
            sy.wait_ge(sV.h, V_MASK)
            sy.wait_ge(sT.h, T_NFM)
            sSt.inc(sy.dma_start(out=dbg_rt[:, 0:NT], in_=s_t[:]), 16)
            sSt.inc(sy.dma_start(out=dbg_rt[:, NT:2 * NT], in_=invr_t[:]), 16)
            sSt.inc(sy.dma_start(out=dbg_rt[:, 2 * NT:3 * NT], in_=mask_t[:]), 16)
            sSt.inc(sy.dma_start(out=dbg_rt[:, 3 * NT:4 * NT], in_=key_t[:]), 16)
            sSt.inc(sy.dma_start(out=dbg_rt[:, 64:65], in_=smax_b[:], allow_slow_non_contiguous=True), 16)
            sSt.inc(sy.dma_start(out=dbg_rt[:, 65:66], in_=Ab[:], allow_slow_non_contiguous=True), 16)
            sSt.inc(sy.dma_start(out=dbg_rt[:, 66:67], in_=tadb[:], allow_slow_non_contiguous=True), 16)
            sSt.inc(sy.dma_start(out=dbg_rt[:, 67:68], in_=lob[:], allow_slow_non_contiguous=True), 16)
            sSt.inc(sy.dma_start(out=dbg_rt[:, 68:69], in_=hib[:], allow_slow_non_contiguous=True), 16)
            sSt.inc(sy.dma_start(out=dbg_rt[:, 69:70], in_=fcb[:], allow_slow_non_contiguous=True), 16)
            sSt.inc(sy.dma_start(out=dbg_rt[0:1, 70:71], in_=gam_t[:], allow_slow_non_contiguous=True), 16)
            sSt.inc(sy.dma_start(out=dbg_rt[0:1, 71:72], in_=tadp[:], allow_slow_non_contiguous=True), 16)
            sSt.inc(sy.dma_start(out=dbg_traj, in_=traj[:]), 16)
            sSt.inc(sy.dma_start(out=dbg_nfm, in_=nfm[:]), 16)


    if dbg:
        @blk.sync
        def _(sy):
            sy.wait_ge(sV.h, V_LIGHT)
            for j in range(NT):
                sSt.inc(sy.dma_start(out=dbg_light[j * 128:(j + 1) * 128, :],
                                     in_=outacc[:, j * D:(j + 1) * D]), 16)

    HV_STATE = {'gm': None, 'am': None}
    for g in range(4):
        @blk.gpsimd
        def _(gg, g=g):
            if g == 0:
                gg.wait_ge(sM.h, W_QKV_DONE)
            else:
                gg.wait_ge(sM.h, sM.n)
            for kc in range(8):
                sWH.inc(gg.dma_start(out=w1h_sb[:, kc * 1024:(kc + 1) * 1024],
                                     in_=w1h_d[kc * 128:(kc + 1) * 128,
                                               g * 1024:(g + 1) * 1024]), 16)
            for hc in range(8):
                sWH.inc(gg.dma_start(out=w2h_sb[:, hc * 1024:(hc + 1) * 1024],
                                     in_=w2h_d[g * 1024 + hc * 128:g * 1024 + (hc + 1) * 128, :]), 16)
            sWH.inc(gg.dma_start(out=b1h_sb[:],
                                 in_=b1h_d[g * 1024:(g + 1) * 1024].rearrange(
                                     "(c p) -> p c", p=128),
                                 allow_slow_non_contiguous=True), 16)
        W_G = sWH.n

        @blk.vector
        def _(v, g=g, W_G=W_G):
            v.wait_ge(sWH.h, W_G)
            if g == 0:
                v.wait_ge(sV.h, V_LIGHT)
            for kc in range(8):
                last = v.tensor_scalar(out=w1h_sb[:, kc * 1024:(kc + 1) * 1024],
                                       in0=w1h_sb[:, kc * 1024:(kc + 1) * 1024],
                                       scalar1=gh_sb[:, kc:kc + 1], scalar2=None, op0=ALU.mult)
            sV.inc(last)
        V_WG = sV.n

        hq = [(pq, 0), (pq, 256), (pq2, 0), (pq2, 256), (pSA, 0), (pSA, 256), (po[0], 0), (po[0], 256)]
        for tg in range(8):
            @blk.tensor
            def _(t, g=g, tg=tg, V_WG=V_WG):
                if tg == 0:
                    t.wait_ge(sV.h, V_WG)
                if HV_STATE['gm'] is not None:
                    t.wait_ge(sS.h, HV_STATE['gm'])
                for hc in range(8):
                    bank, off = hq[hc]
                    for kc in range(8):
                        mm = t.matmul(bank[:, off:off + 256],
                                      w1h_sb[:, kc * 1024 + hc * 128:kc * 1024 + hc * 128 + 128],
                                      nfm[:, kc * T + tg * 256:kc * T + (tg + 1) * 256],
                                      start=(kc == 0), stop=(kc == 7))
                    sM.inc(mm)
                    HV_STATE[('h', hc)] = sM.n

            for hc in range(8):
                @blk.scalar
                def _(sc, g=g, hc=hc):
                    sc.wait_ge(sM.h, HV_STATE[('h', hc | 1)])
                    bank, off = hq[hc]
                    sS.inc(sc.activation(out=A_bf[:, hc * 256:(hc + 1) * 256],
                                         in_=bank[:, off:off + 256], func=AF.Gelu,
                                         bias=b1h_sb[:, hc:hc + 1]))
                HV_STATE[('g', hc)] = sS.n
            HV_STATE['gm'] = sS.n

            for hc in range(8):
                @blk.tensor
                def _(t, g=g, hc=hc):
                    t.wait_ge(sS.h, HV_STATE[('g', hc)])
                    if hc == 0 and HV_STATE['am'] is not None:
                        t.wait_ge(sV.h, HV_STATE['am'])
                    ob = [po[1], po[2], po[3], pmisc]
                    for st in range(2):
                        for of in range(2):
                            mm = t.matmul(ob[st * 2 + of][:],
                                          A_bf[:, hc * 256 + st * 128:hc * 256 + st * 128 + 128],
                                          w2h_sb[:, hc * 1024 + of * 512:hc * 1024 + of * 512 + 512],
                                          start=(hc == 0), stop=(hc == 7 and g < 3))
                    if g == 3 and hc == 7:
                        for st in range(2):
                            for of in range(2):
                                mm = t.matmul(ob[st * 2 + of][:], ones1r_b[:],
                                              rows_b[0:1, 2 * D + of * 512:2 * D + of * 512 + 512],
                                              start=False, stop=True)
                    sM.inc(mm)

            @blk.vector
            def _(v, g=g, tg=tg):
                v.wait_ge(sM.h, sM.n)
                ob = [po[1], po[2], po[3], pmisc]
                for st in range(2):
                    jj = tg * 2 + st
                    for of in range(2):
                        sl = outacc[:, jj * D + of * 512:jj * D + of * 512 + 512]
                        last = v.scalar_tensor_tensor(out=sl, in0=ob[st * 2 + of][:],
                                                      scalar=mask_t[:, jj:jj + 1], in1=sl,
                                                      op0=ALU.mult, op1=ALU.add)
                sV.inc(last)
            HV_STATE['am'] = sV.n

    @blk.sync
    def _(sy):
        sy.wait_ge(sV.h, sV.n)
        for j in range(NT):
            sSt.inc(sy.dma_start(out=out_d[j * 128:(j + 1) * 128, :],
                                 in_=outacc[:, j * D:(j + 1) * D]), 16)
        sy.wait_ge(sSt.h, sSt.n)

    hctx.close()
    ctx.close()
    return nc


_NC_CACHE = []


def kernel(L0_attn_x, L0_attn_W_qkv, L0_attn_ln_gamma, L0_attn_ln_beta, L0_attn_W_proj,
           L0_attn_b_proj, L0_ffn_routing_token, L0_ffn_gamma_light, L0_ffn_w1_light,
           L0_ffn_b1_light, L0_ffn_w2_light, L0_ffn_b2_light, L0_ffn_gamma_heavy,
           L0_ffn_w1_heavy, L0_ffn_b1_heavy, L0_ffn_w2_heavy, L0_ffn_b2_heavy):
    x = np.ascontiguousarray(np.asarray(L0_attn_x, dtype=np.float32))
    shared = {
        "wqkv": np.ascontiguousarray(L0_attn_W_qkv, dtype=np.float32),
        "lng": np.ascontiguousarray(L0_attn_ln_gamma, dtype=np.float32),
        "lnb": np.ascontiguousarray(L0_attn_ln_beta, dtype=np.float32),
        "wproj": np.ascontiguousarray(L0_attn_W_proj, dtype=np.float32),
        "bproj": np.ascontiguousarray(L0_attn_b_proj, dtype=np.float32),
        "rt": np.ascontiguousarray(L0_ffn_routing_token, dtype=np.float32),
        "gl": np.ascontiguousarray(L0_ffn_gamma_light, dtype=np.float32),
        "w1l": np.ascontiguousarray(L0_ffn_w1_light, dtype=np.float32),
        "b1l": np.ascontiguousarray(L0_ffn_b1_light, dtype=np.float32),
        "w2l": np.ascontiguousarray(L0_ffn_w2_light, dtype=np.float32),
        "b2l": np.ascontiguousarray(L0_ffn_b2_light, dtype=np.float32),
        "gh": np.ascontiguousarray(L0_ffn_gamma_heavy, dtype=np.float32),
        "w1h": np.ascontiguousarray(L0_ffn_w1_heavy, dtype=np.float32),
        "b1h": np.ascontiguousarray(L0_ffn_b1_heavy, dtype=np.float32),
        "w2h": np.ascontiguousarray(L0_ffn_w2_heavy, dtype=np.float32),
        "b2h": np.ascontiguousarray(L0_ffn_b2_heavy, dtype=np.float32),
    }
    if not _NC_CACHE:
        _NC_CACHE.append(build_program())
    nc = _NC_CACHE[0]
    in_maps = [dict(shared, x=np.ascontiguousarray(x[b])) for b in range(8)]
    res = run_bass_kernel_spmd(nc, in_maps, core_ids=list(range(8)))
    return np.stack([np.asarray(r["out"]) for r in res.results], axis=0)
